# version 15
# baseline (speedup 1.0000x reference)
import math
import numpy as np
from contextlib import ExitStack

import concourse.bass as bass
import concourse.mybir as mybir
from concourse.bass_utils import run_bass_kernel_spmd

F32 = mybir.dt.float32
BF16 = mybir.dt.bfloat16
I32 = mybir.dt.int32
AF = mybir.ActivationFunctionType
ALU = mybir.AluOpType
AX = mybir.AxisListType
PI = float(np.pi)

L = 4096
D = 1024
DI = 2048
C = 4128
K = 32
H = 64
NE = 6
DE = 512
TA = 512
TB = 512
NTA = L // TA
NTB = L // TB
NVC = 16
NGC = 16


class Op:
    __slots__ = ("stream", "dom", "emit", "deps", "need_inc", "inc_no",
                 "dma", "tag", "gidx", "sidx", "risky")


class Sched:

    def __init__(self):
        self.ops = []
        self.bufs = {}
        self.pending_barrier = None
        self.barrier_absorbed = set()
        self.scount = {}

    def add(self, stream, emit, reads=(), writes=(), sreads=(), dma=False,
            tag="", risky=False):
        o = Op()
        o.stream = stream
        if dma:
            qi = self.scount.get("qn_" + stream, 0)
            self.scount["qn_" + stream] = qi + 1
            o.dom = f"q{stream}L{qi % 16}"
        else:
            o.dom = stream
        o.emit = emit
        o.dma = dma
        o.tag = tag
        o.need_inc = False
        o.risky = risky
        o.sidx = self.scount.get(stream, 0)
        self.scount[stream] = o.sidx + 1
        deps = {}

        def dep(p, wait_needed):
            if p is None or p is o:
                return
            k = id(p)
            if k in deps:
                deps[k] = (p, deps[k][1] or wait_needed)
            else:
                deps[k] = (p, wait_needed)

        def need_wait(p, kind, scalar=False):
            if p.dom != o.dom:
                return True
            if p.dma and o.dma:
                return kind == "raw"
            if stream == "pe":
                return False
            if kind != "raw":
                return False
            return (o.sidx - p.sidx) <= 6

        for b in reads:
            e = self.bufs.get(b)
            if e and e[0] is not None:
                dep(e[0], need_wait(e[0], "raw", False))
        for b in sreads:
            e = self.bufs.get(b)
            if e and e[0] is not None:
                dep(e[0], need_wait(e[0], "raw", True))
        for b in writes:
            e = self.bufs.get(b)
            if e:
                if e[0] is not None:
                    dep(e[0], need_wait(e[0], "waw"))
                for r in e[1]:
                    dep(r, need_wait(r, "war"))
        if self.pending_barrier is not None and stream not in self.barrier_absorbed:
            self.barrier_absorbed.add(stream)
            for p in self.pending_barrier.values():
                dep(p, True)
        o.deps = list(deps.values())
        for b in list(reads) + list(sreads):
            self.bufs.setdefault(b, [None, []])[1].append(o)
        for b in writes:
            self.bufs[b] = [o, []]
        o.gidx = len(self.ops)
        self.ops.append(o)
        return o

    def barrier(self):
        last = {}
        for o in self.ops:
            last[o.dom] = o
        self.pending_barrier = last
        self.barrier_absorbed = set()

    def finalize(self):
        for o in self.ops:
            if o.dma:
                o.need_inc = True
            for (p, w) in o.deps:
                if w:
                    p.need_inc = True
        ctr = {}
        for o in self.ops:
            if o.need_inc:
                ctr[o.dom] = ctr.get(o.dom, 0) + 1
                o.inc_no = ctr[o.dom]
            else:
                o.inc_no = None
        self.totals = ctr

    def emit_stream(self, stream, eng, sems):
        known = {}
        for o in self.ops:
            if o.stream != stream:
                continue
            for (p, w) in o.deps:
                if not w:
                    continue
                v = p.inc_no * (16 if p.dma else 1)
                if known.get(p.dom, 0) < v:
                    eng.wait_ge(sems[p.dom], v)
                    known[p.dom] = v
            inst = o.emit(eng)
            if o.need_inc:
                inst.then_inc(sems[o.dom], 16 if o.dma else 1)


def build_program(dbg=()):
    nc = bass.Bass("TRN2", target_bir_lowering=False)
    S = Sched()

    def din(name, shape):
        return nc.dram_tensor(name, shape, F32, kind="ExternalInput").ap()

    x_d = din("x", [L, D])
    ipw_d = din("ipw", [D, C])
    opw_d = din("opw", [DI, D])
    w1_d = din("w1", [NE, D, DE])
    w2_d = din("w2", [NE, DE, D])
    theta_d = din("theta_t", [128, NVC])
    ck_d = din("ck", [128, 33 * 4])
    cb_d = din("cb", [128, 33])
    score_d = din("score", [32, 1])
    slopes_d = din("slopes", [32, 1])
    deriv_d = din("deriv", [1, 3])
    wre_d = din("Wre2", [128, 64])
    wim_d = din("Wim2", [128, 64])
    nsre_d = din("nsre2", [128, 1])
    nsim_d = din("nsim2", [128, 1])
    pwt_d = din("pwT", [128, 48])
    anch_d = din("anchT", [6, 6])
    rt_d = din("rt", [1, 1])
    sel_d = din("SEL", [32, 2048])
    sel2_d = din("SEL2", [128, 512])
    ident_d = din("ident", [128, 128])
    dist_d = din("dist", [32, L])
    hostc_d = din("hostc_in", [1, 4])

    y_d = nc.dram_tensor("y", [L, D], F32, kind="ExternalOutput").ap()
    if "u1" in dbg:
        u1_d = nc.dram_tensor("u1", [D, L], F32, kind="ExternalOutput").ap()
    else:
        u1_d = nc.dram_tensor("u1", [D, L], F32).ap()
    if "u2" in dbg:
        u2_d = nc.dram_tensor("u2", [D, L], F32, kind="ExternalOutput").ap()
    else:
        u2_d = nc.dram_tensor("u2", [D, L], F32).ap()
    ipbf_d = nc.dram_tensor("ipbf", [8, 128, 4096], BF16).ap()
    opbf_d = nc.dram_tensor("opbf", [8, 128, 2048], BF16).ap()
    if "a1" in dbg:
        dbg_ut = nc.dram_tensor("dbg_ut", [128, 8, TA], F32,
                                kind="ExternalOutput").ap()
        dbg_xln = nc.dram_tensor("dbg_xln", [128, 8 * TA], F32,
                                 kind="ExternalOutput").ap()
        dbg_ewf = nc.dram_tensor("dbg_ewf", [1, NE * TA], F32,
                                 kind="ExternalOutput").ap()
        dbg_hsb = nc.dram_tensor("dbg_hsb", [128, NE * 4 * TA], F32,
                                 kind="ExternalOutput").ap()
        dbg_rpk = nc.dram_tensor("dbg_rpk", [128, 64], F32,
                                 kind="ExternalOutput").ap()
        dbg_cst = nc.dram_tensor("dbg_cst", [128, 16], F32,
                                 kind="ExternalOutput").ap()
        dbg_sm = nc.dram_tensor("dbg_sm", [1, 16], F32,
                                kind="ExternalOutput").ap()
        dbg_b = nc.dram_tensor("dbg_b", [128, 6 * TB], F32,
                               kind="ExternalOutput").ap()
        dbg_bs = nc.dram_tensor("dbg_bs", [32, 4 * TB], F32,
                                kind="ExternalOutput").ap()
        dbg_b2 = nc.dram_tensor("dbg_b2", [128, 2 * TB], F32,
                                kind="ExternalOutput").ap()
    else:
        dbg_ut = dbg_xln = dbg_ewf = dbg_hsb = dbg_rpk = dbg_b = dbg_bs = None
        dbg_b2 = None

    st_p = ExitStack()

    def sb(name, shape, dt, st=st_p):
        return st.enter_context(nc.sbuf_tensor(name, shape, dt))

    def pt(name, shape, dt, st=st_p):
        return st.enter_context(nc.psum_tensor(name, shape, dt))

    xlnbf = sb("xlnbf", [128, 4096], BF16)
    sqb = [sb(f"sqb{i}", [128, 512], BF16) for i in range(2)]
    d1b = sb("d1b", [128, 512], F32)
    rowp = sb("rowp", [1, 2048], F32)
    rpk = sb("rpk", [128, 64], F32)
    cst = sb("cst", [128, 16], F32)
    ebuf = [sb(f"ebuf{i}", [128, 512], F32) for i in range(2)]
    identt = sb("identt", [128, 128], F32)
    onesr = sb("onesr", [1, 128], F32)
    onesrb = sb("onesrb", [1, 128], BF16)
    onescb = sb("onescb", [128, 1], BF16)
    ubb = [sb(f"ubb{i}", [128, 512], BF16) for i in range(2)]
    onesc = sb("onesc", [128, 1], F32)
    theta_t = sb("theta_sb", [128, NVC], F32)
    theta2 = sb("theta2", [128, NVC], F32)
    ckt = sb("ckt", [128, 33 * 4], F32)
    cbt = sb("cbt", [128, 33], F32)
    scoret = sb("scoret", [32, 1], F32)
    slopest = sb("slopest", [32, 1], F32)
    negslope = sb("negslope", [32, 1], F32)
    derivt = sb("derivt", [1, 3], F32)
    wret = sb("wret", [128, 64], F32)
    wimt = sb("wimt", [128, 64], F32)
    nsret = sb("nsret", [128, 1], F32)
    nsimt = sb("nsimt", [128, 1], F32)
    wrens = sb("wrens", [128, 64], BF16)
    wimns = sb("wimns", [128, 64], BF16)
    pwtt = sb("pwtt", [128, 48], F32)
    ancht = sb("ancht", [6, 6], F32)
    rtt = sb("rtt", [1, 1], F32)
    selt = sb("selt", [32, 2048], BF16)
    sel2t = sb("sel2t", [128, 512], BF16)
    ones_scan = sb("ones_scan", [128, 512], F32)
    wrebd = sb("wrebd", [128, 128], BF16)
    wimbd = sb("wimbd", [128, 128], BF16)
    carry_re = sb("carry_re", [128, NVC], F32)
    carry_im = sb("carry_im", [128, NVC], F32)
    carry_den = sb("carry_den", [32, 1], F32)
    tails = sb("tails", [128, 33 * 3], F32)
    dsm = sb("dsm", [1, 8], F32)
    wsm = sb("wsm", [1, 3], F32)
    rts = sb("rts", [1, 4], F32)
    spl = sb("spl", [32, 4], F32)
    hostc = sb("hostc", [1, 4], F32)

    PB = [pt(f"PB{i}", [128, 512], F32) for i in range(8)]
    PBK = [f"PB{i}" for i in range(8)]

    CONSTS = {"c256": 256.5, "n2pi": -2 * PI, "p2pi": 2 * PI,
              "npih": -PI / 2, "nhalf": -0.5, "half": 0.5, "one": 1.0}
    CCOL = {k: i for i, k in enumerate(CONSTS)}
    NDCOL = {"w0b": 8, "w1b": 9, "nw2b": 10, "hirb": 11, "iqt": 12}

    def cc_ap(name, pc=128):
        i = CCOL[name] if name in CCOL else NDCOL[name]
        return cst[0:pc, i:i + 1]

    sems = {}

    def mm(bank, out_ap, lhsT, rhs, start, stop, reads, tag="mm",
           tile_position=None):
        def e(eng, out_ap=out_ap, lhsT=lhsT, rhs=rhs, start=start, stop=stop,
              tp=tile_position):
            return eng.matmul(out_ap, lhsT, rhs, start=start, stop=stop,
                              tile_position=tp)
        return S.add("pe", e, reads=tuple(reads), writes=(bank,), tag=tag)

    def transpose(bank, out_ap, in_ap, reads, ident_ap=None, tag="tr"):
        ia = ident_ap if ident_ap is not None else identt[:, :]
        def e(eng, out_ap=out_ap, in_ap=in_ap, ia=ia):
            return eng.transpose(out_ap, in_ap, ia)
        return S.add("pe", e, reads=tuple(reads) + ("identt",), writes=(bank,),
                     tag=tag)

    def act(out_ap, in_ap, func, reads, writes, scale=None, bias=None,
            sreads=(), tag="act"):
        def e(eng, out_ap=out_ap, in_ap=in_ap, func=func, scale=scale,
              bias=bias):
            kw = {}
            if scale is not None:
                kw["scale"] = scale
            if bias is not None:
                kw["bias"] = bias
            return eng.activation(out_ap, in_ap, func, **kw)
        return S.add("act", e, reads=tuple(reads), writes=tuple(writes),
                     sreads=tuple(sreads), tag=tag)

    def dve(emit, reads, writes, sreads=(), tag="dve"):
        return S.add("dve", emit, reads=tuple(reads), writes=tuple(writes),
                     sreads=tuple(sreads), tag=tag)

    def pool(emit, reads, writes, sreads=(), tag="pool"):
        return S.add("pool", emit, reads=tuple(reads), writes=tuple(writes),
                     sreads=tuple(sreads), tag=tag)

    def dma(queue, out_ap, in_ap, reads, writes, tag="dma"):
        def e(eng, out_ap=out_ap, in_ap=in_ap):
            return eng.dma_start(out=out_ap, in_=in_ap)
        return S.add(queue, e, reads=tuple(reads), writes=tuple(writes),
                     dma=True, tag=tag)

    def ts(out_ap, in_ap, s1, s2, op0, op1, reads, writes, sreads=(),
           stream="dve", tag="ts"):
        def e(eng, out_ap=out_ap, in_ap=in_ap, s1=s1, s2=s2, op0=op0, op1=op1):
            if op1 is None:
                return eng.tensor_scalar(out_ap, in_ap, s1, None, op0=op0)
            return eng.tensor_scalar(out_ap, in_ap, s1, s2, op0=op0, op1=op1)
        return S.add(stream, e, reads=tuple(reads), writes=tuple(writes),
                     sreads=tuple(sreads), tag=tag)

    def tt(out_ap, a_ap, b_ap, op, reads, writes, stream="dve", tag="tt"):
        def e(eng, out_ap=out_ap, a_ap=a_ap, b_ap=b_ap, op=op):
            return eng.tensor_tensor(out_ap, a_ap, b_ap, op=op)
        return S.add(stream, e, reads=tuple(reads), writes=tuple(writes),
                     tag=tag)

    def stt(out_ap, in0_ap, s_ap, in1_ap, op0, op1, reads, writes, sreads=(),
            stream="dve", tag="stt"):
        def e(eng, out_ap=out_ap, in0_ap=in0_ap, s_ap=s_ap, in1_ap=in1_ap,
              op0=op0, op1=op1):
            return eng.scalar_tensor_tensor(out_ap, in0_ap, s_ap, in1_ap,
                                            op0=op0, op1=op1)
        return S.add(stream, e, reads=tuple(reads), writes=tuple(writes),
                     sreads=tuple(sreads), tag=tag)

    def copy(out_ap, in_ap, reads, writes, stream="dve", tag="copy"):
        def e(eng, out_ap=out_ap, in_ap=in_ap):
            return eng.tensor_copy(out_ap, in_ap)
        return S.add(stream, e, reads=tuple(reads), writes=tuple(writes),
                     tag=tag)

    def memset(ap, v, writes, stream="dve", tag="memset"):
        def e(eng, ap=ap, v=v):
            return eng.memset(ap, v)
        return S.add(stream, e, writes=tuple(writes), tag=tag)

    def scan(out_ap, d0_ap, d1_ap, init_ap, reads, writes, sreads, tag="scan"):
        def e(eng, out_ap=out_ap, d0_ap=d0_ap, d1_ap=d1_ap, init_ap=init_ap):
            return eng.tensor_tensor_scan(out_ap, d0_ap, d1_ap, init_ap,
                                          op0=ALU.mult, op1=ALU.add)
        return S.add("dve", e, reads=tuple(reads), writes=tuple(writes),
                     sreads=tuple(sreads), tag=tag, risky=True)

    def reduce(out_ap, in_ap, op, reads, writes, tag="red"):
        def e(eng, out_ap=out_ap, in_ap=in_ap, op=op):
            return eng.tensor_reduce(out_ap, in_ap, axis=AX.X, op=op)
        return S.add("dve", e, reads=tuple(reads), writes=tuple(writes),
                     tag=tag, risky=True)

    def recip(out_ap, in_ap, reads, writes, tag="recip"):
        def e(eng, out_ap=out_ap, in_ap=in_ap):
            return eng.reciprocal(out_ap, in_ap)
        return S.add("dve", e, reads=tuple(reads), writes=tuple(writes),
                     tag=tag)

    for apout, src, wk in [
        (theta_t[:, :], theta_d, "theta"), (ckt[:, :], ck_d, "ck"),
        (cbt[:, :], cb_d, "cb"), (scoret[:, :], score_d, "score"),
        (slopest[:, :], slopes_d, "slopes"), (derivt[:, :], deriv_d, "deriv"),
        (wret[:, :], wre_d, "wre"), (wimt[:, :], wim_d, "wim"),
        (nsret[:, :], nsre_d, "nsre"), (nsimt[:, :], nsim_d, "nsim"),
        (pwtt[:, :], pwt_d, "pwt"), (ancht[:, :], anch_d, "anch"),
        (rtt[:, :], rt_d, "rtsb"),
        (identt[:, :], ident_d, "identt"),
        (hostc[:, :], hostc_d, "hostc"),
    ]:
        dma("sync", apout, src, reads=(), writes=(wk,), tag="ld_" + wk)

    for name, v in CONSTS.items():
        memset(cst[:, CCOL[name]:CCOL[name] + 1], v, writes=(f"cst_{name}",))
    dma("pool", sel2t[:, :], sel2_d, reads=(), writes=("sel2",),
        tag="ld_sel2")
    dma("pool", selt[:, :], sel_d, reads=(), writes=("sel",), tag="ld_sel")
    memset(onesr[:, :], 1.0, writes=("onesr",))
    memset(onesrb[:, :], 1.0, writes=("onesrb",))
    memset(onesc[:, :], 1.0, writes=("onesc",))
    memset(onescb[:, :], 1.0, writes=("onescb",))
    memset(ones_scan[:, :], 1.0, writes=("ones_scan",))
    memset(carry_re[:, :], 0.0, writes=tuple(f"cre{c}" for c in range(NVC)))
    memset(carry_im[:, :], 0.0, writes=tuple(f"cim{c}" for c in range(NVC)))
    memset(carry_den[:, :], 0.0, writes=("cden",))
    memset(tails[:, :], 0.0, writes=tuple(f"tail{c}" for c in range(33)))

    ts(theta2[:, :], theta_t[:, :], 1.0 / (2 * PI), None, ALU.mult, None,
       reads=("theta",), writes=("theta2",))
    NEG1_COL = 13
    memset(cst[:, NEG1_COL:NEG1_COL + 1], -1.0, writes=("cst_neg1",))
    act(spl[:, 0:1], slopest[:, :], AF.Abs, reads=("slopes",), writes=("spl0",))
    act(spl[:, 1:2], spl[:, 0:1], AF.Exp, reads=("spl0",), writes=("spl1",),
        scale=cst[0:32, NEG1_COL:NEG1_COL + 1], sreads=("cst_neg1",))
    act(spl[:, 2:3], spl[:, 1:2], AF.Ln, reads=("spl1",), writes=("spl2",),
        bias=cc_ap("one", 32), sreads=("cst_one",))
    ts(spl[:, 3:4], slopest[:, :], 0.0, None, ALU.max, None,
       reads=("slopes",), writes=("spl3",))
    def _ns(eng):
        return eng.tensor_tensor(negslope[:, :], spl[:, 2:3], spl[:, 3:4],
                                 op=ALU.add)
    S.add("dve", _ns, reads=("spl2", "spl3"), writes=("negslope_pos",))
    ts(negslope[:, :], negslope[:, :], -1.0, None, ALU.mult, None,
       reads=("negslope_pos",), writes=("negslope",))

    reduce(dsm[:, 0:1], derivt[:, :], ALU.max, reads=("deriv",),
           writes=("dsm_max",))
    ts(wsm[:, :], derivt[:, :], dsm[:, 0:1], None, ALU.subtract, None,
       reads=("deriv",), writes=("wsm_sub",), sreads=("dsm_max",))
    act(wsm[:, :], wsm[:, :], AF.Exp, reads=("wsm_sub",), writes=("wsm_exp",))
    reduce(dsm[:, 1:2], wsm[:, :], ALU.add, reads=("wsm_exp",),
           writes=("dsm_sum",))
    act(dsm[:, 3:4], dsm[:, 1:2], AF.Ln, reads=("dsm_sum",),
        writes=("dsm_ln",))
    act(dsm[:, 2:3], dsm[:, 3:4], AF.Exp, reads=("dsm_ln",),
        writes=("dsm_rec",), scale=cst[0:1, NEG1_COL:NEG1_COL + 1],
        sreads=("cst_neg1",))
    ts(wsm[:, :], wsm[:, :], dsm[:, 2:3], None, ALU.mult, None,
       reads=("wsm_exp",), writes=("wsm",), sreads=("dsm_rec",))
    for j, (name, neg) in enumerate([("w0b", False), ("w1b", False),
                                     ("nw2b", True)]):
        bank_ap = PB[7][:, j:j + 1]
        mm("PB7", bank_ap, onesr[:, :], wsm[:, j:j + 1], True, True,
           reads=("onesr", "wsm"), tag="bc_w")
        col = NDCOL[name]
        ts(cst[:, col:col + 1], bank_ap, -1.0 if neg else 1.0, None, ALU.mult,
           None, reads=("PB7",), writes=(f"cst_{name}",))
    ts(rts[:, 0:1], rtt[:, :], 0.1, None, ALU.max, None, reads=("rtsb",),
       writes=("rtc",))
    act(rts[:, 3:4], rts[:, 0:1], AF.Ln, reads=("rtc",), writes=("rtln",))
    act(rts[:, 1:2], rts[:, 3:4], AF.Exp, reads=("rtln",), writes=("rtrec",),
        scale=cst[0:1, NEG1_COL:NEG1_COL + 1], sreads=("cst_neg1",))
    ts(rts[:, 2:3], rts[:, 1:2], 0.5, None, ALU.mult, None, reads=("rtrec",),
       writes=("hir1",))
    mm("PB7", PB[7][:, 4:5], onesr[:, :], rts[:, 2:3], True, True,
       reads=("onesr", "hir1"), tag="bc_hir")
    ts(cst[:, NDCOL["hirb"]:NDCOL["hirb"] + 1], PB[7][:, 4:5], 1.0, None,
       ALU.mult, None, reads=("PB7",), writes=("cst_hirb",))
    memset(cst[:, NDCOL["iqt"]:NDCOL["iqt"] + 1], 1.0 / 0.3,
           writes=("cst_iqt",))
    ts(wrens[:, :], wret[:, :], nsret[:, 0:1], None, ALU.mult, None,
       reads=("wre",), writes=("wrens",), sreads=("nsre",))
    ts(wimns[:, :], wimt[:, :], nsimt[:, 0:1], None, ALU.mult, None,
       reads=("wim",), writes=("wimns",), sreads=("nsim",))
    memset(wrebd[:, :], 0.0, writes=("wrebd0",))
    memset(wimbd[:, :], 0.0, writes=("wimbd0",))
    copy(wrebd[0:64, 0:64], wrens[0:64, :], reads=("wrens", "wrebd0"),
         writes=("wrebd1",))
    copy(wrebd[64:128, 64:128], wrens[64:128, :], reads=("wrens", "wrebd1"),
         writes=("wrebd",))
    copy(wimbd[0:64, 0:64], wimns[0:64, :], reads=("wimns", "wimbd0"),
         writes=("wimbd1",))
    copy(wimbd[64:128, 64:128], wimns[64:128, :], reads=("wimns", "wimbd1"),
         writes=("wimbd",))

    def emit_ln(ut, utkeys, T, bmb, brb, phase, need_nm, xt=None, xpre=None):
        if xt is None:
            xt = xlnbf
            xpre = "xln"
        nd = 8
        sx = PB[2][0:1, 0:T]
        sq = PB[3][0:1, 0:T]
        for dc in range(nd):
            sb_ = sqb[dc % 2]
            ub_ = ubb[dc % 2]
            copy(ub_[:, 0:T], ut[:, dc, :], reads=(utkeys[dc],),
                 writes=(f"ubb{dc % 2}",), stream="pool", tag="ln_ub")
            act(sb_[:, 0:T], ut[:, dc, :], AF.Square, reads=(utkeys[dc],),
                writes=(f"sqb{dc % 2}",), tag="ln_sq")
            mm("PB2", sx, onescb[:, :], ub_[:, 0:T], dc == 0, dc == nd - 1,
               reads=("onescb", f"ubb{dc % 2}"), tag="ln_sx")
            mm("PB3", sq, onescb[:, :], sb_[:, 0:T], dc == 0, dc == nd - 1,
               reads=("onescb", f"sqb{dc % 2}"), tag="ln_sq_mm")
        m = rowp[0:1, 0:T]
        nm = rowp[0:1, 512:512 + T]
        va = rowp[0:1, 1024:1024 + T]
        rs_ = rowp[0:1, 1536:1536 + T]
        ts(m, sx, 1.0 / D, None, ALU.mult, None, reads=("PB2",),
           writes=("lnm",), tag="ln_m")
        ts(va, sq, 1.0 / D, None, ALU.mult, None, reads=("PB3",),
           writes=("lnmsq",), tag="ln_msq")
        if need_nm:
            ts(nm, m, -1.0, None, ALU.mult, None, reads=("lnm",),
               writes=("lnnm",), tag="ln_nm")
        def _mm2(eng):
            return eng.tensor_tensor(d1b[0:1, 0:T], m, m, op=ALU.mult)
        S.add("dve", _mm2, reads=("lnm",), writes=("d1row",), tag="ln_m2")
        def _var2(eng):
            return eng.tensor_tensor(va, va, d1b[0:1, 0:T], op=ALU.subtract)
        S.add("dve", _var2, reads=("lnmsq", "d1row"), writes=("lnvar",),
              tag="ln_var")
        act(va, va, AF.Ln, reads=("lnvar",), writes=("lnsd",),
            bias=hostc[0:1, 0:1], sreads=("hostc",), tag="ln_lnv")
        act(rs_, va, AF.Exp, reads=("lnsd",), writes=("lnrstd",),
            scale=cst[0:1, CCOL["nhalf"]:CCOL["nhalf"] + 1],
            sreads=("cst_nhalf",), tag="ln_rstd")
        mm("PB4", bmb[:, 0:T], onesr[:, :], m, True, True,
           reads=("onesr", "lnm"), tag="ln_mbc")
        mm("PB5", brb[:, 0:T], onesr[:, :], rs_, True, True,
           reads=("onesr", "lnrstd"), tag="ln_rbc")
        for dc in range(nd):
            def _sub(eng, dc=dc, T=T):
                return eng.tensor_tensor(d1b[:, 0:T], ut[:, dc, :],
                                         bmb[:, 0:T], op=ALU.subtract)
            S.add("dve", _sub, reads=(utkeys[dc], "PB4"), writes=("d1b",),
                  tag="ln_sub")
            def _mul(eng, dc=dc, T=T, xt=xt):
                return eng.tensor_tensor(xt[:, dc * T:(dc + 1) * T],
                                         d1b[:, 0:T], brb[:, 0:T],
                                         op=ALU.mult)
            S.add("dve", _mul, reads=("d1b", "PB5"),
                  writes=(f"{xpre}{dc}",), tag="ln_mul")

    def emit_ffn(phase, w1k, w2k, w1sb, w2sb, xin, ut, hsb, hgb, ewbs, ewf,
                 qts, ytm, in_dram, out_dram, out_tm, tile_hook=None):
        T = TA
        for it in range(NTA):
            t0 = it * T
            par = it % len(ut)
            utt = ut[par]
            utkeys = [f"ut_{phase}{par}_{dc}" for dc in range(8)]
            if phase == "A":
                for h in range(2):
                    hsrc = x_d[t0 + h * 256:t0 + (h + 1) * 256, :].rearrange(
                        "(s p) d -> p s d", p=128)
                    dma("sync", xin[:, :, :], hsrc, reads=(),
                        writes=("xin0",), tag="ldx")
                    for dc in range(8):
                        bank = PBK[dc]
                        pb = PB[dc]
                        for sl in range(2):
                            s = h * 2 + sl
                            transpose(bank, pb[:, s * 128:(s + 1) * 128],
                                      xin[:, sl, dc * 128:(dc + 1) * 128],
                                      reads=("xin0",), tag="trin")
                        if h == 1:
                            act(utt[:, dc, :], pb[:, 0:T], AF.Identity,
                                reads=(bank,), writes=(utkeys[dc],),
                                tag="ut_ev")
            else:
                src = in_dram.rearrange("(c p) t -> p c t", p=128)[
                    :, :, t0:t0 + T]
                dma("sync", utt[:, :, :], src,
                    reads=tuple(f"u2d_{it}_{dc}" for dc in range(8)),
                    writes=tuple(utkeys), tag="ldu")
            emit_ln(utt, utkeys, T, PB[4], PB[5], phase + str(it), True)
            nsub = T // 128
            rs_row = rowp[0:1, 1536:1536 + T]
            for s in range(nsub):
                transpose("PB6", PB[6][:, s:s + 1],
                          rs_row[0:1, s * 128:(s + 1) * 128],
                          reads=("lnrstd",),
                          ident_ap=identt[0:1, 0:1], tag="tr_rstd")
            copy(rpk[:, 50:50 + nsub], PB[6][:, 0:nsub], reads=("PB6",),
                 writes=("rstd_tm",), tag="rstd_tm")
            if phase == "A" and it == 0:
                for dc in range(8):
                    mm("PB6", PB[6][0:1, 8:14], onesc[:, :],
                       pwtt[:, dc * 6:(dc + 1) * 6], dc == 0, dc == 7,
                       reads=("pwt", "onesc"), tag="srow_mm")
                copy(rpk[0:1, 56:62], PB[6][0:1, 8:14], reads=("PB6",),
                     writes=("srow",), tag="srow")
            nm_row = rowp[0:1, 512:512 + T]
            for s in range(nsub):
                zr = PB[2][:, 0:6] if s % 2 == 0 else PB[3][:, 0:6]
                zrk = "PB2" if s % 2 == 0 else "PB3"
                for dc in range(8):
                    mm(zrk, zr, utt[:, dc, s * 128:(s + 1) * 128],
                       pwtt[:, dc * 6:(dc + 1) * 6], dc == 0, False,
                       reads=(utkeys[dc], "pwt"), tag="zu")
                mm(zrk, zr, nm_row[0:1, s * 128:(s + 1) * 128],
                   rpk[0:1, 56:62], False, True,
                   reads=("lnnm", "srow"), tag="zfix")
                zt = rpk[:, 38:44]
                ts(zt, zr, rpk[:, 50 + s:51 + s], None, ALU.mult, None,
                   reads=(zrk,), writes=("zt",), sreads=("rstd_tm",),
                   tag="zts")
                qv = rpk[:, 44:50]
                act(qv, zt, AF.Tanh, reads=("zt",), writes=("qv",),
                    scale=cc_ap("iqt"), sreads=("cst_iqt",), tag="qtanh")
                transpose("PB7", PB[7][0:6, 0:128], qv, reads=("qv",),
                          tag="qtr")
                copy(qts[0:6, 0:128], PB[7][0:6, 0:128], reads=("PB7",),
                     writes=("qTs",), tag="qTs")
                mm("PB7", PB[7][:, 128:134], qts[0:6, 0:128],
                   ancht[:, :], True, True, reads=("qTs", "anch"), tag="dot")
                lg = rpk[:, 8:14]
                ts(lg, PB[7][:, 128:134], -6.0, cc_ap("hirb"), ALU.add,
                   ALU.mult, reads=("PB7",), writes=("lg",),
                   sreads=("cst_hirb",), tag="lgts")
                m1 = rpk[:, 0:1]
                reduce(m1, lg, ALU.max, reads=("lg",), writes=("m1",))
                mask1 = rpk[:, 14:20]
                ts(mask1, lg, m1, None, ALU.is_ge, None, reads=("lg",),
                   writes=("mask1",), sreads=("m1",), tag="mask1")
                l2 = rpk[:, 20:26]
                stt(l2, mask1, -1e30, lg, ALU.mult, ALU.add,
                    reads=("mask1", "lg"), writes=("l2",), tag="l2")
                m2 = rpk[:, 1:2]
                reduce(m2, l2, ALU.max, reads=("l2",), writes=("m2",))
                mask2 = rpk[:, 26:32]
                ts(mask2, l2, m2, None, ALU.is_ge, None, reads=("l2",),
                   writes=("mask2",), sreads=("m2",), tag="mask2")
                d21 = rpk[:, 2:3]
                ts(d21, m2, m1, None, ALU.subtract, None, reads=("m2",),
                   writes=("d21",), sreads=("m1",), tag="d21")
                w1s = rpk[:, 5:6]
                act(w1s, d21, AF.Sigmoid, reads=("d21",), writes=("w1s",),
                    scale=cst[:, NEG1_COL:NEG1_COL + 1],
                    sreads=("cst_neg1",), tag="w1s")
                w2s = rpk[:, 6:7]
                act(w2s, d21, AF.Sigmoid, reads=("d21",), writes=("w2s",),
                    tag="w2s")
                ew1 = rpk[:, 32:38]
                ts(ew1, mask1, w1s, None, ALU.mult, None, reads=("mask1",),
                   writes=("ew1",), sreads=("w1s",), tag="ew1")
                stt(ew1, mask2, w2s, ew1, ALU.mult, ALU.add,
                    reads=("mask2", "ew1"), writes=("ew2",), sreads=("w2s",),
                    tag="ew2")
                if phase == "A" and it == 1 and s == 0 and dbg_rpk is not None:
                    dma("sync", dbg_rpk[:, :], rpk[:, :],
                        reads=("ew2", "lg", "m1", "m2", "mask1", "mask2",
                               "zt", "qv", "d21", "e21", "w1s", "w2s",
                               "rstd_tm"),
                        writes=("dbgr",), tag="dbg")
                    dma("sync", dbg_cst[:, :], cst[:, :],
                        reads=tuple("cst_" + k for k in
                                    ["w0b", "w1b", "nw2b", "hirb", "iqt"]),
                        writes=("dbgc",), tag="dbg")
                    dma("sync", dbg_sm[0:1, 0:3], wsm[:, :], reads=("wsm",),
                        writes=("dbgs1",), tag="dbg")
                    dma("sync", dbg_sm[0:1, 4:12], dsm[:, :],
                        reads=("dsm_max", "dsm_sum", "dsm_rec"),
                        writes=("dbgs2",), tag="dbg")
                    dma("sync", dbg_sm[0:1, 12:16], rts[:, :],
                        reads=("rtc", "rtrec", "hir1"),
                        writes=("dbgs3",), tag="dbg")
                for e_ in range(NE):
                    transpose("PB7", PB[7][0:1, 384:512], ew1[:, e_:e_ + 1],
                              reads=("ew2",), tag="ewtr")
                    copy(ewf[0:1, e_ * T + s * 128:e_ * T + (s + 1) * 128],
                         PB[7][0:1, 384:512], reads=("PB7",),
                         writes=(f"ewf{e_}",), tag="ewf")
            for e_ in range(NE):
                ewb = PB[6]
                mm("PB6", ewb[:, 0:T], onesrb[:, :],
                   ewf[0:1, e_ * T:(e_ + 1) * T], True, True,
                   reads=("onesrb", f"ewf{e_}"), tag="ewb")
                ewsb = ewbs[e_ % 2]
                act(ewsb[:, 0:T], ewb[:, 0:T], AF.Identity, reads=("PB6",),
                    writes=(f"ewsb{e_ % 2}",), tag="ewb_ev")
                for hc in range(4):
                    hb = PBK[hc % 2]
                    hpb = PB[hc % 2]
                    for dc in range(8):
                        lhs = w1sb[:, (e_ * 8 + dc) * 512 + hc * 128:
                                   (e_ * 8 + dc) * 512 + hc * 128 + 128]
                        mm(hb, hpb[:, 0:T], lhs,
                           xlnbf[:, dc * T:(dc + 1) * T], dc == 0, dc == 7,
                           reads=(w1k[e_], f"xln{dc}"), tag="w1mm")
                    hgv = hgb[hc % 2]
                    act(hgv[:, 0:T], hpb[:, 0:T], AF.Gelu, reads=(hb,),
                        writes=(f"hg{hc % 2}",), tag="gelu")
                    def _hs(eng, e_=e_, hc=hc, T=T, hgv=hgv, ewsb=ewsb):
                        return eng.tensor_tensor(
                            hsb[:, (e_ * 4 + hc) * T:(e_ * 4 + hc + 1) * T],
                            hgv[:, 0:T], ewsb[:, 0:T], op=ALU.mult)
                    S.add("pool", _hs, reads=(f"hg{hc % 2}", f"ewsb{e_ % 2}"),
                          writes=(f"hs{e_}_{hc}",), tag="hscale")
            if phase == "A" and it == 1 and dbg_ut is not None:
                dma("sync", dbg_ut[:, :, :], utt[:, :, :],
                    reads=tuple(utkeys), writes=("dbg1",), tag="dbg")
                dma("pool", dbg_xln[:, :], xlnbf[:, 0:8 * T],
                    reads=tuple(f"xln{d_}" for d_ in range(8)),
                    writes=("dbg2",), tag="dbg")
                dma("pool", dbg_ewf[:, :], ewf[0:1, :],
                    reads=tuple(f"ewf{e_}" for e_ in range(NE)),
                    writes=("dbg3",), tag="dbg")
                dma("pool", dbg_hsb[:, :], hsb[:, :],
                    reads=tuple(f"hs{e_}_{h_}" for e_ in range(NE)
                                for h_ in range(4)),
                    writes=("dbg4",), tag="dbg")
            for dc in range(8):
                ob = PBK[4 + (dc % 2)]
                opb = PB[4 + (dc % 2)]
                first = True
                for e_ in range(NE):
                    for hc in range(4):
                        lhs = w2sb[:, (e_ * 4 + hc) * 1024 + dc * 128:
                                   (e_ * 4 + hc) * 1024 + dc * 128 + 128]
                        mm(ob, opb[:, 0:T], lhs,
                           hsb[:, (e_ * 4 + hc) * T:(e_ * 4 + hc + 1) * T],
                           first, (e_ == NE - 1) and (hc == 3),
                           reads=(w2k[e_], f"hs{e_}_{hc}"), tag="w2mm")
                        first = False
                eb = ebuf[dc % 2]
                stt(eb[:, 0:T], opb[:, 0:T], 0.5, utt[:, dc, :], ALU.mult,
                    ALU.add, reads=(ob, utkeys[dc]), writes=(f"ebuf{dc % 2}",),
                    sreads=(), tag="res_ev")
                if not out_tm:
                    dst = out_dram[dc * 128:(dc + 1) * 128, t0:t0 + T]
                    dma("act", dst, eb[:, 0:T], reads=(f"ebuf{dc % 2}",),
                        writes=(f"u1d_{it}_{dc}",), tag="st_u1")
                    if dc == 7 and tile_hook is not None:
                        tile_hook(it)
                else:
                    for s in range(T // 128):
                        bank = PBK[6 + (s % 2)]
                        pb = PB[6 + (s % 2)]
                        transpose(bank, pb[:, 0:128],
                                  eb[:, s * 128:(s + 1) * 128],
                                  reads=(f"ebuf{dc % 2}",), tag="trout")
                        yp = ytm[s % 2]
                        act(yp[:, 0:128], pb[:, 0:128],
                            AF.Identity, reads=(bank,),
                            writes=(f"yp{s % 2}",), tag="ytm_ev")
                        dst = y_d[t0 + s * 128:t0 + (s + 1) * 128,
                                  dc * 128:(dc + 1) * 128]
                        dma("sync", dst, yp[:, 0:128],
                            reads=(f"yp{s % 2}",),
                            writes=(f"yd_{it}_{dc}_{s}",), tag="st_y")

    st_a = ExitStack()
    w1sb = sb("w1sb", [128, NE * 8 * 512], BF16, st_a)
    w2sb = sb("w2sb", [128, NE * 4 * 1024], BF16, st_a)
    xin = sb("xin0t", [128, 2, D], F32, st_a)
    utA = [sb("utA0", [128, 8, TA], F32, st_a)]
    hsb = sb("hsb", [128, NE * 4 * TA], BF16, st_a)
    hgb = [sb(f"hgb{i}", [128, TA], BF16, st_a) for i in range(2)]
    ewbsA = [sb(f"ewbs{i}", [128, TA], BF16, st_a) for i in range(2)]
    ewfA = sb("ewfA", [1, NE * TA], BF16, st_a)
    qtsA = sb("qtsA", [6, 128], F32, st_a)
    ipstg = sb("ipstg", [128, 4096], BF16, st_a)

    w1keys = [f"w1_{e}" for e in range(NE)]
    w2keys = [f"w2_{e}" for e in range(NE)]
    for e in range(NE):
        src = w1_d[e].rearrange("(dc p) h -> p dc h", p=128)
        dst = w1sb[:, e * 4096:(e + 1) * 4096].rearrange(
            "p (dc h) -> p dc h", dc=8)
        dma("pool", dst, src, reads=(), writes=(w1keys[e],), tag="ldw1")
    for e in range(NE):
        src = w2_d[e].rearrange("(hc p) d -> p hc d", p=128)
        dst = w2sb[:, e * 4096:(e + 1) * 4096].rearrange(
            "p (hc d) -> p hc d", hc=4)
        dma("pool", dst, src, reads=(), writes=(w2keys[e],), tag="ldw2")
    def stage_hook(it):
        for g in (2 * it, 2 * it + 1):
            if g < 8:
                gsrc = ipw_d[:, g * 512:(g + 1) * 512].rearrange(
                    "(dc p) c -> p dc c", p=128)
                dma("pool", ipstg[:, :].rearrange("p (dc c) -> p dc c", dc=8),
                    gsrc, reads=(), writes=("ipstg",), tag="ipcast")
                dma("sync", ipbf_d[g], ipstg[:, :], reads=("ipstg",),
                    writes=(f"ipbf{g}",), tag="ipstore")
        q = it - 4
        if 0 <= q < 4:
            qsrc = opw_d.rearrange("(kc p) d -> p kc d", p=128)[
                :, 4 * q:4 * q + 4, :]
            dma("pool", ipstg[:, :].rearrange("p (kc d) -> p kc d", kc=4),
                qsrc, reads=(), writes=("ipstg",), tag="opcast")
            for dcq in range(8):
                ssrc = ipstg[:, :].rearrange("p (kc d) -> p kc d", kc=4)[
                    :, :, dcq * 128:(dcq + 1) * 128]
                sdst = opbf_d[dcq][:, q * 512:(q + 1) * 512].rearrange(
                    "p (kc w) -> p kc w", kc=4)
                dma("sync", sdst, ssrc, reads=("ipstg",),
                    writes=(f"opbf{dcq}_{q}",), tag="opstore")

    emit_ffn("A", w1keys, w2keys, w1sb, w2sb, xin, utA, hsb, hgb, ewbsA,
             ewfA, qtsA, None, None, u1_d, False, tile_hook=stage_hook)
    st_a.close()
    S.barrier()

    st_b = ExitStack()
    ipb = [sb(f"ipb{i}", [128, 8 * 512], BF16, st_b) for i in range(3)]
    ipsr = sb("ipsr", [128, 8 * 32], BF16, st_b)
    opd = [sb(f"opd{i}", [128, 2048], BF16, st_b) for i in range(2)]
    xlnb2 = sb("xlnb2", [128, 4096], BF16, st_b)
    utB = [sb(f"utB{i}", [128, 8, TB], F32, st_b) for i in range(2)]
    zb = [sb(f"zb{i}", [128, TB + 8], F32, st_b) for i in range(3)]
    cvA = sb("cvA", [128, TB], F32, st_b)
    cvB = sb("cvB", [128, TB], F32, st_b)
    xvb = [sb(f"xvb{i}", [128, TB], F32, st_b) for i in range(2)]
    ph3b = sb("ph3b", [128, TB], F32, st_b)
    mib = sb("mib", [128, TB], I32, st_b)
    r2b = sb("r2b", [128, TB], F32, st_b)
    sinb2 = [sb(f"sinb{i}", [128, TB], F32, st_b) for i in range(2)]
    cosb2 = [sb(f"cosb{i}", [128, TB], F32, st_b) for i in range(2)]
    sqvb = [sb("sqvb0", [128, TB], F32, st_b)] * 2
    t1b = sb("t1b", [128, TB], F32, st_b)
    polyb = sb("polyb", [128, TB], F32, st_b)
    ppb = [sb(f"ppb{i}", [128, TB], F32, st_b) for i in range(2)]
    areb = [sb(f"areb{i}", [128, TB], F32, st_b) for i in range(2)]
    aimb = [sb(f"aimb{i}", [128, TB], F32, st_b) for i in range(2)]
    nreb = [sb(f"nreb{i}", [128, TB], F32, st_b) for i in range(2)]
    nimb = [sb(f"nimb{i}", [128, TB], F32, st_b) for i in range(2)]
    nbfre = [sb(f"nbfre{i}", [128, TB], BF16, st_b) for i in range(2)]
    nbfim = [sb(f"nbfim{i}", [128, TB], BF16, st_b) for i in range(2)]
    sqbf = [sb(f"sqbf{i}", [128, TB], BF16, st_b) for i in range(4)]
    yvb = sb("yvb", [128, NVC * TB], BF16, st_b)
    gateb = [sb(f"gateb{i}", [128, TB], BF16, st_b) for i in range(2)]
    spk = sb("spk", [32, 4 * TB], F32, st_b)
    twtb = sb("twtb", [32, TB], F32, st_b)
    dsl = sb("dsl", [32, TB], F32, st_b)
    svalb = sb("svalb", [32, TB], F32, st_b)
    pwbb = sb("pwbb", [32, TB], BF16, st_b)
    svbb = sb("svbb", [32, TB], BF16, st_b)
    ysc = polyb

    def ld_op(dcq):
        dma("sync", opd[dcq % 2][:, :], opbf_d[dcq],
            reads=tuple(f"opbf{dcq}_{q}" for q in range(4)),
            writes=(f"opd{dcq % 2}",), tag="ldop")
    srsrc = ipw_d[:, 4096:4128].rearrange("(dc p) c -> p dc c", p=128)
    dma("pool", ipsr[:, :].rearrange("p (dc c) -> p dc c", dc=8), srsrc,
        reads=(), writes=("ipsr",), tag="ldipsr")

    def ip_lhs(gpar, dcc, cc_in_g):
        return ipb[gpar][:, dcc * 512 + cc_in_g * 128:
                         dcc * 512 + cc_in_g * 128 + 128]

    def ld_ipg(gidx):
        g = gidx % 8
        dma("sync", ipb[gidx % 3][:, :], ipbf_d[g], reads=(f"ipbf{g}",),
            writes=(f"ipg{gidx % 3}",), tag="ldipg")

    xlnbufs = [xlnbf, xlnb2]

    def ld_ut_b(it):
        par = it % 2
        utt = utB[par]
        utkeys = [f"utB{par}_{dc}" for dc in range(8)]
        src = u1_d.rearrange("(c p) t -> p c t", p=128)[
            :, :, it * TB:(it + 1) * TB]
        dma("sync", utt[:, :, :], src,
            reads=tuple(f"u1d_{it}_{dc}" for dc in range(8)),
            writes=tuple(utkeys), tag="ldu1")

    def ln_b(it):
        par = it % 2
        emit_ln(utB[par], [f"utB{par}_{dc}" for dc in range(8)], TB,
                PB[4], PB[5], "B" + str(it), False,
                xt=xlnbufs[par], xpre=f"xlnB{par}_")

    ld_ut_b(0)
    ln_b(0)
    for it in range(NTB):
        t0 = it * TB
        par = it % 2
        utt = utB[par]
        utkeys = [f"utB{par}_{dc}" for dc in range(8)]
        xlt = xlnbufs[par]
        xlk = [f"xlnB{par}_{dc}" for dc in range(8)]
        if it == 0:
            for g in range(3):
                ld_ipg(g)
        dma("sync", dsl[:, :], dist_d[:, t0:t0 + TB], reads=(),
            writes=("dsl",), tag="lddist")
        act(twtb[:, :], dsl[:, :], AF.Exp, reads=("dsl",), writes=("twt",),
            scale=negslope[:, 0:1], sreads=("negslope",), tag="tw")

        zrot = [0]
        ZB3 = (0, 1, 7)

        def chunk_z(cc, outbuf, outkey, tag):
            gpar = (it * 8 + cc // 4) % 3
            gkey = f"ipg{gpar}"
            zbi = ZB3[zrot[0] % 3]
            bank = PBK[zbi]
            pb = PB[zbi]
            wk = 128 if cc < 32 else 32
            for dc in range(8):
                if cc < 32:
                    lhs = ip_lhs(gpar, dc, cc % 4)
                    rk = gkey
                else:
                    lhs = ipsr[:, dc * 32:(dc + 1) * 32]
                    rk = "ipsr"
                mm(bank, pb[0:wk, 0:TB], lhs,
                   xlt[:, dc * TB:(dc + 1) * TB], dc == 0, dc == 7,
                   reads=(rk, xlk[dc]), tag="ipmm")
            zr = zb[zrot[0] % 3]
            zrk = f"zb{zrot[0] % 3}"
            zrot[0] += 1
            act(zr[0:wk, 3:3 + TB], pb[0:wk, 0:TB], AF.Identity,
                reads=(bank,), writes=(zrk,), tag="zev")
            def _ti(eng, zr=zr, cc=cc, wk=wk):
                return eng.tensor_copy(zr[0:wk, 0:3],
                                       tails[0:wk, cc * 3:cc * 3 + 3])
            S.add("pool", _ti, reads=(f"tail{cc}",), writes=(zrk + "_t",),
                  tag="ti")
            def _tu(eng, zr=zr, cc=cc, wk=wk):
                return eng.tensor_copy(tails[0:wk, cc * 3:cc * 3 + 3],
                                       zr[0:wk, TB:TB + 3])
            S.add("pool", _tu, reads=(zrk,), writes=(f"tail{cc}",), tag="tu")
            k3 = ckt[0:wk, cc * 4 + 3:cc * 4 + 4]
            cbv = cbt[0:wk, cc:cc + 1]
            act(cvA[0:wk, :], zr[0:wk, 3:3 + TB], AF.Identity,
                reads=(zrk,), writes=("cvA",), scale=k3, bias=cbv,
                sreads=("ck", "cb"), tag="cv1")
            stt(cvB[0:wk, :], zr[0:wk, 2:2 + TB],
                ckt[0:wk, cc * 4 + 2:cc * 4 + 3], cvA[0:wk, :], ALU.mult,
                ALU.add, reads=(zrk, "cvA", zrk + "_t"), writes=("cvB",),
                sreads=("ck",), tag="cv2")
            stt(cvA[0:wk, :], zr[0:wk, 1:1 + TB],
                ckt[0:wk, cc * 4 + 1:cc * 4 + 2], cvB[0:wk, :], ALU.mult,
                ALU.add, reads=(zrk, zrk + "_t", "cvB"), writes=("cvA",),
                sreads=("ck",), tag="cv3")
            stt(outbuf, zr[0:wk, 0:TB], ckt[0:wk, cc * 4:cc * 4 + 1],
                cvA[0:wk, :], ALU.mult, ALU.add,
                reads=(zrk, zrk + "_t", "cvA"), writes=(outkey,),
                sreads=("ck",), tag=tag)

        zs = spk[:, 0:TB]
        chunk_z(32, zs[0:32, :], "zs", "cv_s")
        pwv = spk[:, TB:2 * TB]
        ts(pwv, zs, scoret[:, 0:1], 20.0, ALU.mult, ALU.min, reads=("zs",),
           writes=("pws1",), sreads=("score",), tag="score")
        ts(pwv, pwv, -20.0, None, ALU.max, None, reads=("pws1",),
           writes=("pws2",), tag="clip")
        act(pwv, pwv, AF.Exp, reads=("pws2",), writes=("pexp",), tag="pexp")
        tt(pwbb[:, :], pwv, twtb[:, :], ALU.mult, reads=("pexp", "twt"),
           writes=("pw",), tag="pw")
        denv = spk[:, 2 * TB:3 * TB]
        scan(denv, ones_scan[0:32, :], pwbb[:, :], carry_den[:, 0:1],
             reads=("ones_scan", "pw"), writes=("den",), sreads=("cden",),
             tag="denscan")
        copy(carry_den[:, 0:1], denv[:, TB - 1:TB], reads=("den",),
             writes=("cden",), tag="dencarry")
        ts(denv, denv, 1e-4, None, ALU.max, None, reads=("den",),
           writes=("denc",), tag="denc")

        def val_m0(cc):
            p2 = cc % 2
            chunk_z(cc, xvb[p2][:, :], f"xv{p2}", "cv_xv")
            if cc % 4 == 3 and it * 8 + cc // 4 + 3 < NTB * 8:
                ld_ipg(it * 8 + cc // 4 + 3)
            mm(PBK[2 + p2], PB[2 + p2][:, 0:TB],
               selt[:, cc * 128:(cc + 1) * 128], pwbb[:, :], True, True,
               reads=("sel", "pw"), tag="pwx")

        def val_m1(cc):
            p2 = cc % 2
            xvv = xvb[p2]
            pxb = PBK[2 + p2]
            pxp = PB[2 + p2]
            act(ph3b[:, :], xvv[:, :], AF.Identity, reads=(f"xv{p2}",),
                writes=("ph3",), scale=theta2[:, cc:cc + 1],
                bias=cc_ap("c256"), sreads=("theta2", "cst_c256"), tag="ph3")
            act(mib[:, :], ph3b[:, :], AF.Identity, reads=("ph3",),
                writes=("mi",), tag="mi")
            stt(r2b[:, :], mib[:, :], -1.0, ph3b[:, :], ALU.mult, ALU.add,
                reads=("mi", "ph3"), writes=("r2",), tag="r2")
            act(sinb2[p2][:, :], r2b[:, :], AF.Sin, reads=("r2",),
                writes=(f"sin{p2}",), scale=cc_ap("n2pi"),
                sreads=("cst_n2pi",), tag="sin")
            act(ph3b[:, :], r2b[:, :], AF.Abs, reads=("r2",),
                writes=("absr", "ph3"), scale=cc_ap("p2pi"),
                sreads=("cst_p2pi",), tag="absr")
            act(cosb2[p2][:, :], ph3b[:, :], AF.Sin, reads=("absr",),
                writes=(f"cos{p2}",), bias=cc_ap("npih"),
                sreads=("cst_npih",), tag="cos")
            sqv = sqvb[p2]
            act(sqv[:, :], xvv[:, :], AF.Square, reads=(f"xv{p2}",),
                writes=("sqv",), tag="sqv")
            act(t1b[:, :], xvv[:, :], AF.Identity, reads=(f"xv{p2}",),
                writes=("t1",), scale=cc_ap("w1b"), bias=cc_ap("w0b"),
                sreads=("cst_w1b", "cst_w0b"), tag="t1")
            stt(polyb[:, :], sqv[:, :], cc_ap("nw2b"), t1b[:, :], ALU.mult,
                ALU.add, reads=("sqv", "t1"), writes=("poly",),
                sreads=("cst_nw2b",), tag="poly")
            ppv = ppb[p2]
            tt(ppv[:, :], polyb[:, :], pxp[:, 0:TB], ALU.mult,
               reads=("poly", pxb), writes=(f"pp{p2}",), tag="pp")
            tt(areb[p2][:, :], ppv[:, :], cosb2[p2][:, :], ALU.mult,
               reads=(f"pp{p2}", f"cos{p2}"), writes=(f"are{p2}",),
               stream="pool", tag="are")
            tt(aimb[p2][:, :], ppv[:, :], sinb2[p2][:, :], ALU.mult,
               reads=(f"pp{p2}", f"sin{p2}"), writes=(f"aim{p2}",),
               stream="pool", tag="aim")

        def val_m2a(cc):
            p2 = cc % 2
            nre = nreb[p2]
            nim = nimb[p2]
            scan(nre[:, :], ones_scan[:, :], areb[p2][:, :],
                 carry_re[:, cc:cc + 1], reads=("ones_scan", f"are{p2}"),
                 writes=(f"nre{p2}",), sreads=(f"cre{cc}",), tag="scan_re")
            copy(carry_re[:, cc:cc + 1], nre[:, TB - 1:TB],
                 reads=(f"nre{p2}",), writes=(f"cre{cc}",), stream="pool",
                 tag="cre")
            scan(nim[:, :], ones_scan[:, :], aimb[p2][:, :],
                 carry_im[:, cc:cc + 1], reads=("ones_scan", f"aim{p2}"),
                 writes=(f"nim{p2}",), sreads=(f"cim{cc}",), tag="scan_im")
            copy(carry_im[:, cc:cc + 1], nim[:, TB - 1:TB],
                 reads=(f"nim{p2}",), writes=(f"cim{cc}",), stream="pool",
                 tag="cim")

        def val_m2b(cc):
            p2 = cc % 2
            nre = nreb[p2]
            nim = nimb[p2]
            copy(nbfre[p2][:, :], nre[:, :], reads=(f"nre{p2}",),
                 writes=(f"nbfre{p2}",), tag="nbfre")
            copy(nbfim[p2][:, :], nim[:, :], reads=(f"nim{p2}",),
                 writes=(f"nbfim{p2}",), stream="pool", tag="nbfim")
            tt(sqbf[p2][:, :], nbfre[p2][:, :], nbfre[p2][:, :], ALU.mult,
               reads=(f"nbfre{p2}",), writes=(f"sqbf{p2}",), tag="sqre")
            tt(sqbf[2 + p2][:, :], nbfim[p2][:, :], nbfim[p2][:, :], ALU.mult,
               reads=(f"nbfim{p2}",), writes=(f"sqbf{2 + p2}",), tag="sqim")

        def val_m2c(cc):
            p2 = cc % 2
            mm("PB6", PB[6][0:32, 0:TB], sel2t[:, cc * 32:(cc + 1) * 32],
               sqbf[p2][:, :], cc == 0, False, reads=("sel2", f"sqbf{p2}"),
               tag="ssq_re")
            mm("PB6", PB[6][0:32, 0:TB], sel2t[:, cc * 32:(cc + 1) * 32],
               sqbf[2 + p2][:, :], False, cc == NVC - 1,
               reads=("sel2", f"sqbf{2 + p2}"), tag="ssq_im")
            yb_ = PBK[4 + p2]
            ypb = PB[4 + p2]
            mm(yb_, ypb[:, 0:TB], wrebd[:, :], nbfre[p2][:, :],
               True, False, reads=("wrebd", f"nbfre{p2}"), tag="ymm")
            mm(yb_, ypb[:, 0:TB], wimbd[:, :], nbfim[p2][:, :],
               False, True, reads=("wimbd", f"nbfim{p2}"), tag="ymm")
            act(yvb[:, cc * TB:(cc + 1) * TB], ypb[:, 0:TB], AF.Identity,
                reads=(yb_,), writes=(f"yv{cc}",), tag="yv")

        for tstep in range(NVC + 3):
            if tstep < NVC:
                val_m0(tstep)
            if 1 <= tstep <= NVC:
                val_m1(tstep - 1)
            if 2 <= tstep <= NVC + 1:
                val_m2a(tstep - 2)
                val_m2b(tstep - 2)
            if tstep >= 3:
                val_m2c(tstep - 3)

        if it == 1 and dbg_b is not None:
            dma("sync", dbg_b[:, 0:TB], xvb[1][:, :], reads=("xv1",),
                writes=("dbgb0",), tag="dbg")
            dma("sync", dbg_b[:, TB:2 * TB], nreb[1][:, :], reads=("nre1",),
                writes=("dbgb1",), tag="dbg")
            dma("sync", dbg_b[:, 2 * TB:3 * TB], nimb[1][:, :], reads=("nim1",),
                writes=("dbgb2",), tag="dbg")
            dma("pool", dbg_b[:, 3 * TB:4 * TB], xlnbf[:, 0:TB],
                reads=("xln0",), writes=("dbgb3",), tag="dbg")
            dma("sync", dbg_b[:, 4 * TB:5 * TB], sinb2[1][:, :], reads=("sin1",),
                writes=("dbgb4",), tag="dbg")
            dma("sync", dbg_b[:, 5 * TB:6 * TB], ppb[1][:, :], reads=("pp1",),
                writes=("dbgb5",), tag="dbg")
            dma("sync", dbg_bs[:, 0:TB], spk[:, TB:2 * TB], reads=("pw",),
                writes=("dbgs0",), tag="dbg")
            dma("sync", dbg_bs[:, TB:2 * TB], spk[:, 2 * TB:3 * TB],
                reads=("denc",), writes=("dbgs1",), tag="dbg")
        if it + 1 < NTB:
            ld_ut_b(it + 1)
            ln_b(it + 1)
        sv = spk[:, 3 * TB:4 * TB]
        tmp = spk[:, TB:2 * TB]
        ts(tmp, PB[6][0:32, 0:TB], 1.0 / 128.0, None, ALU.mult, None,
           reads=("PB6",), writes=("ssq_s", "pexp"), tag="ssq_s")
        act(sv, denv, AF.Square, reads=("denc",), writes=("dc2",), tag="dc2")
        stt(spk[:, 0:TB], sv, 1e-5, tmp, ALU.mult, ALU.add,
            reads=("dc2", "ssq_s"), writes=("inner",), tag="inner")
        act(sv, spk[:, 0:TB], AF.Ln, reads=("inner",), writes=("rt2",),
            tag="rt2")
        act(svbb[:, :], sv, AF.Exp, reads=("rt2",), writes=("sval",),
            scale=cst[0:32, CCOL["nhalf"]:CCOL["nhalf"] + 1],
            sreads=("cst_nhalf",), tag="sval")

        if it == 1 and dbg_b2 is not None:
            dma("pool", dbg_b2[:, 0:TB], yvb[:, 15 * TB:16 * TB],
                reads=("yv15",), writes=("dbgy0",), tag="dbg")
            dma("pool", dbg_b2[:, TB:2 * TB], sqbf[0][:, :],
                reads=("sqbf0",), writes=("dbgy1",), tag="dbg")
            dma("sync", dbg_bs[:, 2 * TB:3 * TB], svalb[:, :],
                reads=("sval",), writes=("dbgy2",), tag="dbg")
        def gate_g0(g):
            cc = 16 + g
            p2 = g % 2
            chunk_z(cc, xvb[p2][:, :], f"xv{p2}", "cv_g")
            if cc % 4 == 3 and it * 8 + cc // 4 + 3 < NTB * 8:
                ld_ipg(it * 8 + cc // 4 + 3)

        def gate_g1(g):
            p2 = g % 2
            gv = gateb[p2]
            act(gv[:, :], xvb[p2][:, :], AF.Silu, reads=(f"xv{p2}",),
                writes=(f"gate{p2}",), tag="silu")
            sxb = PBK[2 + p2]
            sxp = PB[2 + p2]
            mm(sxb, sxp[:, 0:TB], selt[:, g * 128:(g + 1) * 128],
               svbb[:, :], True, True, reads=("sel", "sval"), tag="sexp")
            tt(ysc[:, :], yvb[:, g * TB:(g + 1) * TB], sxp[:, 0:TB],
               ALU.mult, reads=(f"yv{g}", sxb), writes=("ysc",), tag="yscale")
            tt(yvb[:, g * TB:(g + 1) * TB], ysc[:, :], gv[:, :], ALU.mult,
               reads=("ysc", f"gate{p2}"), writes=(f"yv{g}",), tag="ygate")

        for tstep in range(NGC + 1):
            if tstep < NGC:
                gate_g0(tstep)
            if tstep >= 1:
                gate_g1(tstep - 1)

        ld_op(0)
        ld_op(1)
        for dc in range(8):
            ob = PBK[dc % 2]
            opb = PB[dc % 2]
            for kc in range(16):
                lhs = opd[dc % 2][:, kc * 128:(kc + 1) * 128]
                mm(ob, opb[:, 0:TB], lhs, yvb[:, kc * TB:(kc + 1) * TB],
                   kc == 0, kc == 15, reads=(f"opd{dc % 2}", f"yv{kc}"),
                   tag="opmm")
            if dc + 2 < 8:
                ld_op(dc + 2)
            eb = ebuf[dc % 2]
            def _res(eng, opb=opb, eb=eb, utt=utt, dc=dc):
                return eng.tensor_tensor(eb[:, :], opb[:, 0:TB],
                                         utt[:, dc, :], op=ALU.add)
            S.add("dve", _res, reads=(ob, utkeys[dc]),
                  writes=(f"ebuf{dc % 2}",), tag="res_b")
            dst = u2_d[dc * 128:(dc + 1) * 128, t0:t0 + TB]
            dma("act", dst, eb[:, :], reads=(f"ebuf{dc % 2}",),
                writes=(f"u2d_{it}_{dc}",), tag="st_u2")

    st_b.close()
    S.barrier()

    st_c = ExitStack()
    w1sbC = sb("w1sbC", [128, NE * 8 * 512], BF16, st_c)
    w2sbC = sb("w2sbC", [128, NE * 4 * 1024], BF16, st_c)
    utC = [sb(f"utC{i}", [128, 8, TA], F32, st_c) for i in range(2)]
    hsbC = sb("hsbC", [128, NE * 4 * TA], BF16, st_c)
    hgbC = [sb(f"hgbC{i}", [128, TA], BF16, st_c) for i in range(2)]
    ewbsC = [sb(f"ewbsC{i}", [128, TA], BF16, st_c) for i in range(2)]
    ewfC = sb("ewfC", [1, NE * TA], BF16, st_c)
    qtsC = sb("qtsC", [6, 128], F32, st_c)
    ytmC = [sb(f"ytmC{i}", [128, 128], F32, st_c) for i in range(2)]

    w1keysC = [f"w1C_{e}" for e in range(NE)]
    w2keysC = [f"w2C_{e}" for e in range(NE)]
    for e in range(NE):
        src = w1_d[e].rearrange("(dc p) h -> p dc h", p=128)
        dst = w1sbC[:, e * 4096:(e + 1) * 4096].rearrange(
            "p (dc h) -> p dc h", dc=8)
        dma("pool", dst, src, reads=(), writes=(w1keysC[e],), tag="ldw1C")
    for e in range(NE):
        src = w2_d[e].rearrange("(hc p) d -> p hc d", p=128)
        dst = w2sbC[:, e * 4096:(e + 1) * 4096].rearrange(
            "p (hc d) -> p hc d", hc=4)
        dma("pool", dst, src, reads=(), writes=(w2keysC[e],), tag="ldw2C")

    emit_ffn("C", w1keysC, w2keysC, w1sbC, w2sbC, None, utC, hsbC, hgbC,
             ewbsC, ewfC, qtsC, ytmC, u2_d, None, True)
    st_c.close()

    S.finalize()
    doms = sorted(S.totals.keys() | {"pe", "dve", "act", "pool"})
    sem_cm = {}
    for d_ in doms:
        sems[d_] = st_p.enter_context(nc.semaphore("sem_" + d_))
    with st_p, nc.Block() as block:
        @block.tensor
        def _(eng):
            S.emit_stream("pe", eng, sems)

        @block.vector
        def _(eng):
            S.emit_stream("dve", eng, sems)

        @block.scalar
        def _(eng):
            S.emit_stream("act", eng, sems)

        @block.gpsimd
        def _(eng):
            S.emit_stream("pool", eng, sems)

        @block.sync
        def _(eng):
            S.emit_stream("sync", eng, sems)
            for d_ in doms:
                tot = S.totals.get(d_, 0)
                if tot:
                    eng.wait_ge(sems[d_], tot * (16 if d_.startswith("q")
                                                 else 1))
    return nc


_PROG_CACHE = {}


def _host_prep(inputs):
    ns = np.asarray(inputs["L0_attn_norm_scale"], np.float32)
    theta = np.asarray(inputs["L0_attn_theta"], np.float32).reshape(2048)
    conv_k = np.asarray(inputs["L0_attn_conv_k"], np.float32)
    conv_b = np.asarray(inputs["L0_attn_conv_b"], np.float32)
    proj_w = np.asarray(inputs["L0_ffn_proj_w"], np.float32)

    theta_t = np.ascontiguousarray(theta.reshape(16, 128).T)
    ck = np.zeros((128, 33 * 4), np.float32)
    cb = np.zeros((128, 33), np.float32)
    for ccc in range(33):
        w = 128 if ccc < 32 else 32
        for j in range(4):
            ck[:w, ccc * 4 + j] = conv_k[j, 0, ccc * 128:ccc * 128 + w]
        cb[:w, ccc] = conv_b[ccc * 128:ccc * 128 + w]
    score = np.asarray(inputs["L0_attn_score_scale"], np.float32).reshape(32, 1)
    slopes = np.concatenate([
        np.asarray(inputs["L0_attn_decay_slopes"], np.float32),
        np.asarray(inputs["L0_attn_anchor_slopes"], np.float32)]).reshape(32, 1)
    deriv = np.asarray(inputs["L0_attn_deriv_logits"],
                       np.float32).reshape(1, 3)
    wre = np.asarray(inputs["L0_attn_W_re"], np.float32)
    wim = np.asarray(inputs["L0_attn_W_im"], np.float32)
    Wre2 = np.ascontiguousarray(np.concatenate([wre, wre], 0))
    Wim2 = np.ascontiguousarray(np.concatenate([wim, wim], 0))
    nsre2 = np.ascontiguousarray(
        np.concatenate([ns[:64], ns[:64]]).reshape(128, 1))
    nsim2 = np.ascontiguousarray(
        np.concatenate([ns[64:], ns[64:]]).reshape(128, 1))
    pwT = np.zeros((128, 48), np.float32)
    for dc in range(8):
        pwT[:, dc * 6:(dc + 1) * 6] = proj_w[:, dc * 128:(dc + 1) * 128].T
    anchT = np.ascontiguousarray(
        np.asarray(inputs["L0_ffn_anchors"], np.float32).T)
    rt = np.asarray(inputs["L0_ffn_routing_temp"],
                    np.float32).reshape(1, 1)
    SEL = np.zeros((32, 2048), np.float32)
    SEL2 = np.zeros((128, 512), np.float32)
    for ccc in range(16):
        for p in range(128):
            k = 2 * ccc + p // 64
            SEL[k, ccc * 128 + p] = 1.0
            SEL2[p, ccc * 32 + k] = 1.0
    ident = np.eye(128, dtype=np.float32)
    pos = np.arange(L, dtype=np.float32)
    dist = np.zeros((32, L), np.float32)
    dist[0:28, :] = (L - 1) - pos[None, :]
    dist[28:32, :] = pos[None, :]

    shared = {
        "ipw": np.ascontiguousarray(np.asarray(inputs["L0_attn_in_proj_w"],
                                               np.float32)),
        "opw": np.ascontiguousarray(np.asarray(inputs["L0_attn_out_proj_w"],
                                               np.float32)),
        "w1": np.ascontiguousarray(np.asarray(inputs["L0_ffn_w1"],
                                              np.float32)),
        "w2": np.ascontiguousarray(np.asarray(inputs["L0_ffn_w2"],
                                              np.float32)),
        "theta_t": theta_t, "ck": ck, "cb": cb, "score": score,
        "slopes": slopes, "deriv": deriv, "Wre2": Wre2, "Wim2": Wim2,
        "nsre2": nsre2, "nsim2": nsim2, "pwT": pwT, "anchT": anchT,
        "rt": rt, "SEL": SEL, "SEL2": SEL2, "ident": ident, "dist": dist,
        "hostc_in": np.array([[1e-5, 0.0, 0.0, 0.0]], np.float32),
    }
    return shared


def kernel(**inputs):
    x = np.asarray(inputs["L0_attn_x"], np.float32)
    shared = _host_prep(inputs)
    if "prog" not in _PROG_CACHE:
        _PROG_CACHE["prog"] = build_program()
    nc = _PROG_CACHE["prog"]
    in_maps = []
    for b in range(8):
        m = dict(shared)
        m["x"] = np.ascontiguousarray(x[b])
        in_maps.append(m)
    res = run_bass_kernel_spmd(nc, in_maps, core_ids=list(range(8)))
    out = np.stack([res.results[b]["y"] for b in range(8)], axis=0)
    return out.astype(np.float32)


if __name__ == "__main__":
    import reference as _r
    ins = _r.setup_inputs(0)
    out = kernel(**{k: np.asarray(v) for k, v in ins.items()})
    ref = np.asarray(_r.reference(**ins))
    err = np.sqrt(((out - ref) ** 2).mean() / (ref ** 2).mean())
    print("rel l2:", err)


# version 16
# speedup vs baseline: 1.1850x; 1.1850x over previous
import math
import numpy as np
from contextlib import ExitStack

import concourse.bass as bass
import concourse.mybir as mybir
from concourse.bass_utils import run_bass_kernel_spmd

F32 = mybir.dt.float32
BF16 = mybir.dt.bfloat16
I32 = mybir.dt.int32
AF = mybir.ActivationFunctionType
ALU = mybir.AluOpType
AX = mybir.AxisListType
PI = float(np.pi)

L = 4096
D = 1024
DI = 2048
C = 4128
K = 32
H = 64
NE = 6
DE = 512
TA = 512
TB = 512
NTA = L // TA
NTB = L // TB
NVC = 16
NGC = 16


class Op:
    __slots__ = ("stream", "dom", "emit", "deps", "need_inc", "inc_no",
                 "dma", "tag", "gidx", "sidx", "risky")


class Sched:

    def __init__(self):
        self.ops = []
        self.bufs = {}
        self.pending_barrier = None
        self.barrier_absorbed = set()
        self.scount = {}

    def add(self, stream, emit, reads=(), writes=(), sreads=(), dma=False,
            tag="", risky=False):
        o = Op()
        o.stream = stream
        if dma:
            qi = self.scount.get("qn_" + stream, 0)
            self.scount["qn_" + stream] = qi + 1
            o.dom = f"q{stream}L{qi % 16}"
        else:
            o.dom = stream
        o.emit = emit
        o.dma = dma
        o.tag = tag
        o.need_inc = False
        o.risky = risky
        o.sidx = self.scount.get(stream, 0)
        self.scount[stream] = o.sidx + 1
        deps = {}

        def dep(p, wait_needed):
            if p is None or p is o:
                return
            k = id(p)
            if k in deps:
                deps[k] = (p, deps[k][1] or wait_needed)
            else:
                deps[k] = (p, wait_needed)

        def need_wait(p, kind, scalar=False):
            if p.dom != o.dom:
                return True
            if p.dma and o.dma:
                return kind == "raw"
            if stream == "pe":
                return False
            if kind != "raw":
                return False
            return (o.sidx - p.sidx) <= 6

        for b in reads:
            e = self.bufs.get(b)
            if e and e[0] is not None:
                dep(e[0], need_wait(e[0], "raw", False))
        for b in sreads:
            e = self.bufs.get(b)
            if e and e[0] is not None:
                dep(e[0], need_wait(e[0], "raw", True))
        for b in writes:
            e = self.bufs.get(b)
            if e:
                if e[0] is not None:
                    dep(e[0], need_wait(e[0], "waw"))
                for r in e[1]:
                    dep(r, need_wait(r, "war"))
        if self.pending_barrier is not None and stream not in self.barrier_absorbed:
            self.barrier_absorbed.add(stream)
            for p in self.pending_barrier.values():
                dep(p, True)
        o.deps = list(deps.values())
        for b in list(reads) + list(sreads):
            self.bufs.setdefault(b, [None, []])[1].append(o)
        for b in writes:
            self.bufs[b] = [o, []]
        o.gidx = len(self.ops)
        self.ops.append(o)
        return o

    def barrier(self):
        last = {}
        for o in self.ops:
            last[o.dom] = o
        self.pending_barrier = last
        self.barrier_absorbed = set()

    def finalize(self):
        for o in self.ops:
            if o.dma:
                o.need_inc = True
            for (p, w) in o.deps:
                if w:
                    p.need_inc = True
        ctr = {}
        for o in self.ops:
            if o.need_inc:
                ctr[o.dom] = ctr.get(o.dom, 0) + 1
                o.inc_no = ctr[o.dom]
            else:
                o.inc_no = None
        self.totals = ctr

    def emit_stream(self, stream, eng, sems):
        known = {}
        for o in self.ops:
            if o.stream != stream:
                continue
            for (p, w) in o.deps:
                if not w:
                    continue
                v = p.inc_no * (16 if p.dma else 1)
                if known.get(p.dom, 0) < v:
                    eng.wait_ge(sems[p.dom], v)
                    known[p.dom] = v
            inst = o.emit(eng)
            if o.need_inc:
                inst.then_inc(sems[o.dom], 16 if o.dma else 1)


def build_program(dbg=()):
    nc = bass.Bass("TRN2", target_bir_lowering=False)
    S = Sched()

    def din(name, shape):
        return nc.dram_tensor(name, shape, F32, kind="ExternalInput").ap()

    x_d = din("x", [L, D])
    ipw_d = din("ipw", [D, C])
    opw_d = din("opw", [DI, D])
    w1_d = din("w1", [NE, D, DE])
    w2_d = din("w2", [NE, DE, D])
    theta_d = din("theta_t", [128, NVC])
    ck_d = din("ck", [128, 33 * 4])
    cb_d = din("cb", [128, 33])
    score_d = din("score", [32, 1])
    slopes_d = din("slopes", [32, 1])
    deriv_d = din("deriv", [1, 3])
    wre_d = din("Wre2", [128, 64])
    wim_d = din("Wim2", [128, 64])
    nsre_d = din("nsre2", [128, 1])
    nsim_d = din("nsim2", [128, 1])
    pwt_d = din("pwT", [128, 48])
    anch_d = din("anchT", [6, 6])
    rt_d = din("rt", [1, 1])
    sel_d = din("SEL", [32, 2048])
    sel2_d = din("SEL2", [128, 512])
    ident_d = din("ident", [128, 128])
    dist_d = din("dist", [32, L])
    hostc_d = din("hostc_in", [1, 4])

    y_d = nc.dram_tensor("y", [L, D], F32, kind="ExternalOutput").ap()
    if "u1" in dbg:
        u1_d = nc.dram_tensor("u1", [D, L], F32, kind="ExternalOutput").ap()
    else:
        u1_d = nc.dram_tensor("u1", [D, L], F32).ap()
    if "u2" in dbg:
        u2_d = nc.dram_tensor("u2", [D, L], F32, kind="ExternalOutput").ap()
    else:
        u2_d = nc.dram_tensor("u2", [D, L], F32).ap()
    ipbf_d = nc.dram_tensor("ipbf", [8, 128, 4096], BF16).ap()
    opbf_d = nc.dram_tensor("opbf", [8, 128, 2048], BF16).ap()
    if "a1" in dbg:
        dbg_ut = nc.dram_tensor("dbg_ut", [128, 8, TA], F32,
                                kind="ExternalOutput").ap()
        dbg_xln = nc.dram_tensor("dbg_xln", [128, 8 * TA], F32,
                                 kind="ExternalOutput").ap()
        dbg_ewf = nc.dram_tensor("dbg_ewf", [1, NE * TA], F32,
                                 kind="ExternalOutput").ap()
        dbg_hsb = nc.dram_tensor("dbg_hsb", [128, NE * 4 * TA], F32,
                                 kind="ExternalOutput").ap()
        dbg_rpk = nc.dram_tensor("dbg_rpk", [128, 64], F32,
                                 kind="ExternalOutput").ap()
        dbg_cst = nc.dram_tensor("dbg_cst", [128, 16], F32,
                                 kind="ExternalOutput").ap()
        dbg_sm = nc.dram_tensor("dbg_sm", [1, 16], F32,
                                kind="ExternalOutput").ap()
        dbg_b = nc.dram_tensor("dbg_b", [128, 6 * TB], F32,
                               kind="ExternalOutput").ap()
        dbg_bs = nc.dram_tensor("dbg_bs", [32, 4 * TB], F32,
                                kind="ExternalOutput").ap()
        dbg_b2 = nc.dram_tensor("dbg_b2", [128, 2 * TB], F32,
                                kind="ExternalOutput").ap()
    else:
        dbg_ut = dbg_xln = dbg_ewf = dbg_hsb = dbg_rpk = dbg_b = dbg_bs = None
        dbg_b2 = None

    st_p = ExitStack()

    def sb(name, shape, dt, st=st_p):
        return st.enter_context(nc.sbuf_tensor(name, shape, dt))

    def pt(name, shape, dt, st=st_p):
        return st.enter_context(nc.psum_tensor(name, shape, dt))

    xlnbf = sb("xlnbf", [128, 4096], BF16)
    sqb = [sb(f"sqb{i}", [128, 512], BF16) for i in range(2)]
    d1b = sb("d1b", [128, 512], F32)
    rowp = sb("rowp", [1, 2048], F32)
    rpk = sb("rpk", [128, 64], F32)
    cst = sb("cst", [128, 16], F32)
    ebuf = [sb(f"ebuf{i}", [128, 512], F32) for i in range(2)]
    identt = sb("identt", [128, 128], F32)
    onesr = sb("onesr", [1, 128], F32)
    onesrb = sb("onesrb", [1, 128], BF16)
    onescb = sb("onescb", [128, 1], BF16)
    ubb = [sb(f"ubb{i}", [128, 512], BF16) for i in range(2)]
    onesc = sb("onesc", [128, 1], F32)
    theta_t = sb("theta_sb", [128, NVC], F32)
    theta2 = sb("theta2", [128, NVC], F32)
    ckt = sb("ckt", [128, 33 * 4], F32)
    cbt = sb("cbt", [128, 33], F32)
    scoret = sb("scoret", [32, 1], F32)
    slopest = sb("slopest", [32, 1], F32)
    negslope = sb("negslope", [32, 1], F32)
    derivt = sb("derivt", [1, 3], F32)
    wret = sb("wret", [128, 64], F32)
    wimt = sb("wimt", [128, 64], F32)
    nsret = sb("nsret", [128, 1], F32)
    nsimt = sb("nsimt", [128, 1], F32)
    wrens = sb("wrens", [128, 64], BF16)
    wimns = sb("wimns", [128, 64], BF16)
    pwtt = sb("pwtt", [128, 48], F32)
    ancht = sb("ancht", [6, 6], F32)
    rtt = sb("rtt", [1, 1], F32)
    selt = sb("selt", [32, 2048], BF16)
    sel2t = sb("sel2t", [128, 512], BF16)
    ones_scan = sb("ones_scan", [128, 512], F32)
    wrebd = sb("wrebd", [128, 128], BF16)
    wimbd = sb("wimbd", [128, 128], BF16)
    carry_re = sb("carry_re", [128, NVC], F32)
    carry_im = sb("carry_im", [128, NVC], F32)
    carry_den = sb("carry_den", [32, 1], F32)
    tails = sb("tails", [128, 33 * 3], F32)
    dsm = sb("dsm", [1, 8], F32)
    wsm = sb("wsm", [1, 3], F32)
    rts = sb("rts", [1, 4], F32)
    spl = sb("spl", [32, 4], F32)
    hostc = sb("hostc", [1, 4], F32)

    PB = [pt(f"PB{i}", [128, 512], F32) for i in range(8)]
    PBK = [f"PB{i}" for i in range(8)]

    CONSTS = {"c256": 256.5, "n2pi": -2 * PI, "p2pi": 2 * PI,
              "npih": -PI / 2, "nhalf": -0.5, "half": 0.5, "one": 1.0}
    CCOL = {k: i for i, k in enumerate(CONSTS)}
    NDCOL = {"w0b": 8, "w1b": 9, "nw2b": 10, "hirb": 11, "iqt": 12}

    def cc_ap(name, pc=128):
        i = CCOL[name] if name in CCOL else NDCOL[name]
        return cst[0:pc, i:i + 1]

    sems = {}

    def mm(bank, out_ap, lhsT, rhs, start, stop, reads, tag="mm",
           tile_position=None):
        def e(eng, out_ap=out_ap, lhsT=lhsT, rhs=rhs, start=start, stop=stop,
              tp=tile_position):
            return eng.matmul(out_ap, lhsT, rhs, start=start, stop=stop,
                              tile_position=tp)
        return S.add("pe", e, reads=tuple(reads), writes=(bank,), tag=tag)

    def transpose(bank, out_ap, in_ap, reads, ident_ap=None, tag="tr"):
        ia = ident_ap if ident_ap is not None else identt[:, :]
        def e(eng, out_ap=out_ap, in_ap=in_ap, ia=ia):
            return eng.transpose(out_ap, in_ap, ia)
        return S.add("pe", e, reads=tuple(reads) + ("identt",), writes=(bank,),
                     tag=tag)

    def act(out_ap, in_ap, func, reads, writes, scale=None, bias=None,
            sreads=(), tag="act"):
        def e(eng, out_ap=out_ap, in_ap=in_ap, func=func, scale=scale,
              bias=bias):
            kw = {}
            if scale is not None:
                kw["scale"] = scale
            if bias is not None:
                kw["bias"] = bias
            return eng.activation(out_ap, in_ap, func, **kw)
        return S.add("act", e, reads=tuple(reads), writes=tuple(writes),
                     sreads=tuple(sreads), tag=tag)

    def dve(emit, reads, writes, sreads=(), tag="dve"):
        return S.add("dve", emit, reads=tuple(reads), writes=tuple(writes),
                     sreads=tuple(sreads), tag=tag)

    def pool(emit, reads, writes, sreads=(), tag="pool"):
        return S.add("pool", emit, reads=tuple(reads), writes=tuple(writes),
                     sreads=tuple(sreads), tag=tag)

    def dma(queue, out_ap, in_ap, reads, writes, tag="dma"):
        def e(eng, out_ap=out_ap, in_ap=in_ap):
            return eng.dma_start(out=out_ap, in_=in_ap)
        return S.add(queue, e, reads=tuple(reads), writes=tuple(writes),
                     dma=True, tag=tag)

    def ts(out_ap, in_ap, s1, s2, op0, op1, reads, writes, sreads=(),
           stream="dve", tag="ts"):
        def e(eng, out_ap=out_ap, in_ap=in_ap, s1=s1, s2=s2, op0=op0, op1=op1):
            if op1 is None:
                return eng.tensor_scalar(out_ap, in_ap, s1, None, op0=op0)
            return eng.tensor_scalar(out_ap, in_ap, s1, s2, op0=op0, op1=op1)
        return S.add(stream, e, reads=tuple(reads), writes=tuple(writes),
                     sreads=tuple(sreads), tag=tag)

    def tt(out_ap, a_ap, b_ap, op, reads, writes, stream="dve", tag="tt"):
        def e(eng, out_ap=out_ap, a_ap=a_ap, b_ap=b_ap, op=op):
            return eng.tensor_tensor(out_ap, a_ap, b_ap, op=op)
        return S.add(stream, e, reads=tuple(reads), writes=tuple(writes),
                     tag=tag)

    def stt(out_ap, in0_ap, s_ap, in1_ap, op0, op1, reads, writes, sreads=(),
            stream="dve", tag="stt"):
        def e(eng, out_ap=out_ap, in0_ap=in0_ap, s_ap=s_ap, in1_ap=in1_ap,
              op0=op0, op1=op1):
            return eng.scalar_tensor_tensor(out_ap, in0_ap, s_ap, in1_ap,
                                            op0=op0, op1=op1)
        return S.add(stream, e, reads=tuple(reads), writes=tuple(writes),
                     sreads=tuple(sreads), tag=tag)

    def copy(out_ap, in_ap, reads, writes, stream="dve", tag="copy"):
        def e(eng, out_ap=out_ap, in_ap=in_ap):
            return eng.tensor_copy(out_ap, in_ap)
        return S.add(stream, e, reads=tuple(reads), writes=tuple(writes),
                     tag=tag)

    def memset(ap, v, writes, stream="dve", tag="memset"):
        def e(eng, ap=ap, v=v):
            return eng.memset(ap, v)
        return S.add(stream, e, writes=tuple(writes), tag=tag)

    def scan(out_ap, d0_ap, d1_ap, init_ap, reads, writes, sreads, tag="scan"):
        def e(eng, out_ap=out_ap, d0_ap=d0_ap, d1_ap=d1_ap, init_ap=init_ap):
            return eng.tensor_tensor_scan(out_ap, d0_ap, d1_ap, init_ap,
                                          op0=ALU.mult, op1=ALU.add)
        return S.add("dve", e, reads=tuple(reads), writes=tuple(writes),
                     sreads=tuple(sreads), tag=tag, risky=True)

    def reduce(out_ap, in_ap, op, reads, writes, tag="red"):
        def e(eng, out_ap=out_ap, in_ap=in_ap, op=op):
            return eng.tensor_reduce(out_ap, in_ap, axis=AX.X, op=op)
        return S.add("dve", e, reads=tuple(reads), writes=tuple(writes),
                     tag=tag, risky=True)

    def recip(out_ap, in_ap, reads, writes, tag="recip"):
        def e(eng, out_ap=out_ap, in_ap=in_ap):
            return eng.reciprocal(out_ap, in_ap)
        return S.add("dve", e, reads=tuple(reads), writes=tuple(writes),
                     tag=tag)

    for apout, src, wk in [
        (theta_t[:, :], theta_d, "theta"), (ckt[:, :], ck_d, "ck"),
        (cbt[:, :], cb_d, "cb"), (scoret[:, :], score_d, "score"),
        (slopest[:, :], slopes_d, "slopes"), (derivt[:, :], deriv_d, "deriv"),
        (wret[:, :], wre_d, "wre"), (wimt[:, :], wim_d, "wim"),
        (nsret[:, :], nsre_d, "nsre"), (nsimt[:, :], nsim_d, "nsim"),
        (pwtt[:, :], pwt_d, "pwt"), (ancht[:, :], anch_d, "anch"),
        (rtt[:, :], rt_d, "rtsb"),
        (identt[:, :], ident_d, "identt"),
        (hostc[:, :], hostc_d, "hostc"),
    ]:
        dma("sync", apout, src, reads=(), writes=(wk,), tag="ld_" + wk)

    for name, v in CONSTS.items():
        memset(cst[:, CCOL[name]:CCOL[name] + 1], v, writes=(f"cst_{name}",))
    dma("pool", sel2t[:, :], sel2_d, reads=(), writes=("sel2",),
        tag="ld_sel2")
    dma("pool", selt[:, :], sel_d, reads=(), writes=("sel",), tag="ld_sel")
    memset(onesr[:, :], 1.0, writes=("onesr",))
    memset(onesrb[:, :], 1.0, writes=("onesrb",))
    memset(onesc[:, :], 1.0, writes=("onesc",))
    memset(onescb[:, :], 1.0, writes=("onescb",))
    memset(ones_scan[:, :], 1.0, writes=("ones_scan",))
    memset(carry_re[:, :], 0.0, writes=tuple(f"cre{c}" for c in range(NVC)))
    memset(carry_im[:, :], 0.0, writes=tuple(f"cim{c}" for c in range(NVC)))
    memset(carry_den[:, :], 0.0, writes=("cden",))
    memset(tails[:, :], 0.0, writes=tuple(f"tail{c}" for c in range(33)))

    ts(theta2[:, :], theta_t[:, :], 1.0 / (2 * PI), None, ALU.mult, None,
       reads=("theta",), writes=("theta2",))
    NEG1_COL = 13
    memset(cst[:, NEG1_COL:NEG1_COL + 1], -1.0, writes=("cst_neg1",))
    act(spl[:, 0:1], slopest[:, :], AF.Abs, reads=("slopes",), writes=("spl0",))
    act(spl[:, 1:2], spl[:, 0:1], AF.Exp, reads=("spl0",), writes=("spl1",),
        scale=cst[0:32, NEG1_COL:NEG1_COL + 1], sreads=("cst_neg1",))
    act(spl[:, 2:3], spl[:, 1:2], AF.Ln, reads=("spl1",), writes=("spl2",),
        bias=cc_ap("one", 32), sreads=("cst_one",))
    ts(spl[:, 3:4], slopest[:, :], 0.0, None, ALU.max, None,
       reads=("slopes",), writes=("spl3",))
    def _ns(eng):
        return eng.tensor_tensor(negslope[:, :], spl[:, 2:3], spl[:, 3:4],
                                 op=ALU.add)
    S.add("dve", _ns, reads=("spl2", "spl3"), writes=("negslope_pos",))
    ts(negslope[:, :], negslope[:, :], -1.0, None, ALU.mult, None,
       reads=("negslope_pos",), writes=("negslope",))

    reduce(dsm[:, 0:1], derivt[:, :], ALU.max, reads=("deriv",),
           writes=("dsm_max",))
    ts(wsm[:, :], derivt[:, :], dsm[:, 0:1], None, ALU.subtract, None,
       reads=("deriv",), writes=("wsm_sub",), sreads=("dsm_max",))
    act(wsm[:, :], wsm[:, :], AF.Exp, reads=("wsm_sub",), writes=("wsm_exp",))
    reduce(dsm[:, 1:2], wsm[:, :], ALU.add, reads=("wsm_exp",),
           writes=("dsm_sum",))
    act(dsm[:, 3:4], dsm[:, 1:2], AF.Ln, reads=("dsm_sum",),
        writes=("dsm_ln",))
    act(dsm[:, 2:3], dsm[:, 3:4], AF.Exp, reads=("dsm_ln",),
        writes=("dsm_rec",), scale=cst[0:1, NEG1_COL:NEG1_COL + 1],
        sreads=("cst_neg1",))
    ts(wsm[:, :], wsm[:, :], dsm[:, 2:3], None, ALU.mult, None,
       reads=("wsm_exp",), writes=("wsm",), sreads=("dsm_rec",))
    for j, (name, neg) in enumerate([("w0b", False), ("w1b", False),
                                     ("nw2b", True)]):
        bank_ap = PB[7][:, j:j + 1]
        mm("PB7", bank_ap, onesr[:, :], wsm[:, j:j + 1], True, True,
           reads=("onesr", "wsm"), tag="bc_w")
        col = NDCOL[name]
        ts(cst[:, col:col + 1], bank_ap, -1.0 if neg else 1.0, None, ALU.mult,
           None, reads=("PB7",), writes=(f"cst_{name}",))
    ts(rts[:, 0:1], rtt[:, :], 0.1, None, ALU.max, None, reads=("rtsb",),
       writes=("rtc",))
    act(rts[:, 3:4], rts[:, 0:1], AF.Ln, reads=("rtc",), writes=("rtln",))
    act(rts[:, 1:2], rts[:, 3:4], AF.Exp, reads=("rtln",), writes=("rtrec",),
        scale=cst[0:1, NEG1_COL:NEG1_COL + 1], sreads=("cst_neg1",))
    ts(rts[:, 2:3], rts[:, 1:2], 0.5, None, ALU.mult, None, reads=("rtrec",),
       writes=("hir1",))
    mm("PB7", PB[7][:, 4:5], onesr[:, :], rts[:, 2:3], True, True,
       reads=("onesr", "hir1"), tag="bc_hir")
    ts(cst[:, NDCOL["hirb"]:NDCOL["hirb"] + 1], PB[7][:, 4:5], 1.0, None,
       ALU.mult, None, reads=("PB7",), writes=("cst_hirb",))
    memset(cst[:, NDCOL["iqt"]:NDCOL["iqt"] + 1], 1.0 / 0.3,
           writes=("cst_iqt",))
    ts(wrens[:, :], wret[:, :], nsret[:, 0:1], None, ALU.mult, None,
       reads=("wre",), writes=("wrens",), sreads=("nsre",))
    ts(wimns[:, :], wimt[:, :], nsimt[:, 0:1], None, ALU.mult, None,
       reads=("wim",), writes=("wimns",), sreads=("nsim",))
    memset(wrebd[:, :], 0.0, writes=("wrebd0",))
    memset(wimbd[:, :], 0.0, writes=("wimbd0",))
    copy(wrebd[0:64, 0:64], wrens[0:64, :], reads=("wrens", "wrebd0"),
         writes=("wrebd1",))
    copy(wrebd[64:128, 64:128], wrens[64:128, :], reads=("wrens", "wrebd1"),
         writes=("wrebd",))
    copy(wimbd[0:64, 0:64], wimns[0:64, :], reads=("wimns", "wimbd0"),
         writes=("wimbd1",))
    copy(wimbd[64:128, 64:128], wimns[64:128, :], reads=("wimns", "wimbd1"),
         writes=("wimbd",))

    def emit_ln(ut, utkeys, T, bmb, brb, phase, need_nm, xt=None, xpre=None):
        if xt is None:
            xt = xlnbf
            xpre = "xln"
        nd = 8
        sx = PB[2][0:1, 0:T]
        sq = PB[3][0:1, 0:T]
        for dc in range(nd):
            sb_ = sqb[dc % 2]
            ub_ = ubb[dc % 2]
            copy(ub_[:, 0:T], ut[:, dc, :], reads=(utkeys[dc],),
                 writes=(f"ubb{dc % 2}",), stream="pool", tag="ln_ub")
            act(sb_[:, 0:T], ut[:, dc, :], AF.Square, reads=(utkeys[dc],),
                writes=(f"sqb{dc % 2}",), tag="ln_sq")
            mm("PB2", sx, onescb[:, :], ub_[:, 0:T], dc == 0, dc == nd - 1,
               reads=("onescb", f"ubb{dc % 2}"), tag="ln_sx")
            mm("PB3", sq, onescb[:, :], sb_[:, 0:T], dc == 0, dc == nd - 1,
               reads=("onescb", f"sqb{dc % 2}"), tag="ln_sq_mm")
        m = rowp[0:1, 0:T]
        nm = rowp[0:1, 512:512 + T]
        va = rowp[0:1, 1024:1024 + T]
        rs_ = rowp[0:1, 1536:1536 + T]
        ts(m, sx, 1.0 / D, None, ALU.mult, None, reads=("PB2",),
           writes=("lnm",), tag="ln_m")
        ts(va, sq, 1.0 / D, None, ALU.mult, None, reads=("PB3",),
           writes=("lnmsq",), tag="ln_msq")
        if need_nm:
            ts(nm, m, -1.0, None, ALU.mult, None, reads=("lnm",),
               writes=("lnnm",), tag="ln_nm")
        def _mm2(eng):
            return eng.tensor_tensor(d1b[0:1, 0:T], m, m, op=ALU.mult)
        S.add("dve", _mm2, reads=("lnm",), writes=("d1row",), tag="ln_m2")
        def _var2(eng):
            return eng.tensor_tensor(va, va, d1b[0:1, 0:T], op=ALU.subtract)
        S.add("dve", _var2, reads=("lnmsq", "d1row"), writes=("lnvar",),
              tag="ln_var")
        act(va, va, AF.Ln, reads=("lnvar",), writes=("lnsd",),
            bias=hostc[0:1, 0:1], sreads=("hostc",), tag="ln_lnv")
        act(rs_, va, AF.Exp, reads=("lnsd",), writes=("lnrstd",),
            scale=cst[0:1, CCOL["nhalf"]:CCOL["nhalf"] + 1],
            sreads=("cst_nhalf",), tag="ln_rstd")
        mm("PB4", bmb[:, 0:T], onesr[:, :], m, True, True,
           reads=("onesr", "lnm"), tag="ln_mbc")
        mm("PB5", brb[:, 0:T], onesr[:, :], rs_, True, True,
           reads=("onesr", "lnrstd"), tag="ln_rbc")
        for dc in range(nd):
            def _sub(eng, dc=dc, T=T):
                return eng.tensor_tensor(d1b[:, 0:T], ut[:, dc, :],
                                         bmb[:, 0:T], op=ALU.subtract)
            S.add("dve", _sub, reads=(utkeys[dc], "PB4"), writes=("d1b",),
                  tag="ln_sub")
            def _mul(eng, dc=dc, T=T, xt=xt):
                return eng.tensor_tensor(xt[:, dc * T:(dc + 1) * T],
                                         d1b[:, 0:T], brb[:, 0:T],
                                         op=ALU.mult)
            S.add("dve", _mul, reads=("d1b", "PB5"),
                  writes=(f"{xpre}{dc}",), tag="ln_mul")

    def emit_ffn(phase, w1k, w2k, w1sb, w2sb, xin, ut, hsb, hgb, ewbs, ewf,
                 qts, ytm, in_dram, out_dram, out_tm, tile_hook=None):
        T = TA
        for it in range(NTA):
            t0 = it * T
            par = it % len(ut)
            utt = ut[par]
            utkeys = [f"ut_{phase}{par}_{dc}" for dc in range(8)]
            if phase == "A":
                for h in range(2):
                    hsrc = x_d[t0 + h * 256:t0 + (h + 1) * 256, :].rearrange(
                        "(s p) d -> p s d", p=128)
                    dma("sync", xin[:, :, :], hsrc, reads=(),
                        writes=("xin0",), tag="ldx")
                    for dc in range(8):
                        bank = PBK[dc]
                        pb = PB[dc]
                        for sl in range(2):
                            s = h * 2 + sl
                            transpose(bank, pb[:, s * 128:(s + 1) * 128],
                                      xin[:, sl, dc * 128:(dc + 1) * 128],
                                      reads=("xin0",), tag="trin")
                        if h == 1:
                            act(utt[:, dc, :], pb[:, 0:T], AF.Identity,
                                reads=(bank,), writes=(utkeys[dc],),
                                tag="ut_ev")
            else:
                src = in_dram.rearrange("(c p) t -> p c t", p=128)[
                    :, :, t0:t0 + T]
                dma("sync", utt[:, :, :], src,
                    reads=tuple(f"u2d_{it}_{dc}" for dc in range(8)),
                    writes=tuple(utkeys), tag="ldu")
            emit_ln(utt, utkeys, T, PB[4], PB[5], phase + str(it), True)
            nsub = T // 128
            rs_row = rowp[0:1, 1536:1536 + T]
            for s in range(nsub):
                transpose("PB6", PB[6][:, s:s + 1],
                          rs_row[0:1, s * 128:(s + 1) * 128],
                          reads=("lnrstd",),
                          ident_ap=identt[0:1, 0:1], tag="tr_rstd")
            copy(rpk[:, 50:50 + nsub], PB[6][:, 0:nsub], reads=("PB6",),
                 writes=("rstd_tm",), tag="rstd_tm")
            if phase == "A" and it == 0:
                for dc in range(8):
                    mm("PB6", PB[6][0:1, 8:14], onesc[:, :],
                       pwtt[:, dc * 6:(dc + 1) * 6], dc == 0, dc == 7,
                       reads=("pwt", "onesc"), tag="srow_mm")
                copy(rpk[0:1, 56:62], PB[6][0:1, 8:14], reads=("PB6",),
                     writes=("srow",), tag="srow")
            nm_row = rowp[0:1, 512:512 + T]
            for s in range(nsub):
                zr = PB[2][:, 0:6] if s % 2 == 0 else PB[3][:, 0:6]
                zrk = "PB2" if s % 2 == 0 else "PB3"
                for dc in range(8):
                    mm(zrk, zr, utt[:, dc, s * 128:(s + 1) * 128],
                       pwtt[:, dc * 6:(dc + 1) * 6], dc == 0, False,
                       reads=(utkeys[dc], "pwt"), tag="zu")
                mm(zrk, zr, nm_row[0:1, s * 128:(s + 1) * 128],
                   rpk[0:1, 56:62], False, True,
                   reads=("lnnm", "srow"), tag="zfix")
                zt = rpk[:, 38:44]
                ts(zt, zr, rpk[:, 50 + s:51 + s], None, ALU.mult, None,
                   reads=(zrk,), writes=("zt",), sreads=("rstd_tm",),
                   tag="zts")
                qv = rpk[:, 44:50]
                act(qv, zt, AF.Tanh, reads=("zt",), writes=("qv",),
                    scale=cc_ap("iqt"), sreads=("cst_iqt",), tag="qtanh")
                transpose("PB7", PB[7][0:6, 0:128], qv, reads=("qv",),
                          tag="qtr")
                copy(qts[0:6, 0:128], PB[7][0:6, 0:128], reads=("PB7",),
                     writes=("qTs",), tag="qTs")
                mm("PB7", PB[7][:, 128:134], qts[0:6, 0:128],
                   ancht[:, :], True, True, reads=("qTs", "anch"), tag="dot")
                lg = rpk[:, 8:14]
                ts(lg, PB[7][:, 128:134], -6.0, cc_ap("hirb"), ALU.add,
                   ALU.mult, reads=("PB7",), writes=("lg",),
                   sreads=("cst_hirb",), tag="lgts")
                m1 = rpk[:, 0:1]
                reduce(m1, lg, ALU.max, reads=("lg",), writes=("m1",))
                mask1 = rpk[:, 14:20]
                ts(mask1, lg, m1, None, ALU.is_ge, None, reads=("lg",),
                   writes=("mask1",), sreads=("m1",), tag="mask1")
                l2 = rpk[:, 20:26]
                stt(l2, mask1, -1e30, lg, ALU.mult, ALU.add,
                    reads=("mask1", "lg"), writes=("l2",), tag="l2")
                m2 = rpk[:, 1:2]
                reduce(m2, l2, ALU.max, reads=("l2",), writes=("m2",))
                mask2 = rpk[:, 26:32]
                ts(mask2, l2, m2, None, ALU.is_ge, None, reads=("l2",),
                   writes=("mask2",), sreads=("m2",), tag="mask2")
                d21 = rpk[:, 2:3]
                ts(d21, m2, m1, None, ALU.subtract, None, reads=("m2",),
                   writes=("d21",), sreads=("m1",), tag="d21")
                w1s = rpk[:, 5:6]
                act(w1s, d21, AF.Sigmoid, reads=("d21",), writes=("w1s",),
                    scale=cst[:, NEG1_COL:NEG1_COL + 1],
                    sreads=("cst_neg1",), tag="w1s")
                w2s = rpk[:, 6:7]
                act(w2s, d21, AF.Sigmoid, reads=("d21",), writes=("w2s",),
                    tag="w2s")
                ew1 = rpk[:, 32:38]
                ts(ew1, mask1, w1s, None, ALU.mult, None, reads=("mask1",),
                   writes=("ew1",), sreads=("w1s",), tag="ew1")
                stt(ew1, mask2, w2s, ew1, ALU.mult, ALU.add,
                    reads=("mask2", "ew1"), writes=("ew2",), sreads=("w2s",),
                    tag="ew2")
                if phase == "A" and it == 1 and s == 0 and dbg_rpk is not None:
                    dma("sync", dbg_rpk[:, :], rpk[:, :],
                        reads=("ew2", "lg", "m1", "m2", "mask1", "mask2",
                               "zt", "qv", "d21", "e21", "w1s", "w2s",
                               "rstd_tm"),
                        writes=("dbgr",), tag="dbg")
                    dma("sync", dbg_cst[:, :], cst[:, :],
                        reads=tuple("cst_" + k for k in
                                    ["w0b", "w1b", "nw2b", "hirb", "iqt"]),
                        writes=("dbgc",), tag="dbg")
                    dma("sync", dbg_sm[0:1, 0:3], wsm[:, :], reads=("wsm",),
                        writes=("dbgs1",), tag="dbg")
                    dma("sync", dbg_sm[0:1, 4:12], dsm[:, :],
                        reads=("dsm_max", "dsm_sum", "dsm_rec"),
                        writes=("dbgs2",), tag="dbg")
                    dma("sync", dbg_sm[0:1, 12:16], rts[:, :],
                        reads=("rtc", "rtrec", "hir1"),
                        writes=("dbgs3",), tag="dbg")
                for e_ in range(NE):
                    transpose("PB7", PB[7][0:1, 384:512], ew1[:, e_:e_ + 1],
                              reads=("ew2",), tag="ewtr")
                    copy(ewf[0:1, e_ * T + s * 128:e_ * T + (s + 1) * 128],
                         PB[7][0:1, 384:512], reads=("PB7",),
                         writes=(f"ewf{e_}",), tag="ewf")
            for e_ in range(NE):
                ewb = PB[6]
                mm("PB6", ewb[:, 0:T], onesrb[:, :],
                   ewf[0:1, e_ * T:(e_ + 1) * T], True, True,
                   reads=("onesrb", f"ewf{e_}"), tag="ewb")
                ewsb = ewbs[e_ % 2]
                act(ewsb[:, 0:T], ewb[:, 0:T], AF.Identity, reads=("PB6",),
                    writes=(f"ewsb{e_ % 2}",), tag="ewb_ev")
                for hc in range(4):
                    hb = PBK[hc % 2]
                    hpb = PB[hc % 2]
                    for dc in range(8):
                        lhs = w1sb[:, (e_ * 8 + dc) * 512 + hc * 128:
                                   (e_ * 8 + dc) * 512 + hc * 128 + 128]
                        mm(hb, hpb[:, 0:T], lhs,
                           xlnbf[:, dc * T:(dc + 1) * T], dc == 0, dc == 7,
                           reads=(w1k[e_], f"xln{dc}"), tag="w1mm")
                    hgv = hgb[hc % 2]
                    act(hgv[:, 0:T], hpb[:, 0:T], AF.Gelu, reads=(hb,),
                        writes=(f"hg{hc % 2}",), tag="gelu")
                    def _hs(eng, e_=e_, hc=hc, T=T, hgv=hgv, ewsb=ewsb):
                        return eng.tensor_tensor(
                            hsb[:, (e_ * 4 + hc) * T:(e_ * 4 + hc + 1) * T],
                            hgv[:, 0:T], ewsb[:, 0:T], op=ALU.mult)
                    S.add("dve", _hs, reads=(f"hg{hc % 2}", f"ewsb{e_ % 2}"),
                          writes=(f"hs{e_}_{hc}",), tag="hscale")
            if phase == "A" and it == 1 and dbg_ut is not None:
                dma("sync", dbg_ut[:, :, :], utt[:, :, :],
                    reads=tuple(utkeys), writes=("dbg1",), tag="dbg")
                dma("pool", dbg_xln[:, :], xlnbf[:, 0:8 * T],
                    reads=tuple(f"xln{d_}" for d_ in range(8)),
                    writes=("dbg2",), tag="dbg")
                dma("pool", dbg_ewf[:, :], ewf[0:1, :],
                    reads=tuple(f"ewf{e_}" for e_ in range(NE)),
                    writes=("dbg3",), tag="dbg")
                dma("pool", dbg_hsb[:, :], hsb[:, :],
                    reads=tuple(f"hs{e_}_{h_}" for e_ in range(NE)
                                for h_ in range(4)),
                    writes=("dbg4",), tag="dbg")
            for dc in range(8):
                ob = PBK[4 + (dc % 2)]
                opb = PB[4 + (dc % 2)]
                first = True
                for e_ in range(NE):
                    for hc in range(4):
                        lhs = w2sb[:, (e_ * 4 + hc) * 1024 + dc * 128:
                                   (e_ * 4 + hc) * 1024 + dc * 128 + 128]
                        mm(ob, opb[:, 0:T], lhs,
                           hsb[:, (e_ * 4 + hc) * T:(e_ * 4 + hc + 1) * T],
                           first, (e_ == NE - 1) and (hc == 3),
                           reads=(w2k[e_], f"hs{e_}_{hc}"), tag="w2mm")
                        first = False
                eb = ebuf[dc % 2]
                stt(eb[:, 0:T], opb[:, 0:T], 0.5, utt[:, dc, :], ALU.mult,
                    ALU.add, reads=(ob, utkeys[dc]), writes=(f"ebuf{dc % 2}",),
                    sreads=(), tag="res_ev")
                if not out_tm:
                    dst = out_dram[dc * 128:(dc + 1) * 128, t0:t0 + T]
                    dma("act", dst, eb[:, 0:T], reads=(f"ebuf{dc % 2}",),
                        writes=(f"u1d_{it}_{dc}",), tag="st_u1")
                    if dc == 7 and tile_hook is not None:
                        tile_hook(it)
                else:
                    for s in range(T // 128):
                        bank = PBK[6 + (s % 2)]
                        pb = PB[6 + (s % 2)]
                        transpose(bank, pb[:, 0:128],
                                  eb[:, s * 128:(s + 1) * 128],
                                  reads=(f"ebuf{dc % 2}",), tag="trout")
                        yp = ytm[s % 2]
                        act(yp[:, 0:128], pb[:, 0:128],
                            AF.Identity, reads=(bank,),
                            writes=(f"yp{s % 2}",), tag="ytm_ev")
                        dst = y_d[t0 + s * 128:t0 + (s + 1) * 128,
                                  dc * 128:(dc + 1) * 128]
                        dma("sync", dst, yp[:, 0:128],
                            reads=(f"yp{s % 2}",),
                            writes=(f"yd_{it}_{dc}_{s}",), tag="st_y")

    st_a = ExitStack()
    w1sb = sb("w1sb", [128, NE * 8 * 512], BF16, st_a)
    w2sb = sb("w2sb", [128, NE * 4 * 1024], BF16, st_a)
    xin = sb("xin0t", [128, 2, D], F32, st_a)
    utA = [sb("utA0", [128, 8, TA], F32, st_a)]
    hsb = sb("hsb", [128, NE * 4 * TA], BF16, st_a)
    hgb = [sb(f"hgb{i}", [128, TA], BF16, st_a) for i in range(2)]
    ewbsA = [sb(f"ewbs{i}", [128, TA], BF16, st_a) for i in range(2)]
    ewfA = sb("ewfA", [1, NE * TA], BF16, st_a)
    qtsA = sb("qtsA", [6, 128], F32, st_a)
    ipstg = sb("ipstg", [128, 4096], BF16, st_a)

    w1keys = [f"w1_{e}" for e in range(NE)]
    w2keys = [f"w2_{e}" for e in range(NE)]
    for e in range(NE):
        src = w1_d[e].rearrange("(dc p) h -> p dc h", p=128)
        dst = w1sb[:, e * 4096:(e + 1) * 4096].rearrange(
            "p (dc h) -> p dc h", dc=8)
        dma("pool", dst, src, reads=(), writes=(w1keys[e],), tag="ldw1")
    for e in range(NE):
        src = w2_d[e].rearrange("(hc p) d -> p hc d", p=128)
        dst = w2sb[:, e * 4096:(e + 1) * 4096].rearrange(
            "p (hc d) -> p hc d", hc=4)
        dma("pool", dst, src, reads=(), writes=(w2keys[e],), tag="ldw2")
    def stage_hook(it):
        for g in (2 * it, 2 * it + 1):
            if g < 8:
                gsrc = ipw_d[:, g * 512:(g + 1) * 512].rearrange(
                    "(dc p) c -> p dc c", p=128)
                dma("pool", ipstg[:, :].rearrange("p (dc c) -> p dc c", dc=8),
                    gsrc, reads=(), writes=("ipstg",), tag="ipcast")
                dma("sync", ipbf_d[g], ipstg[:, :], reads=("ipstg",),
                    writes=(f"ipbf{g}",), tag="ipstore")
        q = it - 4
        if 0 <= q < 4:
            qsrc = opw_d.rearrange("(kc p) d -> p kc d", p=128)[
                :, 4 * q:4 * q + 4, :]
            dma("pool", ipstg[:, :].rearrange("p (kc d) -> p kc d", kc=4),
                qsrc, reads=(), writes=("ipstg",), tag="opcast")
            for dcq in range(8):
                ssrc = ipstg[:, :].rearrange("p (kc d) -> p kc d", kc=4)[
                    :, :, dcq * 128:(dcq + 1) * 128]
                sdst = opbf_d[dcq][:, q * 512:(q + 1) * 512].rearrange(
                    "p (kc w) -> p kc w", kc=4)
                dma("sync", sdst, ssrc, reads=("ipstg",),
                    writes=(f"opbf{dcq}_{q}",), tag="opstore")

    emit_ffn("A", w1keys, w2keys, w1sb, w2sb, xin, utA, hsb, hgb, ewbsA,
             ewfA, qtsA, None, None, u1_d, False, tile_hook=stage_hook)
    st_a.close()
    S.barrier()

    st_b = ExitStack()
    ipb = [sb(f"ipb{i}", [128, 8 * 512], BF16, st_b) for i in range(3)]
    ipsr = sb("ipsr", [128, 8 * 32], BF16, st_b)
    opd = [sb(f"opd{i}", [128, 2048], BF16, st_b) for i in range(2)]
    xlnb2 = sb("xlnb2", [128, 4096], BF16, st_b)
    utB = [sb(f"utB{i}", [128, 8, TB], F32, st_b) for i in range(2)]
    zb = [sb(f"zb{i}", [128, TB + 8], F32, st_b) for i in range(3)]
    cvA = sb("cvA", [128, TB], F32, st_b)
    cvB = sb("cvB", [128, TB], F32, st_b)
    xvb = [sb(f"xvb{i}", [128, TB], F32, st_b) for i in range(2)]
    ph3b = sb("ph3b", [128, TB], F32, st_b)
    mib = sb("mib", [128, TB], I32, st_b)
    r2b = sb("r2b", [128, TB], F32, st_b)
    sinb2 = [sb(f"sinb{i}", [128, TB], F32, st_b) for i in range(2)]
    cosb2 = [sb(f"cosb{i}", [128, TB], F32, st_b) for i in range(2)]
    sqvb = [sb("sqvb0", [128, TB], F32, st_b)] * 2
    t1b = sb("t1b", [128, TB], F32, st_b)
    polyb = sb("polyb", [128, TB], F32, st_b)
    ppb = [sb(f"ppb{i}", [128, TB], F32, st_b) for i in range(2)]
    areb = [sb(f"areb{i}", [128, TB], F32, st_b) for i in range(2)]
    aimb = [sb(f"aimb{i}", [128, TB], F32, st_b) for i in range(2)]
    nreb = [sb(f"nreb{i}", [128, TB], F32, st_b) for i in range(2)]
    nimb = [sb(f"nimb{i}", [128, TB], F32, st_b) for i in range(2)]
    nbfre = [sb(f"nbfre{i}", [128, TB], BF16, st_b) for i in range(2)]
    nbfim = [sb(f"nbfim{i}", [128, TB], BF16, st_b) for i in range(2)]
    sqbf = [sb(f"sqbf{i}", [128, TB], BF16, st_b) for i in range(4)]
    yvb = sb("yvb", [128, NVC * TB], BF16, st_b)
    gateb = [sb(f"gateb{i}", [128, TB], BF16, st_b) for i in range(2)]
    spk = sb("spk", [32, 4 * TB], F32, st_b)
    twtb = sb("twtb", [32, TB], F32, st_b)
    dsl = sb("dsl", [32, TB], F32, st_b)
    svalb = sb("svalb", [32, TB], F32, st_b)
    pwbb = sb("pwbb", [32, TB], BF16, st_b)
    svbb = sb("svbb", [32, TB], BF16, st_b)
    ysc = polyb

    def ld_op(dcq):
        dma("sync", opd[dcq % 2][:, :], opbf_d[dcq],
            reads=tuple(f"opbf{dcq}_{q}" for q in range(4)),
            writes=(f"opd{dcq % 2}",), tag="ldop")
    srsrc = ipw_d[:, 4096:4128].rearrange("(dc p) c -> p dc c", p=128)
    dma("pool", ipsr[:, :].rearrange("p (dc c) -> p dc c", dc=8), srsrc,
        reads=(), writes=("ipsr",), tag="ldipsr")

    def ip_lhs(gpar, dcc, cc_in_g):
        return ipb[gpar][:, dcc * 512 + cc_in_g * 128:
                         dcc * 512 + cc_in_g * 128 + 128]

    def ld_ipg(gidx):
        g = gidx % 8
        dma("sync", ipb[gidx % 3][:, :], ipbf_d[g], reads=(f"ipbf{g}",),
            writes=(f"ipg{gidx % 3}",), tag="ldipg")

    xlnbufs = [xlnbf, xlnb2]

    def ld_ut_b(it):
        par = it % 2
        utt = utB[par]
        utkeys = [f"utB{par}_{dc}" for dc in range(8)]
        src = u1_d.rearrange("(c p) t -> p c t", p=128)[
            :, :, it * TB:(it + 1) * TB]
        dma("sync", utt[:, :, :], src,
            reads=tuple(f"u1d_{it}_{dc}" for dc in range(8)),
            writes=tuple(utkeys), tag="ldu1")

    def ln_b(it):
        par = it % 2
        emit_ln(utB[par], [f"utB{par}_{dc}" for dc in range(8)], TB,
                PB[4], PB[5], "B" + str(it), False,
                xt=xlnbufs[par], xpre=f"xlnB{par}_")

    ld_ut_b(0)
    ln_b(0)
    for it in range(NTB):
        t0 = it * TB
        par = it % 2
        utt = utB[par]
        utkeys = [f"utB{par}_{dc}" for dc in range(8)]
        xlt = xlnbufs[par]
        xlk = [f"xlnB{par}_{dc}" for dc in range(8)]
        if it == 0:
            for g in range(3):
                ld_ipg(g)
        dma("sync", dsl[:, :], dist_d[:, t0:t0 + TB], reads=(),
            writes=("dsl",), tag="lddist")
        act(twtb[:, :], dsl[:, :], AF.Exp, reads=("dsl",), writes=("twt",),
            scale=negslope[:, 0:1], sreads=("negslope",), tag="tw")

        zrot = [0]
        ZB3 = (0, 1, 7)

        def chunk_z(cc, outbuf, outkey, tag):
            gpar = (it * 8 + cc // 4) % 3
            gkey = f"ipg{gpar}"
            zbi = ZB3[zrot[0] % 3]
            bank = PBK[zbi]
            pb = PB[zbi]
            wk = 128 if cc < 32 else 32
            for dc in range(8):
                if cc < 32:
                    lhs = ip_lhs(gpar, dc, cc % 4)
                    rk = gkey
                else:
                    lhs = ipsr[:, dc * 32:(dc + 1) * 32]
                    rk = "ipsr"
                mm(bank, pb[0:wk, 0:TB], lhs,
                   xlt[:, dc * TB:(dc + 1) * TB], dc == 0, dc == 7,
                   reads=(rk, xlk[dc]), tag="ipmm")
            zr = zb[zrot[0] % 3]
            zrk = f"zb{zrot[0] % 3}"
            zrot[0] += 1
            act(zr[0:wk, 3:3 + TB], pb[0:wk, 0:TB], AF.Identity,
                reads=(bank,), writes=(zrk,), tag="zev")
            def _ti(eng, zr=zr, cc=cc, wk=wk):
                return eng.tensor_copy(zr[0:wk, 0:3],
                                       tails[0:wk, cc * 3:cc * 3 + 3])
            S.add("pool", _ti, reads=(f"tail{cc}",), writes=(zrk + "_t",),
                  tag="ti")
            def _tu(eng, zr=zr, cc=cc, wk=wk):
                return eng.tensor_copy(tails[0:wk, cc * 3:cc * 3 + 3],
                                       zr[0:wk, TB:TB + 3])
            S.add("pool", _tu, reads=(zrk,), writes=(f"tail{cc}",), tag="tu")
            k3 = ckt[0:wk, cc * 4 + 3:cc * 4 + 4]
            cbv = cbt[0:wk, cc:cc + 1]
            act(cvA[0:wk, :], zr[0:wk, 3:3 + TB], AF.Identity,
                reads=(zrk,), writes=("cvA",), scale=k3, bias=cbv,
                sreads=("ck", "cb"), tag="cv1")
            stt(cvB[0:wk, :], zr[0:wk, 2:2 + TB],
                ckt[0:wk, cc * 4 + 2:cc * 4 + 3], cvA[0:wk, :], ALU.mult,
                ALU.add, reads=(zrk, "cvA", zrk + "_t"), writes=("cvB",),
                sreads=("ck",), tag="cv2")
            stt(cvA[0:wk, :], zr[0:wk, 1:1 + TB],
                ckt[0:wk, cc * 4 + 1:cc * 4 + 2], cvB[0:wk, :], ALU.mult,
                ALU.add, reads=(zrk, zrk + "_t", "cvB"), writes=("cvA",),
                sreads=("ck",), tag="cv3")
            stt(outbuf, zr[0:wk, 0:TB], ckt[0:wk, cc * 4:cc * 4 + 1],
                cvA[0:wk, :], ALU.mult, ALU.add,
                reads=(zrk, zrk + "_t", "cvA"), writes=(outkey,),
                sreads=("ck",), tag=tag)

        zs = spk[:, 0:TB]
        chunk_z(32, zs[0:32, :], "zs", "cv_s")
        pwv = spk[:, TB:2 * TB]
        ts(pwv, zs, scoret[:, 0:1], 20.0, ALU.mult, ALU.min, reads=("zs",),
           writes=("pws1",), sreads=("score",), tag="score")
        ts(pwv, pwv, -20.0, None, ALU.max, None, reads=("pws1",),
           writes=("pws2",), tag="clip")
        act(pwv, pwv, AF.Exp, reads=("pws2",), writes=("pexp",), tag="pexp")
        tt(pwbb[:, :], pwv, twtb[:, :], ALU.mult, reads=("pexp", "twt"),
           writes=("pw",), tag="pw")
        denv = spk[:, 2 * TB:3 * TB]
        scan(denv, ones_scan[0:32, :], pwbb[:, :], carry_den[:, 0:1],
             reads=("ones_scan", "pw"), writes=("den",), sreads=("cden",),
             tag="denscan")
        copy(carry_den[:, 0:1], denv[:, TB - 1:TB], reads=("den",),
             writes=("cden",), tag="dencarry")
        ts(denv, denv, 1e-4, None, ALU.max, None, reads=("den",),
           writes=("denc",), tag="denc")

        def val_m0(cc):
            p2 = cc % 2
            chunk_z(cc, xvb[p2][:, :], f"xv{p2}", "cv_xv")
            if cc % 4 == 3 and it * 8 + cc // 4 + 3 < NTB * 8:
                ld_ipg(it * 8 + cc // 4 + 3)
            mm(PBK[2 + p2], PB[2 + p2][:, 0:TB],
               selt[:, cc * 128:(cc + 1) * 128], pwbb[:, :], True, True,
               reads=("sel", "pw"), tag="pwx")

        def val_m1(cc):
            p2 = cc % 2
            xvv = xvb[p2]
            pxb = PBK[2 + p2]
            pxp = PB[2 + p2]
            act(ph3b[:, :], xvv[:, :], AF.Identity, reads=(f"xv{p2}",),
                writes=("ph3",), scale=theta2[:, cc:cc + 1],
                bias=cc_ap("c256"), sreads=("theta2", "cst_c256"), tag="ph3")
            act(mib[:, :], ph3b[:, :], AF.Identity, reads=("ph3",),
                writes=("mi",), tag="mi")
            stt(r2b[:, :], mib[:, :], -1.0, ph3b[:, :], ALU.mult, ALU.add,
                reads=("mi", "ph3"), writes=("r2",), tag="r2")
            act(sinb2[p2][:, :], r2b[:, :], AF.Sin, reads=("r2",),
                writes=(f"sin{p2}",), scale=cc_ap("n2pi"),
                sreads=("cst_n2pi",), tag="sin")
            act(ph3b[:, :], r2b[:, :], AF.Abs, reads=("r2",),
                writes=("absr", "ph3"), scale=cc_ap("p2pi"),
                sreads=("cst_p2pi",), tag="absr")
            act(cosb2[p2][:, :], ph3b[:, :], AF.Sin, reads=("absr",),
                writes=(f"cos{p2}",), bias=cc_ap("npih"),
                sreads=("cst_npih",), tag="cos")
            sqv = sqvb[p2]
            act(sqv[:, :], xvv[:, :], AF.Square, reads=(f"xv{p2}",),
                writes=("sqv",), tag="sqv")
            act(t1b[:, :], xvv[:, :], AF.Identity, reads=(f"xv{p2}",),
                writes=("t1",), scale=cc_ap("w1b"), bias=cc_ap("w0b"),
                sreads=("cst_w1b", "cst_w0b"), tag="t1")
            stt(polyb[:, :], sqv[:, :], cc_ap("nw2b"), t1b[:, :], ALU.mult,
                ALU.add, reads=("sqv", "t1"), writes=("poly",),
                sreads=("cst_nw2b",), tag="poly")
            ppv = ppb[p2]
            tt(ppv[:, :], polyb[:, :], pxp[:, 0:TB], ALU.mult,
               reads=("poly", pxb), writes=(f"pp{p2}",), tag="pp")
            tt(areb[p2][:, :], ppv[:, :], cosb2[p2][:, :], ALU.mult,
               reads=(f"pp{p2}", f"cos{p2}"), writes=(f"are{p2}",),
               stream="pool", tag="are")
            tt(aimb[p2][:, :], ppv[:, :], sinb2[p2][:, :], ALU.mult,
               reads=(f"pp{p2}", f"sin{p2}"), writes=(f"aim{p2}",),
               stream="pool", tag="aim")

        def val_m2a(cc):
            p2 = cc % 2
            nre = nreb[p2]
            nim = nimb[p2]
            scan(nre[:, :], ones_scan[:, :], areb[p2][:, :],
                 carry_re[:, cc:cc + 1], reads=("ones_scan", f"are{p2}"),
                 writes=(f"nre{p2}",), sreads=(f"cre{cc}",), tag="scan_re")
            copy(carry_re[:, cc:cc + 1], nre[:, TB - 1:TB],
                 reads=(f"nre{p2}",), writes=(f"cre{cc}",), stream="pool",
                 tag="cre")
            scan(nim[:, :], ones_scan[:, :], aimb[p2][:, :],
                 carry_im[:, cc:cc + 1], reads=("ones_scan", f"aim{p2}"),
                 writes=(f"nim{p2}",), sreads=(f"cim{cc}",), tag="scan_im")
            copy(carry_im[:, cc:cc + 1], nim[:, TB - 1:TB],
                 reads=(f"nim{p2}",), writes=(f"cim{cc}",), stream="pool",
                 tag="cim")

        def val_m2b(cc):
            p2 = cc % 2
            nre = nreb[p2]
            nim = nimb[p2]
            copy(nbfre[p2][:, :], nre[:, :], reads=(f"nre{p2}",),
                 writes=(f"nbfre{p2}",), tag="nbfre")
            copy(nbfim[p2][:, :], nim[:, :], reads=(f"nim{p2}",),
                 writes=(f"nbfim{p2}",), stream="pool", tag="nbfim")
            tt(sqbf[p2][:, :], nbfre[p2][:, :], nbfre[p2][:, :], ALU.mult,
               reads=(f"nbfre{p2}",), writes=(f"sqbf{p2}",), tag="sqre")
            tt(sqbf[2 + p2][:, :], nbfim[p2][:, :], nbfim[p2][:, :], ALU.mult,
               reads=(f"nbfim{p2}",), writes=(f"sqbf{2 + p2}",), tag="sqim")

        def val_m2c(cc):
            p2 = cc % 2
            mm("PB6", PB[6][0:32, 0:TB], sel2t[:, cc * 32:(cc + 1) * 32],
               sqbf[p2][:, :], cc == 0, False, reads=("sel2", f"sqbf{p2}"),
               tag="ssq_re")
            mm("PB6", PB[6][0:32, 0:TB], sel2t[:, cc * 32:(cc + 1) * 32],
               sqbf[2 + p2][:, :], False, cc == NVC - 1,
               reads=("sel2", f"sqbf{2 + p2}"), tag="ssq_im")
            yb_ = PBK[4 + p2]
            ypb = PB[4 + p2]
            mm(yb_, ypb[:, 0:TB], wrebd[:, :], nbfre[p2][:, :],
               True, False, reads=("wrebd", f"nbfre{p2}"), tag="ymm")
            mm(yb_, ypb[:, 0:TB], wimbd[:, :], nbfim[p2][:, :],
               False, True, reads=("wimbd", f"nbfim{p2}"), tag="ymm")
            act(yvb[:, cc * TB:(cc + 1) * TB], ypb[:, 0:TB], AF.Identity,
                reads=(yb_,), writes=(f"yv{cc}",), tag="yv")

        for tstep in range(NVC + 3):
            if tstep < NVC:
                val_m0(tstep)
            if 1 <= tstep <= NVC:
                val_m1(tstep - 1)
            if 2 <= tstep <= NVC + 1:
                val_m2a(tstep - 2)
                val_m2b(tstep - 2)
            if tstep >= 3:
                val_m2c(tstep - 3)

        if it == 1 and dbg_b is not None:
            dma("sync", dbg_b[:, 0:TB], xvb[1][:, :], reads=("xv1",),
                writes=("dbgb0",), tag="dbg")
            dma("sync", dbg_b[:, TB:2 * TB], nreb[1][:, :], reads=("nre1",),
                writes=("dbgb1",), tag="dbg")
            dma("sync", dbg_b[:, 2 * TB:3 * TB], nimb[1][:, :], reads=("nim1",),
                writes=("dbgb2",), tag="dbg")
            dma("pool", dbg_b[:, 3 * TB:4 * TB], xlnbf[:, 0:TB],
                reads=("xln0",), writes=("dbgb3",), tag="dbg")
            dma("sync", dbg_b[:, 4 * TB:5 * TB], sinb2[1][:, :], reads=("sin1",),
                writes=("dbgb4",), tag="dbg")
            dma("sync", dbg_b[:, 5 * TB:6 * TB], ppb[1][:, :], reads=("pp1",),
                writes=("dbgb5",), tag="dbg")
            dma("sync", dbg_bs[:, 0:TB], spk[:, TB:2 * TB], reads=("pw",),
                writes=("dbgs0",), tag="dbg")
            dma("sync", dbg_bs[:, TB:2 * TB], spk[:, 2 * TB:3 * TB],
                reads=("denc",), writes=("dbgs1",), tag="dbg")
        if it + 1 < NTB:
            ld_ut_b(it + 1)
            ln_b(it + 1)
        sv = spk[:, 3 * TB:4 * TB]
        tmp = spk[:, TB:2 * TB]
        ts(tmp, PB[6][0:32, 0:TB], 1.0 / 128.0, None, ALU.mult, None,
           reads=("PB6",), writes=("ssq_s", "pexp"), tag="ssq_s")
        act(sv, denv, AF.Square, reads=("denc",), writes=("dc2",), tag="dc2")
        stt(spk[:, 0:TB], sv, 1e-5, tmp, ALU.mult, ALU.add,
            reads=("dc2", "ssq_s"), writes=("inner",), tag="inner")
        act(sv, spk[:, 0:TB], AF.Ln, reads=("inner",), writes=("rt2",),
            tag="rt2")
        act(svbb[:, :], sv, AF.Exp, reads=("rt2",), writes=("sval",),
            scale=cst[0:32, CCOL["nhalf"]:CCOL["nhalf"] + 1],
            sreads=("cst_nhalf",), tag="sval")

        if it == 1 and dbg_b2 is not None:
            dma("pool", dbg_b2[:, 0:TB], yvb[:, 15 * TB:16 * TB],
                reads=("yv15",), writes=("dbgy0",), tag="dbg")
            dma("pool", dbg_b2[:, TB:2 * TB], sqbf[0][:, :],
                reads=("sqbf0",), writes=("dbgy1",), tag="dbg")
            dma("sync", dbg_bs[:, 2 * TB:3 * TB], svalb[:, :],
                reads=("sval",), writes=("dbgy2",), tag="dbg")
        def gate_g0(g):
            cc = 16 + g
            p2 = g % 2
            chunk_z(cc, xvb[p2][:, :], f"xv{p2}", "cv_g")
            if cc % 4 == 3 and it * 8 + cc // 4 + 3 < NTB * 8:
                ld_ipg(it * 8 + cc // 4 + 3)

        def gate_g1(g):
            p2 = g % 2
            gv = gateb[p2]
            act(gv[:, :], xvb[p2][:, :], AF.Silu, reads=(f"xv{p2}",),
                writes=(f"gate{p2}",), tag="silu")
            sxb = PBK[2 + p2]
            sxp = PB[2 + p2]
            mm(sxb, sxp[:, 0:TB], selt[:, g * 128:(g + 1) * 128],
               svbb[:, :], True, True, reads=("sel", "sval"), tag="sexp")
            tt(ysc[:, :], yvb[:, g * TB:(g + 1) * TB], sxp[:, 0:TB],
               ALU.mult, reads=(f"yv{g}", sxb), writes=("ysc",), tag="yscale")
            tt(yvb[:, g * TB:(g + 1) * TB], ysc[:, :], gv[:, :], ALU.mult,
               reads=("ysc", f"gate{p2}"), writes=(f"yv{g}",), tag="ygate")

        for tstep in range(NGC + 1):
            if tstep < NGC:
                gate_g0(tstep)
            if tstep >= 1:
                gate_g1(tstep - 1)

        ld_op(0)
        ld_op(1)
        for dc in range(8):
            ob = PBK[dc % 2]
            opb = PB[dc % 2]
            for kc in range(16):
                lhs = opd[dc % 2][:, kc * 128:(kc + 1) * 128]
                mm(ob, opb[:, 0:TB], lhs, yvb[:, kc * TB:(kc + 1) * TB],
                   kc == 0, kc == 15, reads=(f"opd{dc % 2}", f"yv{kc}"),
                   tag="opmm")
            if dc + 2 < 8:
                ld_op(dc + 2)
            eb = ebuf[dc % 2]
            def _res(eng, opb=opb, eb=eb, utt=utt, dc=dc):
                return eng.tensor_tensor(eb[:, :], opb[:, 0:TB],
                                         utt[:, dc, :], op=ALU.add)
            S.add("dve", _res, reads=(ob, utkeys[dc]),
                  writes=(f"ebuf{dc % 2}",), tag="res_b")
            dst = u2_d[dc * 128:(dc + 1) * 128, t0:t0 + TB]
            dma("act", dst, eb[:, :], reads=(f"ebuf{dc % 2}",),
                writes=(f"u2d_{it}_{dc}",), tag="st_u2")

    st_b.close()
    S.barrier()

    st_c = ExitStack()
    w1sbC = sb("w1sbC", [128, NE * 8 * 512], BF16, st_c)
    w2sbC = sb("w2sbC", [128, NE * 4 * 1024], BF16, st_c)
    utC = [sb(f"utC{i}", [128, 8, TA], F32, st_c) for i in range(2)]
    hsbC = sb("hsbC", [128, NE * 4 * TA], BF16, st_c)
    hgbC = [sb(f"hgbC{i}", [128, TA], BF16, st_c) for i in range(2)]
    ewbsC = [sb(f"ewbsC{i}", [128, TA], BF16, st_c) for i in range(2)]
    ewfC = sb("ewfC", [1, NE * TA], BF16, st_c)
    qtsC = sb("qtsC", [6, 128], F32, st_c)
    ytmC = [sb(f"ytmC{i}", [128, 128], F32, st_c) for i in range(2)]

    w1keysC = [f"w1C_{e}" for e in range(NE)]
    w2keysC = [f"w2C_{e}" for e in range(NE)]
    for e in range(NE):
        src = w1_d[e].rearrange("(dc p) h -> p dc h", p=128)
        dst = w1sbC[:, e * 4096:(e + 1) * 4096].rearrange(
            "p (dc h) -> p dc h", dc=8)
        dma("pool", dst, src, reads=(), writes=(w1keysC[e],), tag="ldw1C")
    for e in range(NE):
        src = w2_d[e].rearrange("(hc p) d -> p hc d", p=128)
        dst = w2sbC[:, e * 4096:(e + 1) * 4096].rearrange(
            "p (hc d) -> p hc d", hc=4)
        dma("pool", dst, src, reads=(), writes=(w2keysC[e],), tag="ldw2C")

    emit_ffn("C", w1keysC, w2keysC, w1sbC, w2sbC, None, utC, hsbC, hgbC,
             ewbsC, ewfC, qtsC, ytmC, u2_d, None, True)
    st_c.close()

    S.finalize()
    doms = sorted(S.totals.keys() | {"pe", "dve", "act", "pool"})
    sem_cm = {}
    for d_ in doms:
        sems[d_] = st_p.enter_context(nc.semaphore("sem_" + d_))
    with st_p, nc.Block() as block:
        @block.tensor
        def _(eng):
            S.emit_stream("pe", eng, sems)

        @block.vector
        def _(eng):
            S.emit_stream("dve", eng, sems)

        @block.scalar
        def _(eng):
            S.emit_stream("act", eng, sems)

        @block.gpsimd
        def _(eng):
            S.emit_stream("pool", eng, sems)

        @block.sync
        def _(eng):
            S.emit_stream("sync", eng, sems)
            for d_ in doms:
                tot = S.totals.get(d_, 0)
                if tot:
                    eng.wait_ge(sems[d_], tot * (16 if d_.startswith("q")
                                                 else 1))
    return nc


_PROG_CACHE = {}


def _host_prep(inputs):
    ns = np.asarray(inputs["L0_attn_norm_scale"], np.float32)
    theta = np.asarray(inputs["L0_attn_theta"], np.float32).reshape(2048)
    conv_k = np.asarray(inputs["L0_attn_conv_k"], np.float32)
    conv_b = np.asarray(inputs["L0_attn_conv_b"], np.float32)
    proj_w = np.asarray(inputs["L0_ffn_proj_w"], np.float32)

    theta_t = np.ascontiguousarray(theta.reshape(16, 128).T)
    ck = np.zeros((128, 33 * 4), np.float32)
    cb = np.zeros((128, 33), np.float32)
    for ccc in range(33):
        w = 128 if ccc < 32 else 32
        for j in range(4):
            ck[:w, ccc * 4 + j] = conv_k[j, 0, ccc * 128:ccc * 128 + w]
        cb[:w, ccc] = conv_b[ccc * 128:ccc * 128 + w]
    score = np.asarray(inputs["L0_attn_score_scale"], np.float32).reshape(32, 1)
    slopes = np.concatenate([
        np.asarray(inputs["L0_attn_decay_slopes"], np.float32),
        np.asarray(inputs["L0_attn_anchor_slopes"], np.float32)]).reshape(32, 1)
    deriv = np.asarray(inputs["L0_attn_deriv_logits"],
                       np.float32).reshape(1, 3)
    wre = np.asarray(inputs["L0_attn_W_re"], np.float32)
    wim = np.asarray(inputs["L0_attn_W_im"], np.float32)
    Wre2 = np.ascontiguousarray(np.concatenate([wre, wre], 0))
    Wim2 = np.ascontiguousarray(np.concatenate([wim, wim], 0))
    nsre2 = np.ascontiguousarray(
        np.concatenate([ns[:64], ns[:64]]).reshape(128, 1))
    nsim2 = np.ascontiguousarray(
        np.concatenate([ns[64:], ns[64:]]).reshape(128, 1))
    pwT = np.zeros((128, 48), np.float32)
    for dc in range(8):
        pwT[:, dc * 6:(dc + 1) * 6] = proj_w[:, dc * 128:(dc + 1) * 128].T
    anchT = np.ascontiguousarray(
        np.asarray(inputs["L0_ffn_anchors"], np.float32).T)
    rt = np.asarray(inputs["L0_ffn_routing_temp"],
                    np.float32).reshape(1, 1)
    SEL = np.zeros((32, 2048), np.float32)
    SEL2 = np.zeros((128, 512), np.float32)
    for ccc in range(16):
        for p in range(128):
            k = 2 * ccc + p // 64
            SEL[k, ccc * 128 + p] = 1.0
            SEL2[p, ccc * 32 + k] = 1.0
    ident = np.eye(128, dtype=np.float32)
    pos = np.arange(L, dtype=np.float32)
    dist = np.zeros((32, L), np.float32)
    dist[0:28, :] = (L - 1) - pos[None, :]
    dist[28:32, :] = pos[None, :]

    shared = {
        "ipw": np.ascontiguousarray(np.asarray(inputs["L0_attn_in_proj_w"],
                                               np.float32)),
        "opw": np.ascontiguousarray(np.asarray(inputs["L0_attn_out_proj_w"],
                                               np.float32)),
        "w1": np.ascontiguousarray(np.asarray(inputs["L0_ffn_w1"],
                                              np.float32)),
        "w2": np.ascontiguousarray(np.asarray(inputs["L0_ffn_w2"],
                                              np.float32)),
        "theta_t": theta_t, "ck": ck, "cb": cb, "score": score,
        "slopes": slopes, "deriv": deriv, "Wre2": Wre2, "Wim2": Wim2,
        "nsre2": nsre2, "nsim2": nsim2, "pwT": pwT, "anchT": anchT,
        "rt": rt, "SEL": SEL, "SEL2": SEL2, "ident": ident, "dist": dist,
        "hostc_in": np.array([[1e-5, 0.0, 0.0, 0.0]], np.float32),
    }
    return shared


def kernel(**inputs):
    x = np.asarray(inputs["L0_attn_x"], np.float32)
    shared = _host_prep(inputs)
    if "prog" not in _PROG_CACHE:
        _PROG_CACHE["prog"] = build_program()
    nc = _PROG_CACHE["prog"]
    in_maps = []
    for b in range(8):
        m = dict(shared)
        m["x"] = np.ascontiguousarray(x[b])
        in_maps.append(m)
    res = run_bass_kernel_spmd(nc, in_maps, core_ids=list(range(8)))
    out = np.stack([res.results[b]["y"] for b in range(8)], axis=0)
    return out.astype(np.float32)


if __name__ == "__main__":
    import reference as _r
    ins = _r.setup_inputs(0)
    out = kernel(**{k: np.asarray(v) for k, v in ins.items()})
    ref = np.asarray(_r.reference(**ins))
    err = np.sqrt(((out - ref) ** 2).mean() / (ref ** 2).mean())
    print("rel l2:", err)


# version 17
# speedup vs baseline: 1.1868x; 1.0015x over previous
import math
import numpy as np
from contextlib import ExitStack

import concourse.bass as bass
import concourse.mybir as mybir
from concourse.bass_utils import run_bass_kernel_spmd

F32 = mybir.dt.float32
BF16 = mybir.dt.bfloat16
I32 = mybir.dt.int32
AF = mybir.ActivationFunctionType
ALU = mybir.AluOpType
AX = mybir.AxisListType
PI = float(np.pi)

L = 4096
D = 1024
DI = 2048
C = 4128
K = 32
H = 64
NE = 6
DE = 512
TA = 512
TB = 512
NTA = L // TA
NTB = L // TB
NVC = 16
NGC = 16


class Op:
    __slots__ = ("stream", "dom", "emit", "deps", "need_inc", "inc_no",
                 "dma", "tag", "gidx", "sidx", "risky")


class Sched:

    def __init__(self):
        self.ops = []
        self.bufs = {}
        self.pending_barrier = None
        self.barrier_absorbed = set()
        self.scount = {}

    def add(self, stream, emit, reads=(), writes=(), sreads=(), dma=False,
            tag="", risky=False):
        o = Op()
        o.stream = stream
        if dma:
            qi = self.scount.get("qn_" + stream, 0)
            self.scount["qn_" + stream] = qi + 1
            o.dom = f"q{stream}L{qi % 16}"
        else:
            o.dom = stream
        o.emit = emit
        o.dma = dma
        o.tag = tag
        o.need_inc = False
        o.risky = risky
        o.sidx = self.scount.get(stream, 0)
        self.scount[stream] = o.sidx + 1
        deps = {}

        def dep(p, wait_needed):
            if p is None or p is o:
                return
            k = id(p)
            if k in deps:
                deps[k] = (p, deps[k][1] or wait_needed)
            else:
                deps[k] = (p, wait_needed)

        def need_wait(p, kind, scalar=False):
            if p.dom != o.dom:
                return True
            if p.dma and o.dma:
                return kind == "raw"
            if stream == "pe":
                return False
            if kind != "raw":
                return False
            return True

        for b in reads:
            e = self.bufs.get(b)
            if e and e[0] is not None:
                dep(e[0], need_wait(e[0], "raw", False))
        for b in sreads:
            e = self.bufs.get(b)
            if e and e[0] is not None:
                dep(e[0], need_wait(e[0], "raw", True))
        for b in writes:
            e = self.bufs.get(b)
            if e:
                if e[0] is not None:
                    dep(e[0], need_wait(e[0], "waw"))
                for r in e[1]:
                    dep(r, need_wait(r, "war"))
        if self.pending_barrier is not None and stream not in self.barrier_absorbed:
            self.barrier_absorbed.add(stream)
            for p in self.pending_barrier.values():
                dep(p, True)
        o.deps = list(deps.values())
        for b in list(reads) + list(sreads):
            self.bufs.setdefault(b, [None, []])[1].append(o)
        for b in writes:
            self.bufs[b] = [o, []]
        o.gidx = len(self.ops)
        self.ops.append(o)
        return o

    def barrier(self):
        last = {}
        for o in self.ops:
            last[o.dom] = o
        self.pending_barrier = last
        self.barrier_absorbed = set()

    def finalize(self):
        for o in self.ops:
            if o.dma:
                o.need_inc = True
            for (p, w) in o.deps:
                if w:
                    p.need_inc = True
        ctr = {}
        for o in self.ops:
            if o.need_inc:
                ctr[o.dom] = ctr.get(o.dom, 0) + 1
                o.inc_no = ctr[o.dom]
            else:
                o.inc_no = None
        self.totals = ctr

    def emit_stream(self, stream, eng, sems):
        known = {}
        for o in self.ops:
            if o.stream != stream:
                continue
            for (p, w) in o.deps:
                if not w:
                    continue
                v = p.inc_no * (16 if p.dma else 1)
                if known.get(p.dom, 0) < v:
                    eng.wait_ge(sems[p.dom], v)
                    known[p.dom] = v
            inst = o.emit(eng)
            if o.need_inc:
                inst.then_inc(sems[o.dom], 16 if o.dma else 1)


def build_program(dbg=()):
    nc = bass.Bass("TRN2", target_bir_lowering=False)
    S = Sched()

    def din(name, shape):
        return nc.dram_tensor(name, shape, F32, kind="ExternalInput").ap()

    x_d = din("x", [L, D])
    ipw_d = din("ipw", [D, C])
    opw_d = din("opw", [DI, D])
    w1_d = din("w1", [NE, D, DE])
    w2_d = din("w2", [NE, DE, D])
    theta_d = din("theta_t", [128, NVC])
    ck_d = din("ck", [128, 33 * 4])
    cb_d = din("cb", [128, 33])
    score_d = din("score", [32, 1])
    slopes_d = din("slopes", [32, 1])
    deriv_d = din("deriv", [1, 3])
    wre_d = din("Wre2", [128, 64])
    wim_d = din("Wim2", [128, 64])
    nsre_d = din("nsre2", [128, 1])
    nsim_d = din("nsim2", [128, 1])
    pwt_d = din("pwT", [128, 48])
    anch_d = din("anchT", [6, 6])
    rt_d = din("rt", [1, 1])
    sel_d = din("SEL", [32, 2048])
    sel2_d = din("SEL2", [128, 512])
    ident_d = din("ident", [128, 128])
    dist_d = din("dist", [32, L])
    hostc_d = din("hostc_in", [1, 4])

    y_d = nc.dram_tensor("y", [L, D], F32, kind="ExternalOutput").ap()
    if "u1" in dbg:
        u1_d = nc.dram_tensor("u1", [D, L], F32, kind="ExternalOutput").ap()
    else:
        u1_d = nc.dram_tensor("u1", [D, L], F32).ap()
    if "u2" in dbg:
        u2_d = nc.dram_tensor("u2", [D, L], F32, kind="ExternalOutput").ap()
    else:
        u2_d = nc.dram_tensor("u2", [D, L], F32).ap()
    ipbf_d = nc.dram_tensor("ipbf", [8, 128, 4096], BF16).ap()
    opbf_d = nc.dram_tensor("opbf", [8, 128, 2048], BF16).ap()
    if "a1" in dbg:
        dbg_ut = nc.dram_tensor("dbg_ut", [128, 8, TA], F32,
                                kind="ExternalOutput").ap()
        dbg_xln = nc.dram_tensor("dbg_xln", [128, 8 * TA], F32,
                                 kind="ExternalOutput").ap()
        dbg_ewf = nc.dram_tensor("dbg_ewf", [1, NE * TA], F32,
                                 kind="ExternalOutput").ap()
        dbg_hsb = nc.dram_tensor("dbg_hsb", [128, NE * 4 * TA], F32,
                                 kind="ExternalOutput").ap()
        dbg_rpk = nc.dram_tensor("dbg_rpk", [128, 64], F32,
                                 kind="ExternalOutput").ap()
        dbg_cst = nc.dram_tensor("dbg_cst", [128, 16], F32,
                                 kind="ExternalOutput").ap()
        dbg_sm = nc.dram_tensor("dbg_sm", [1, 16], F32,
                                kind="ExternalOutput").ap()
        dbg_b = nc.dram_tensor("dbg_b", [128, 6 * TB], F32,
                               kind="ExternalOutput").ap()
        dbg_bs = nc.dram_tensor("dbg_bs", [32, 4 * TB], F32,
                                kind="ExternalOutput").ap()
        dbg_b2 = nc.dram_tensor("dbg_b2", [128, 2 * TB], F32,
                                kind="ExternalOutput").ap()
    else:
        dbg_ut = dbg_xln = dbg_ewf = dbg_hsb = dbg_rpk = dbg_b = dbg_bs = None
        dbg_b2 = None

    st_p = ExitStack()

    def sb(name, shape, dt, st=st_p):
        return st.enter_context(nc.sbuf_tensor(name, shape, dt))

    def pt(name, shape, dt, st=st_p):
        return st.enter_context(nc.psum_tensor(name, shape, dt))

    xlnbf = sb("xlnbf", [128, 4096], BF16)
    sqb = [sb(f"sqb{i}", [128, 512], BF16) for i in range(2)]
    d1b = sb("d1b", [128, 512], F32)
    rowp = sb("rowp", [1, 2048], F32)
    rpk = sb("rpk", [128, 64], F32)
    cst = sb("cst", [128, 16], F32)
    ebuf = [sb(f"ebuf{i}", [128, 512], F32) for i in range(2)]
    identt = sb("identt", [128, 128], F32)
    onesr = sb("onesr", [1, 128], F32)
    onesrb = sb("onesrb", [1, 128], BF16)
    onescb = sb("onescb", [128, 1], BF16)
    ubb = [sb(f"ubb{i}", [128, 512], BF16) for i in range(2)]
    onesc = sb("onesc", [128, 1], F32)
    theta_t = sb("theta_sb", [128, NVC], F32)
    theta2 = sb("theta2", [128, NVC], F32)
    ckt = sb("ckt", [128, 33 * 4], F32)
    cbt = sb("cbt", [128, 33], F32)
    scoret = sb("scoret", [32, 1], F32)
    slopest = sb("slopest", [32, 1], F32)
    negslope = sb("negslope", [32, 1], F32)
    derivt = sb("derivt", [1, 3], F32)
    wret = sb("wret", [128, 64], F32)
    wimt = sb("wimt", [128, 64], F32)
    nsret = sb("nsret", [128, 1], F32)
    nsimt = sb("nsimt", [128, 1], F32)
    wrens = sb("wrens", [128, 64], BF16)
    wimns = sb("wimns", [128, 64], BF16)
    pwtt = sb("pwtt", [128, 48], F32)
    ancht = sb("ancht", [6, 6], F32)
    rtt = sb("rtt", [1, 1], F32)
    selt = sb("selt", [32, 2048], BF16)
    sel2t = sb("sel2t", [128, 512], BF16)
    ones_scan = sb("ones_scan", [128, 512], F32)
    wrebd = sb("wrebd", [128, 128], BF16)
    wimbd = sb("wimbd", [128, 128], BF16)
    carry_re = sb("carry_re", [128, NVC], F32)
    carry_im = sb("carry_im", [128, NVC], F32)
    carry_den = sb("carry_den", [32, 1], F32)
    tails = sb("tails", [128, 33 * 3], F32)
    dsm = sb("dsm", [1, 8], F32)
    wsm = sb("wsm", [1, 3], F32)
    rts = sb("rts", [1, 4], F32)
    spl = sb("spl", [32, 4], F32)
    hostc = sb("hostc", [1, 4], F32)

    PB = [pt(f"PB{i}", [128, 512], F32) for i in range(8)]
    PBK = [f"PB{i}" for i in range(8)]

    CONSTS = {"c256": 256.5, "n2pi": -2 * PI, "p2pi": 2 * PI,
              "npih": -PI / 2, "nhalf": -0.5, "half": 0.5, "one": 1.0}
    CCOL = {k: i for i, k in enumerate(CONSTS)}
    NDCOL = {"w0b": 8, "w1b": 9, "nw2b": 10, "hirb": 11, "iqt": 12}

    def cc_ap(name, pc=128):
        i = CCOL[name] if name in CCOL else NDCOL[name]
        return cst[0:pc, i:i + 1]

    sems = {}

    def mm(bank, out_ap, lhsT, rhs, start, stop, reads, tag="mm",
           tile_position=None):
        def e(eng, out_ap=out_ap, lhsT=lhsT, rhs=rhs, start=start, stop=stop,
              tp=tile_position):
            return eng.matmul(out_ap, lhsT, rhs, start=start, stop=stop,
                              tile_position=tp)
        return S.add("pe", e, reads=tuple(reads), writes=(bank,), tag=tag)

    def transpose(bank, out_ap, in_ap, reads, ident_ap=None, tag="tr"):
        ia = ident_ap if ident_ap is not None else identt[:, :]
        def e(eng, out_ap=out_ap, in_ap=in_ap, ia=ia):
            return eng.transpose(out_ap, in_ap, ia)
        return S.add("pe", e, reads=tuple(reads) + ("identt",), writes=(bank,),
                     tag=tag)

    def act(out_ap, in_ap, func, reads, writes, scale=None, bias=None,
            sreads=(), tag="act"):
        def e(eng, out_ap=out_ap, in_ap=in_ap, func=func, scale=scale,
              bias=bias):
            kw = {}
            if scale is not None:
                kw["scale"] = scale
            if bias is not None:
                kw["bias"] = bias
            return eng.activation(out_ap, in_ap, func, **kw)
        return S.add("act", e, reads=tuple(reads), writes=tuple(writes),
                     sreads=tuple(sreads), tag=tag)

    def dve(emit, reads, writes, sreads=(), tag="dve"):
        return S.add("dve", emit, reads=tuple(reads), writes=tuple(writes),
                     sreads=tuple(sreads), tag=tag)

    def pool(emit, reads, writes, sreads=(), tag="pool"):
        return S.add("pool", emit, reads=tuple(reads), writes=tuple(writes),
                     sreads=tuple(sreads), tag=tag)

    def dma(queue, out_ap, in_ap, reads, writes, tag="dma"):
        def e(eng, out_ap=out_ap, in_ap=in_ap):
            return eng.dma_start(out=out_ap, in_=in_ap)
        return S.add(queue, e, reads=tuple(reads), writes=tuple(writes),
                     dma=True, tag=tag)

    def ts(out_ap, in_ap, s1, s2, op0, op1, reads, writes, sreads=(),
           stream="dve", tag="ts"):
        def e(eng, out_ap=out_ap, in_ap=in_ap, s1=s1, s2=s2, op0=op0, op1=op1):
            if op1 is None:
                return eng.tensor_scalar(out_ap, in_ap, s1, None, op0=op0)
            return eng.tensor_scalar(out_ap, in_ap, s1, s2, op0=op0, op1=op1)
        return S.add(stream, e, reads=tuple(reads), writes=tuple(writes),
                     sreads=tuple(sreads), tag=tag)

    def tt(out_ap, a_ap, b_ap, op, reads, writes, stream="dve", tag="tt"):
        def e(eng, out_ap=out_ap, a_ap=a_ap, b_ap=b_ap, op=op):
            return eng.tensor_tensor(out_ap, a_ap, b_ap, op=op)
        return S.add(stream, e, reads=tuple(reads), writes=tuple(writes),
                     tag=tag)

    def stt(out_ap, in0_ap, s_ap, in1_ap, op0, op1, reads, writes, sreads=(),
            stream="dve", tag="stt"):
        def e(eng, out_ap=out_ap, in0_ap=in0_ap, s_ap=s_ap, in1_ap=in1_ap,
              op0=op0, op1=op1):
            return eng.scalar_tensor_tensor(out_ap, in0_ap, s_ap, in1_ap,
                                            op0=op0, op1=op1)
        return S.add(stream, e, reads=tuple(reads), writes=tuple(writes),
                     sreads=tuple(sreads), tag=tag)

    def copy(out_ap, in_ap, reads, writes, stream="dve", tag="copy"):
        def e(eng, out_ap=out_ap, in_ap=in_ap):
            return eng.tensor_copy(out_ap, in_ap)
        return S.add(stream, e, reads=tuple(reads), writes=tuple(writes),
                     tag=tag)

    def memset(ap, v, writes, stream="dve", tag="memset"):
        def e(eng, ap=ap, v=v):
            return eng.memset(ap, v)
        return S.add(stream, e, writes=tuple(writes), tag=tag)

    def scan(out_ap, d0_ap, d1_ap, init_ap, reads, writes, sreads, tag="scan"):
        def e(eng, out_ap=out_ap, d0_ap=d0_ap, d1_ap=d1_ap, init_ap=init_ap):
            return eng.tensor_tensor_scan(out_ap, d0_ap, d1_ap, init_ap,
                                          op0=ALU.mult, op1=ALU.add)
        return S.add("dve", e, reads=tuple(reads), writes=tuple(writes),
                     sreads=tuple(sreads), tag=tag, risky=True)

    def reduce(out_ap, in_ap, op, reads, writes, tag="red"):
        def e(eng, out_ap=out_ap, in_ap=in_ap, op=op):
            return eng.tensor_reduce(out_ap, in_ap, axis=AX.X, op=op)
        return S.add("dve", e, reads=tuple(reads), writes=tuple(writes),
                     tag=tag, risky=True)

    def recip(out_ap, in_ap, reads, writes, tag="recip"):
        def e(eng, out_ap=out_ap, in_ap=in_ap):
            return eng.reciprocal(out_ap, in_ap)
        return S.add("dve", e, reads=tuple(reads), writes=tuple(writes),
                     tag=tag)

    for apout, src, wk in [
        (theta_t[:, :], theta_d, "theta"), (ckt[:, :], ck_d, "ck"),
        (cbt[:, :], cb_d, "cb"), (scoret[:, :], score_d, "score"),
        (slopest[:, :], slopes_d, "slopes"), (derivt[:, :], deriv_d, "deriv"),
        (wret[:, :], wre_d, "wre"), (wimt[:, :], wim_d, "wim"),
        (nsret[:, :], nsre_d, "nsre"), (nsimt[:, :], nsim_d, "nsim"),
        (pwtt[:, :], pwt_d, "pwt"), (ancht[:, :], anch_d, "anch"),
        (rtt[:, :], rt_d, "rtsb"),
        (identt[:, :], ident_d, "identt"),
        (hostc[:, :], hostc_d, "hostc"),
    ]:
        dma("sync", apout, src, reads=(), writes=(wk,), tag="ld_" + wk)

    for name, v in CONSTS.items():
        memset(cst[:, CCOL[name]:CCOL[name] + 1], v, writes=(f"cst_{name}",))
    dma("pool", sel2t[:, :], sel2_d, reads=(), writes=("sel2",),
        tag="ld_sel2")
    dma("pool", selt[:, :], sel_d, reads=(), writes=("sel",), tag="ld_sel")
    memset(onesr[:, :], 1.0, writes=("onesr",))
    memset(onesrb[:, :], 1.0, writes=("onesrb",))
    memset(onesc[:, :], 1.0, writes=("onesc",))
    memset(onescb[:, :], 1.0, writes=("onescb",))
    memset(ones_scan[:, :], 1.0, writes=("ones_scan",))
    memset(carry_re[:, :], 0.0, writes=tuple(f"cre{c}" for c in range(NVC)))
    memset(carry_im[:, :], 0.0, writes=tuple(f"cim{c}" for c in range(NVC)))
    memset(carry_den[:, :], 0.0, writes=("cden",))
    memset(tails[:, :], 0.0, writes=tuple(f"tail{c}" for c in range(33)))

    ts(theta2[:, :], theta_t[:, :], 1.0 / (2 * PI), None, ALU.mult, None,
       reads=("theta",), writes=("theta2",))
    NEG1_COL = 13
    memset(cst[:, NEG1_COL:NEG1_COL + 1], -1.0, writes=("cst_neg1",))
    act(spl[:, 0:1], slopest[:, :], AF.Abs, reads=("slopes",), writes=("spl0",))
    act(spl[:, 1:2], spl[:, 0:1], AF.Exp, reads=("spl0",), writes=("spl1",),
        scale=cst[0:32, NEG1_COL:NEG1_COL + 1], sreads=("cst_neg1",))
    act(spl[:, 2:3], spl[:, 1:2], AF.Ln, reads=("spl1",), writes=("spl2",),
        bias=cc_ap("one", 32), sreads=("cst_one",))
    ts(spl[:, 3:4], slopest[:, :], 0.0, None, ALU.max, None,
       reads=("slopes",), writes=("spl3",))
    def _ns(eng):
        return eng.tensor_tensor(negslope[:, :], spl[:, 2:3], spl[:, 3:4],
                                 op=ALU.add)
    S.add("dve", _ns, reads=("spl2", "spl3"), writes=("negslope_pos",))
    ts(negslope[:, :], negslope[:, :], -1.0, None, ALU.mult, None,
       reads=("negslope_pos",), writes=("negslope",))

    reduce(dsm[:, 0:1], derivt[:, :], ALU.max, reads=("deriv",),
           writes=("dsm_max",))
    ts(wsm[:, :], derivt[:, :], dsm[:, 0:1], None, ALU.subtract, None,
       reads=("deriv",), writes=("wsm_sub",), sreads=("dsm_max",))
    act(wsm[:, :], wsm[:, :], AF.Exp, reads=("wsm_sub",), writes=("wsm_exp",))
    reduce(dsm[:, 1:2], wsm[:, :], ALU.add, reads=("wsm_exp",),
           writes=("dsm_sum",))
    act(dsm[:, 3:4], dsm[:, 1:2], AF.Ln, reads=("dsm_sum",),
        writes=("dsm_ln",))
    act(dsm[:, 2:3], dsm[:, 3:4], AF.Exp, reads=("dsm_ln",),
        writes=("dsm_rec",), scale=cst[0:1, NEG1_COL:NEG1_COL + 1],
        sreads=("cst_neg1",))
    ts(wsm[:, :], wsm[:, :], dsm[:, 2:3], None, ALU.mult, None,
       reads=("wsm_exp",), writes=("wsm",), sreads=("dsm_rec",))
    for j, (name, neg) in enumerate([("w0b", False), ("w1b", False),
                                     ("nw2b", True)]):
        bank_ap = PB[7][:, j:j + 1]
        mm("PB7", bank_ap, onesr[:, :], wsm[:, j:j + 1], True, True,
           reads=("onesr", "wsm"), tag="bc_w")
        col = NDCOL[name]
        ts(cst[:, col:col + 1], bank_ap, -1.0 if neg else 1.0, None, ALU.mult,
           None, reads=("PB7",), writes=(f"cst_{name}",))
    ts(rts[:, 0:1], rtt[:, :], 0.1, None, ALU.max, None, reads=("rtsb",),
       writes=("rtc",))
    act(rts[:, 3:4], rts[:, 0:1], AF.Ln, reads=("rtc",), writes=("rtln",))
    act(rts[:, 1:2], rts[:, 3:4], AF.Exp, reads=("rtln",), writes=("rtrec",),
        scale=cst[0:1, NEG1_COL:NEG1_COL + 1], sreads=("cst_neg1",))
    ts(rts[:, 2:3], rts[:, 1:2], 0.5, None, ALU.mult, None, reads=("rtrec",),
       writes=("hir1",))
    mm("PB7", PB[7][:, 4:5], onesr[:, :], rts[:, 2:3], True, True,
       reads=("onesr", "hir1"), tag="bc_hir")
    ts(cst[:, NDCOL["hirb"]:NDCOL["hirb"] + 1], PB[7][:, 4:5], 1.0, None,
       ALU.mult, None, reads=("PB7",), writes=("cst_hirb",))
    memset(cst[:, NDCOL["iqt"]:NDCOL["iqt"] + 1], 1.0 / 0.3,
           writes=("cst_iqt",))
    ts(wrens[:, :], wret[:, :], nsret[:, 0:1], None, ALU.mult, None,
       reads=("wre",), writes=("wrens",), sreads=("nsre",))
    ts(wimns[:, :], wimt[:, :], nsimt[:, 0:1], None, ALU.mult, None,
       reads=("wim",), writes=("wimns",), sreads=("nsim",))
    memset(wrebd[:, :], 0.0, writes=("wrebd0",))
    memset(wimbd[:, :], 0.0, writes=("wimbd0",))
    copy(wrebd[0:64, 0:64], wrens[0:64, :], reads=("wrens", "wrebd0"),
         writes=("wrebd1",))
    copy(wrebd[64:128, 64:128], wrens[64:128, :], reads=("wrens", "wrebd1"),
         writes=("wrebd",))
    copy(wimbd[0:64, 0:64], wimns[0:64, :], reads=("wimns", "wimbd0"),
         writes=("wimbd1",))
    copy(wimbd[64:128, 64:128], wimns[64:128, :], reads=("wimns", "wimbd1"),
         writes=("wimbd",))

    def emit_ln(ut, utkeys, T, bmb, brb, phase, need_nm, xt=None, xpre=None):
        if xt is None:
            xt = xlnbf
            xpre = "xln"
        nd = 8
        sx = PB[2][0:1, 0:T]
        sq = PB[3][0:1, 0:T]
        for dc in range(nd):
            sb_ = sqb[dc % 2]
            ub_ = ubb[dc % 2]
            copy(ub_[:, 0:T], ut[:, dc, :], reads=(utkeys[dc],),
                 writes=(f"ubb{dc % 2}",), stream="pool", tag="ln_ub")
            act(sb_[:, 0:T], ut[:, dc, :], AF.Square, reads=(utkeys[dc],),
                writes=(f"sqb{dc % 2}",), tag="ln_sq")
            mm("PB2", sx, onescb[:, :], ub_[:, 0:T], dc == 0, dc == nd - 1,
               reads=("onescb", f"ubb{dc % 2}"), tag="ln_sx")
            mm("PB3", sq, onescb[:, :], sb_[:, 0:T], dc == 0, dc == nd - 1,
               reads=("onescb", f"sqb{dc % 2}"), tag="ln_sq_mm")
        m = rowp[0:1, 0:T]
        nm = rowp[0:1, 512:512 + T]
        va = rowp[0:1, 1024:1024 + T]
        rs_ = rowp[0:1, 1536:1536 + T]
        ts(m, sx, 1.0 / D, None, ALU.mult, None, reads=("PB2",),
           writes=("lnm",), tag="ln_m")
        ts(va, sq, 1.0 / D, None, ALU.mult, None, reads=("PB3",),
           writes=("lnmsq",), tag="ln_msq")
        if need_nm:
            ts(nm, m, -1.0, None, ALU.mult, None, reads=("lnm",),
               writes=("lnnm",), tag="ln_nm")
        def _mm2(eng):
            return eng.tensor_tensor(d1b[0:1, 0:T], m, m, op=ALU.mult)
        S.add("dve", _mm2, reads=("lnm",), writes=("d1row",), tag="ln_m2")
        def _var2(eng):
            return eng.tensor_tensor(va, va, d1b[0:1, 0:T], op=ALU.subtract)
        S.add("dve", _var2, reads=("lnmsq", "d1row"), writes=("lnvar",),
              tag="ln_var")
        act(va, va, AF.Ln, reads=("lnvar",), writes=("lnsd",),
            bias=hostc[0:1, 0:1], sreads=("hostc",), tag="ln_lnv")
        act(rs_, va, AF.Exp, reads=("lnsd",), writes=("lnrstd",),
            scale=cst[0:1, CCOL["nhalf"]:CCOL["nhalf"] + 1],
            sreads=("cst_nhalf",), tag="ln_rstd")
        mm("PB4", bmb[:, 0:T], onesr[:, :], m, True, True,
           reads=("onesr", "lnm"), tag="ln_mbc")
        mm("PB5", brb[:, 0:T], onesr[:, :], rs_, True, True,
           reads=("onesr", "lnrstd"), tag="ln_rbc")
        for dc in range(nd):
            def _sub(eng, dc=dc, T=T):
                return eng.tensor_tensor(d1b[:, 0:T], ut[:, dc, :],
                                         bmb[:, 0:T], op=ALU.subtract)
            S.add("dve", _sub, reads=(utkeys[dc], "PB4"), writes=("d1b",),
                  tag="ln_sub")
            def _mul(eng, dc=dc, T=T, xt=xt):
                return eng.tensor_tensor(xt[:, dc * T:(dc + 1) * T],
                                         d1b[:, 0:T], brb[:, 0:T],
                                         op=ALU.mult)
            S.add("dve", _mul, reads=("d1b", "PB5"),
                  writes=(f"{xpre}{dc}",), tag="ln_mul")

    def emit_ffn(phase, w1k, w2k, w1sb, w2sb, xin, ut, hsb, hgb, ewbs, ewf,
                 qts, ytm, in_dram, out_dram, out_tm, tile_hook=None):
        T = TA
        for it in range(NTA):
            t0 = it * T
            par = it % len(ut)
            utt = ut[par]
            utkeys = [f"ut_{phase}{par}_{dc}" for dc in range(8)]
            if phase == "A":
                for h in range(2):
                    hsrc = x_d[t0 + h * 256:t0 + (h + 1) * 256, :].rearrange(
                        "(s p) d -> p s d", p=128)
                    dma("sync", xin[:, :, :], hsrc, reads=(),
                        writes=("xin0",), tag="ldx")
                    for dc in range(8):
                        bank = PBK[dc]
                        pb = PB[dc]
                        for sl in range(2):
                            s = h * 2 + sl
                            transpose(bank, pb[:, s * 128:(s + 1) * 128],
                                      xin[:, sl, dc * 128:(dc + 1) * 128],
                                      reads=("xin0",), tag="trin")
                        if h == 1:
                            act(utt[:, dc, :], pb[:, 0:T], AF.Identity,
                                reads=(bank,), writes=(utkeys[dc],),
                                tag="ut_ev")
            else:
                src = in_dram.rearrange("(c p) t -> p c t", p=128)[
                    :, :, t0:t0 + T]
                dma("sync", utt[:, :, :], src,
                    reads=tuple(f"u2d_{it}_{dc}" for dc in range(8)),
                    writes=tuple(utkeys), tag="ldu")
            emit_ln(utt, utkeys, T, PB[4], PB[5], phase + str(it), True)
            nsub = T // 128
            rs_row = rowp[0:1, 1536:1536 + T]
            for s in range(nsub):
                transpose("PB6", PB[6][:, s:s + 1],
                          rs_row[0:1, s * 128:(s + 1) * 128],
                          reads=("lnrstd",),
                          ident_ap=identt[0:1, 0:1], tag="tr_rstd")
            copy(rpk[:, 50:50 + nsub], PB[6][:, 0:nsub], reads=("PB6",),
                 writes=("rstd_tm",), tag="rstd_tm")
            if phase == "A" and it == 0:
                for dc in range(8):
                    mm("PB6", PB[6][0:1, 8:14], onesc[:, :],
                       pwtt[:, dc * 6:(dc + 1) * 6], dc == 0, dc == 7,
                       reads=("pwt", "onesc"), tag="srow_mm")
                copy(rpk[0:1, 56:62], PB[6][0:1, 8:14], reads=("PB6",),
                     writes=("srow",), tag="srow")
            nm_row = rowp[0:1, 512:512 + T]
            for s in range(nsub):
                zr = PB[2][:, 0:6] if s % 2 == 0 else PB[3][:, 0:6]
                zrk = "PB2" if s % 2 == 0 else "PB3"
                for dc in range(8):
                    mm(zrk, zr, utt[:, dc, s * 128:(s + 1) * 128],
                       pwtt[:, dc * 6:(dc + 1) * 6], dc == 0, False,
                       reads=(utkeys[dc], "pwt"), tag="zu")
                mm(zrk, zr, nm_row[0:1, s * 128:(s + 1) * 128],
                   rpk[0:1, 56:62], False, True,
                   reads=("lnnm", "srow"), tag="zfix")
                zt = rpk[:, 38:44]
                ts(zt, zr, rpk[:, 50 + s:51 + s], None, ALU.mult, None,
                   reads=(zrk,), writes=("zt",), sreads=("rstd_tm",),
                   tag="zts")
                qv = rpk[:, 44:50]
                act(qv, zt, AF.Tanh, reads=("zt",), writes=("qv",),
                    scale=cc_ap("iqt"), sreads=("cst_iqt",), tag="qtanh")
                transpose("PB7", PB[7][0:6, 0:128], qv, reads=("qv",),
                          tag="qtr")
                copy(qts[0:6, 0:128], PB[7][0:6, 0:128], reads=("PB7",),
                     writes=("qTs",), tag="qTs")
                mm("PB7", PB[7][:, 128:134], qts[0:6, 0:128],
                   ancht[:, :], True, True, reads=("qTs", "anch"), tag="dot")
                lg = rpk[:, 8:14]
                ts(lg, PB[7][:, 128:134], -6.0, cc_ap("hirb"), ALU.add,
                   ALU.mult, reads=("PB7",), writes=("lg",),
                   sreads=("cst_hirb",), tag="lgts")
                m1 = rpk[:, 0:1]
                reduce(m1, lg, ALU.max, reads=("lg",), writes=("m1",))
                mask1 = rpk[:, 14:20]
                ts(mask1, lg, m1, None, ALU.is_ge, None, reads=("lg",),
                   writes=("mask1",), sreads=("m1",), tag="mask1")
                l2 = rpk[:, 20:26]
                stt(l2, mask1, -1e30, lg, ALU.mult, ALU.add,
                    reads=("mask1", "lg"), writes=("l2",), tag="l2")
                m2 = rpk[:, 1:2]
                reduce(m2, l2, ALU.max, reads=("l2",), writes=("m2",))
                mask2 = rpk[:, 26:32]
                ts(mask2, l2, m2, None, ALU.is_ge, None, reads=("l2",),
                   writes=("mask2",), sreads=("m2",), tag="mask2")
                d21 = rpk[:, 2:3]
                ts(d21, m2, m1, None, ALU.subtract, None, reads=("m2",),
                   writes=("d21",), sreads=("m1",), tag="d21")
                w1s = rpk[:, 5:6]
                act(w1s, d21, AF.Sigmoid, reads=("d21",), writes=("w1s",),
                    scale=cst[:, NEG1_COL:NEG1_COL + 1],
                    sreads=("cst_neg1",), tag="w1s")
                w2s = rpk[:, 6:7]
                act(w2s, d21, AF.Sigmoid, reads=("d21",), writes=("w2s",),
                    tag="w2s")
                ew1 = rpk[:, 32:38]
                ts(ew1, mask1, w1s, None, ALU.mult, None, reads=("mask1",),
                   writes=("ew1",), sreads=("w1s",), tag="ew1")
                stt(ew1, mask2, w2s, ew1, ALU.mult, ALU.add,
                    reads=("mask2", "ew1"), writes=("ew2",), sreads=("w2s",),
                    tag="ew2")
                if phase == "A" and it == 1 and s == 0 and dbg_rpk is not None:
                    dma("sync", dbg_rpk[:, :], rpk[:, :],
                        reads=("ew2", "lg", "m1", "m2", "mask1", "mask2",
                               "zt", "qv", "d21", "e21", "w1s", "w2s",
                               "rstd_tm"),
                        writes=("dbgr",), tag="dbg")
                    dma("sync", dbg_cst[:, :], cst[:, :],
                        reads=tuple("cst_" + k for k in
                                    ["w0b", "w1b", "nw2b", "hirb", "iqt"]),
                        writes=("dbgc",), tag="dbg")
                    dma("sync", dbg_sm[0:1, 0:3], wsm[:, :], reads=("wsm",),
                        writes=("dbgs1",), tag="dbg")
                    dma("sync", dbg_sm[0:1, 4:12], dsm[:, :],
                        reads=("dsm_max", "dsm_sum", "dsm_rec"),
                        writes=("dbgs2",), tag="dbg")
                    dma("sync", dbg_sm[0:1, 12:16], rts[:, :],
                        reads=("rtc", "rtrec", "hir1"),
                        writes=("dbgs3",), tag="dbg")
                for e_ in range(NE):
                    transpose("PB7", PB[7][0:1, 384:512], ew1[:, e_:e_ + 1],
                              reads=("ew2",), tag="ewtr")
                    copy(ewf[0:1, e_ * T + s * 128:e_ * T + (s + 1) * 128],
                         PB[7][0:1, 384:512], reads=("PB7",),
                         writes=(f"ewf{e_}",), tag="ewf")
            for e_ in range(NE):
                ewb = PB[6]
                mm("PB6", ewb[:, 0:T], onesrb[:, :],
                   ewf[0:1, e_ * T:(e_ + 1) * T], True, True,
                   reads=("onesrb", f"ewf{e_}"), tag="ewb")
                ewsb = ewbs[e_ % 2]
                act(ewsb[:, 0:T], ewb[:, 0:T], AF.Identity, reads=("PB6",),
                    writes=(f"ewsb{e_ % 2}",), tag="ewb_ev")
                for hc in range(4):
                    hb = PBK[hc % 2]
                    hpb = PB[hc % 2]
                    for dc in range(8):
                        lhs = w1sb[:, (e_ * 8 + dc) * 512 + hc * 128:
                                   (e_ * 8 + dc) * 512 + hc * 128 + 128]
                        mm(hb, hpb[:, 0:T], lhs,
                           xlnbf[:, dc * T:(dc + 1) * T], dc == 0, dc == 7,
                           reads=(w1k[e_], f"xln{dc}"), tag="w1mm")
                    hgv = hgb[hc % 2]
                    act(hgv[:, 0:T], hpb[:, 0:T], AF.Gelu, reads=(hb,),
                        writes=(f"hg{hc % 2}",), tag="gelu")
                    def _hs(eng, e_=e_, hc=hc, T=T, hgv=hgv, ewsb=ewsb):
                        return eng.tensor_tensor(
                            hsb[:, (e_ * 4 + hc) * T:(e_ * 4 + hc + 1) * T],
                            hgv[:, 0:T], ewsb[:, 0:T], op=ALU.mult)
                    S.add("pool", _hs, reads=(f"hg{hc % 2}", f"ewsb{e_ % 2}"),
                          writes=(f"hs{e_}_{hc}",), tag="hscale")
            if phase == "A" and it == 1 and dbg_ut is not None:
                dma("sync", dbg_ut[:, :, :], utt[:, :, :],
                    reads=tuple(utkeys), writes=("dbg1",), tag="dbg")
                dma("pool", dbg_xln[:, :], xlnbf[:, 0:8 * T],
                    reads=tuple(f"xln{d_}" for d_ in range(8)),
                    writes=("dbg2",), tag="dbg")
                dma("pool", dbg_ewf[:, :], ewf[0:1, :],
                    reads=tuple(f"ewf{e_}" for e_ in range(NE)),
                    writes=("dbg3",), tag="dbg")
                dma("pool", dbg_hsb[:, :], hsb[:, :],
                    reads=tuple(f"hs{e_}_{h_}" for e_ in range(NE)
                                for h_ in range(4)),
                    writes=("dbg4",), tag="dbg")
            for dc in range(8):
                ob = PBK[4 + (dc % 2)]
                opb = PB[4 + (dc % 2)]
                first = True
                for e_ in range(NE):
                    for hc in range(4):
                        lhs = w2sb[:, (e_ * 4 + hc) * 1024 + dc * 128:
                                   (e_ * 4 + hc) * 1024 + dc * 128 + 128]
                        mm(ob, opb[:, 0:T], lhs,
                           hsb[:, (e_ * 4 + hc) * T:(e_ * 4 + hc + 1) * T],
                           first, (e_ == NE - 1) and (hc == 3),
                           reads=(w2k[e_], f"hs{e_}_{hc}"), tag="w2mm")
                        first = False
                eb = ebuf[dc % 2]
                stt(eb[:, 0:T], opb[:, 0:T], 0.5, utt[:, dc, :], ALU.mult,
                    ALU.add, reads=(ob, utkeys[dc]), writes=(f"ebuf{dc % 2}",),
                    sreads=(), tag="res_ev")
                if not out_tm:
                    dst = out_dram[dc * 128:(dc + 1) * 128, t0:t0 + T]
                    dma("act", dst, eb[:, 0:T], reads=(f"ebuf{dc % 2}",),
                        writes=(f"u1d_{it}_{dc}",), tag="st_u1")
                    if dc == 7 and tile_hook is not None:
                        tile_hook(it)
                else:
                    for s in range(T // 128):
                        bank = PBK[6 + (s % 2)]
                        pb = PB[6 + (s % 2)]
                        transpose(bank, pb[:, 0:128],
                                  eb[:, s * 128:(s + 1) * 128],
                                  reads=(f"ebuf{dc % 2}",), tag="trout")
                        yp = ytm[s % 2]
                        act(yp[:, 0:128], pb[:, 0:128],
                            AF.Identity, reads=(bank,),
                            writes=(f"yp{s % 2}",), tag="ytm_ev")
                        dst = y_d[t0 + s * 128:t0 + (s + 1) * 128,
                                  dc * 128:(dc + 1) * 128]
                        dma("sync", dst, yp[:, 0:128],
                            reads=(f"yp{s % 2}",),
                            writes=(f"yd_{it}_{dc}_{s}",), tag="st_y")

    st_a = ExitStack()
    w1sb = sb("w1sb", [128, NE * 8 * 512], BF16, st_a)
    w2sb = sb("w2sb", [128, NE * 4 * 1024], BF16, st_a)
    xin = sb("xin0t", [128, 2, D], F32, st_a)
    utA = [sb("utA0", [128, 8, TA], F32, st_a)]
    hsb = sb("hsb", [128, NE * 4 * TA], BF16, st_a)
    hgb = [sb(f"hgb{i}", [128, TA], BF16, st_a) for i in range(2)]
    ewbsA = [sb(f"ewbs{i}", [128, TA], BF16, st_a) for i in range(2)]
    ewfA = sb("ewfA", [1, NE * TA], BF16, st_a)
    qtsA = sb("qtsA", [6, 128], F32, st_a)
    ipstg = sb("ipstg", [128, 4096], BF16, st_a)

    w1keys = [f"w1_{e}" for e in range(NE)]
    w2keys = [f"w2_{e}" for e in range(NE)]
    for e in range(NE):
        src = w1_d[e].rearrange("(dc p) h -> p dc h", p=128)
        dst = w1sb[:, e * 4096:(e + 1) * 4096].rearrange(
            "p (dc h) -> p dc h", dc=8)
        dma("pool", dst, src, reads=(), writes=(w1keys[e],), tag="ldw1")
    for e in range(NE):
        src = w2_d[e].rearrange("(hc p) d -> p hc d", p=128)
        dst = w2sb[:, e * 4096:(e + 1) * 4096].rearrange(
            "p (hc d) -> p hc d", hc=4)
        dma("pool", dst, src, reads=(), writes=(w2keys[e],), tag="ldw2")
    def stage_hook(it):
        for g in (2 * it, 2 * it + 1):
            if g < 8:
                gsrc = ipw_d[:, g * 512:(g + 1) * 512].rearrange(
                    "(dc p) c -> p dc c", p=128)
                dma("pool", ipstg[:, :].rearrange("p (dc c) -> p dc c", dc=8),
                    gsrc, reads=(), writes=("ipstg",), tag="ipcast")
                dma("sync", ipbf_d[g], ipstg[:, :], reads=("ipstg",),
                    writes=(f"ipbf{g}",), tag="ipstore")
        q = it - 4
        if 0 <= q < 4:
            qsrc = opw_d.rearrange("(kc p) d -> p kc d", p=128)[
                :, 4 * q:4 * q + 4, :]
            dma("pool", ipstg[:, :].rearrange("p (kc d) -> p kc d", kc=4),
                qsrc, reads=(), writes=("ipstg",), tag="opcast")
            for dcq in range(8):
                ssrc = ipstg[:, :].rearrange("p (kc d) -> p kc d", kc=4)[
                    :, :, dcq * 128:(dcq + 1) * 128]
                sdst = opbf_d[dcq][:, q * 512:(q + 1) * 512].rearrange(
                    "p (kc w) -> p kc w", kc=4)
                dma("sync", sdst, ssrc, reads=("ipstg",),
                    writes=(f"opbf{dcq}_{q}",), tag="opstore")

    emit_ffn("A", w1keys, w2keys, w1sb, w2sb, xin, utA, hsb, hgb, ewbsA,
             ewfA, qtsA, None, None, u1_d, False, tile_hook=stage_hook)
    st_a.close()
    S.barrier()

    st_b = ExitStack()
    ipb = [sb(f"ipb{i}", [128, 8 * 512], BF16, st_b) for i in range(3)]
    ipsr = sb("ipsr", [128, 8 * 32], BF16, st_b)
    opd = [sb(f"opd{i}", [128, 2048], BF16, st_b) for i in range(2)]
    xlnb2 = sb("xlnb2", [128, 4096], BF16, st_b)
    utB = [sb(f"utB{i}", [128, 8, TB], F32, st_b) for i in range(2)]
    zb = [sb(f"zb{i}", [128, TB + 8], F32, st_b) for i in range(3)]
    cvA = sb("cvA", [128, TB], F32, st_b)
    cvB = sb("cvB", [128, TB], F32, st_b)
    xvb = [sb(f"xvb{i}", [128, TB], F32, st_b) for i in range(2)]
    ph3b = sb("ph3b", [128, TB], F32, st_b)
    mib = sb("mib", [128, TB], I32, st_b)
    r2b = sb("r2b", [128, TB], F32, st_b)
    sinb2 = [sb(f"sinb{i}", [128, TB], F32, st_b) for i in range(2)]
    cosb2 = [sb(f"cosb{i}", [128, TB], F32, st_b) for i in range(2)]
    sqvb = [sb("sqvb0", [128, TB], F32, st_b)] * 2
    t1b = sb("t1b", [128, TB], F32, st_b)
    polyb = sb("polyb", [128, TB], F32, st_b)
    ppb = [sb(f"ppb{i}", [128, TB], F32, st_b) for i in range(2)]
    areb = [sb(f"areb{i}", [128, TB], F32, st_b) for i in range(2)]
    aimb = [sb(f"aimb{i}", [128, TB], F32, st_b) for i in range(2)]
    nreb = [sb(f"nreb{i}", [128, TB], F32, st_b) for i in range(2)]
    nimb = [sb(f"nimb{i}", [128, TB], F32, st_b) for i in range(2)]
    nbfre = [sb(f"nbfre{i}", [128, TB], BF16, st_b) for i in range(2)]
    nbfim = [sb(f"nbfim{i}", [128, TB], BF16, st_b) for i in range(2)]
    sqbf = [sb(f"sqbf{i}", [128, TB], BF16, st_b) for i in range(4)]
    yvb = sb("yvb", [128, NVC * TB], BF16, st_b)
    gateb = [sb(f"gateb{i}", [128, TB], BF16, st_b) for i in range(2)]
    spk = sb("spk", [32, 4 * TB], F32, st_b)
    twtb = sb("twtb", [32, TB], F32, st_b)
    dsl = sb("dsl", [32, TB], F32, st_b)
    svalb = sb("svalb", [32, TB], F32, st_b)
    pwbb = sb("pwbb", [32, TB], BF16, st_b)
    svbb = sb("svbb", [32, TB], BF16, st_b)
    ysc = polyb

    def ld_op(dcq):
        dma("sync", opd[dcq % 2][:, :], opbf_d[dcq],
            reads=tuple(f"opbf{dcq}_{q}" for q in range(4)),
            writes=(f"opd{dcq % 2}",), tag="ldop")
    srsrc = ipw_d[:, 4096:4128].rearrange("(dc p) c -> p dc c", p=128)
    dma("pool", ipsr[:, :].rearrange("p (dc c) -> p dc c", dc=8), srsrc,
        reads=(), writes=("ipsr",), tag="ldipsr")

    def ip_lhs(gpar, dcc, cc_in_g):
        return ipb[gpar][:, dcc * 512 + cc_in_g * 128:
                         dcc * 512 + cc_in_g * 128 + 128]

    def ld_ipg(gidx):
        g = gidx % 8
        dma("sync", ipb[gidx % 3][:, :], ipbf_d[g], reads=(f"ipbf{g}",),
            writes=(f"ipg{gidx % 3}",), tag="ldipg")

    xlnbufs = [xlnbf, xlnb2]

    def ld_ut_b(it):
        par = it % 2
        utt = utB[par]
        utkeys = [f"utB{par}_{dc}" for dc in range(8)]
        src = u1_d.rearrange("(c p) t -> p c t", p=128)[
            :, :, it * TB:(it + 1) * TB]
        dma("sync", utt[:, :, :], src,
            reads=tuple(f"u1d_{it}_{dc}" for dc in range(8)),
            writes=tuple(utkeys), tag="ldu1")

    def ln_b(it):
        par = it % 2
        emit_ln(utB[par], [f"utB{par}_{dc}" for dc in range(8)], TB,
                PB[4], PB[5], "B" + str(it), False,
                xt=xlnbufs[par], xpre=f"xlnB{par}_")

    ld_ut_b(0)
    ln_b(0)
    for it in range(NTB):
        t0 = it * TB
        par = it % 2
        utt = utB[par]
        utkeys = [f"utB{par}_{dc}" for dc in range(8)]
        xlt = xlnbufs[par]
        xlk = [f"xlnB{par}_{dc}" for dc in range(8)]
        if it == 0:
            for g in range(3):
                ld_ipg(g)
        dma("sync", dsl[:, :], dist_d[:, t0:t0 + TB], reads=(),
            writes=("dsl",), tag="lddist")
        act(twtb[:, :], dsl[:, :], AF.Exp, reads=("dsl",), writes=("twt",),
            scale=negslope[:, 0:1], sreads=("negslope",), tag="tw")

        zrot = [0]
        ZB3 = (0, 1, 7)

        def chunk_z(cc, outbuf, outkey, tag):
            gpar = (it * 8 + cc // 4) % 3
            gkey = f"ipg{gpar}"
            zbi = ZB3[zrot[0] % 3]
            bank = PBK[zbi]
            pb = PB[zbi]
            wk = 128 if cc < 32 else 32
            for dc in range(8):
                if cc < 32:
                    lhs = ip_lhs(gpar, dc, cc % 4)
                    rk = gkey
                else:
                    lhs = ipsr[:, dc * 32:(dc + 1) * 32]
                    rk = "ipsr"
                mm(bank, pb[0:wk, 0:TB], lhs,
                   xlt[:, dc * TB:(dc + 1) * TB], dc == 0, dc == 7,
                   reads=(rk, xlk[dc]), tag="ipmm")
            zr = zb[zrot[0] % 3]
            zrk = f"zb{zrot[0] % 3}"
            zrot[0] += 1
            act(zr[0:wk, 3:3 + TB], pb[0:wk, 0:TB], AF.Identity,
                reads=(bank,), writes=(zrk,), tag="zev")
            def _ti(eng, zr=zr, cc=cc, wk=wk):
                return eng.tensor_copy(zr[0:wk, 0:3],
                                       tails[0:wk, cc * 3:cc * 3 + 3])
            S.add("pool", _ti, reads=(f"tail{cc}",), writes=(zrk + "_t",),
                  tag="ti")
            def _tu(eng, zr=zr, cc=cc, wk=wk):
                return eng.tensor_copy(tails[0:wk, cc * 3:cc * 3 + 3],
                                       zr[0:wk, TB:TB + 3])
            S.add("pool", _tu, reads=(zrk,), writes=(f"tail{cc}",), tag="tu")
            k3 = ckt[0:wk, cc * 4 + 3:cc * 4 + 4]
            cbv = cbt[0:wk, cc:cc + 1]
            act(cvA[0:wk, :], zr[0:wk, 3:3 + TB], AF.Identity,
                reads=(zrk,), writes=("cvA",), scale=k3, bias=cbv,
                sreads=("ck", "cb"), tag="cv1")
            stt(cvB[0:wk, :], zr[0:wk, 2:2 + TB],
                ckt[0:wk, cc * 4 + 2:cc * 4 + 3], cvA[0:wk, :], ALU.mult,
                ALU.add, reads=(zrk, "cvA", zrk + "_t"), writes=("cvB",),
                sreads=("ck",), tag="cv2")
            stt(cvA[0:wk, :], zr[0:wk, 1:1 + TB],
                ckt[0:wk, cc * 4 + 1:cc * 4 + 2], cvB[0:wk, :], ALU.mult,
                ALU.add, reads=(zrk, zrk + "_t", "cvB"), writes=("cvA",),
                sreads=("ck",), tag="cv3")
            stt(outbuf, zr[0:wk, 0:TB], ckt[0:wk, cc * 4:cc * 4 + 1],
                cvA[0:wk, :], ALU.mult, ALU.add,
                reads=(zrk, zrk + "_t", "cvA"), writes=(outkey,),
                sreads=("ck",), tag=tag)

        zs = spk[:, 0:TB]
        chunk_z(32, zs[0:32, :], "zs", "cv_s")
        pwv = spk[:, TB:2 * TB]
        ts(pwv, zs, scoret[:, 0:1], 20.0, ALU.mult, ALU.min, reads=("zs",),
           writes=("pws1",), sreads=("score",), tag="score")
        ts(pwv, pwv, -20.0, None, ALU.max, None, reads=("pws1",),
           writes=("pws2",), tag="clip")
        act(pwv, pwv, AF.Exp, reads=("pws2",), writes=("pexp",), tag="pexp")
        tt(pwbb[:, :], pwv, twtb[:, :], ALU.mult, reads=("pexp", "twt"),
           writes=("pw",), tag="pw")
        denv = spk[:, 2 * TB:3 * TB]
        scan(denv, ones_scan[0:32, :], pwbb[:, :], carry_den[:, 0:1],
             reads=("ones_scan", "pw"), writes=("den",), sreads=("cden",),
             tag="denscan")
        copy(carry_den[:, 0:1], denv[:, TB - 1:TB], reads=("den",),
             writes=("cden",), tag="dencarry")
        ts(denv, denv, 1e-4, None, ALU.max, None, reads=("den",),
           writes=("denc",), tag="denc")

        def val_m0(cc):
            p2 = cc % 2
            chunk_z(cc, xvb[p2][:, :], f"xv{p2}", "cv_xv")
            if cc % 4 == 3 and it * 8 + cc // 4 + 3 < NTB * 8:
                ld_ipg(it * 8 + cc // 4 + 3)
            mm(PBK[2 + p2], PB[2 + p2][:, 0:TB],
               selt[:, cc * 128:(cc + 1) * 128], pwbb[:, :], True, True,
               reads=("sel", "pw"), tag="pwx")

        def val_m1(cc):
            p2 = cc % 2
            xvv = xvb[p2]
            pxb = PBK[2 + p2]
            pxp = PB[2 + p2]
            act(ph3b[:, :], xvv[:, :], AF.Identity, reads=(f"xv{p2}",),
                writes=("ph3",), scale=theta2[:, cc:cc + 1],
                bias=cc_ap("c256"), sreads=("theta2", "cst_c256"), tag="ph3")
            act(mib[:, :], ph3b[:, :], AF.Identity, reads=("ph3",),
                writes=("mi",), tag="mi")
            stt(r2b[:, :], mib[:, :], -1.0, ph3b[:, :], ALU.mult, ALU.add,
                reads=("mi", "ph3"), writes=("r2",), tag="r2")
            act(sinb2[p2][:, :], r2b[:, :], AF.Sin, reads=("r2",),
                writes=(f"sin{p2}",), scale=cc_ap("n2pi"),
                sreads=("cst_n2pi",), tag="sin")
            act(ph3b[:, :], r2b[:, :], AF.Abs, reads=("r2",),
                writes=("absr", "ph3"), scale=cc_ap("p2pi"),
                sreads=("cst_p2pi",), tag="absr")
            act(cosb2[p2][:, :], ph3b[:, :], AF.Sin, reads=("absr",),
                writes=(f"cos{p2}",), bias=cc_ap("npih"),
                sreads=("cst_npih",), tag="cos")
            sqv = sqvb[p2]
            act(sqv[:, :], xvv[:, :], AF.Square, reads=(f"xv{p2}",),
                writes=("sqv",), tag="sqv")
            act(t1b[:, :], xvv[:, :], AF.Identity, reads=(f"xv{p2}",),
                writes=("t1",), scale=cc_ap("w1b"), bias=cc_ap("w0b"),
                sreads=("cst_w1b", "cst_w0b"), tag="t1")
            stt(polyb[:, :], sqv[:, :], cc_ap("nw2b"), t1b[:, :], ALU.mult,
                ALU.add, reads=("sqv", "t1"), writes=("poly",),
                sreads=("cst_nw2b",), tag="poly")
            ppv = ppb[p2]
            tt(ppv[:, :], polyb[:, :], pxp[:, 0:TB], ALU.mult,
               reads=("poly", pxb), writes=(f"pp{p2}",), tag="pp")
            tt(areb[p2][:, :], ppv[:, :], cosb2[p2][:, :], ALU.mult,
               reads=(f"pp{p2}", f"cos{p2}"), writes=(f"are{p2}",),
               stream="pool", tag="are")
            tt(aimb[p2][:, :], ppv[:, :], sinb2[p2][:, :], ALU.mult,
               reads=(f"pp{p2}", f"sin{p2}"), writes=(f"aim{p2}",),
               stream="pool", tag="aim")

        def val_m2a(cc):
            p2 = cc % 2
            nre = nreb[p2]
            nim = nimb[p2]
            scan(nre[:, :], ones_scan[:, :], areb[p2][:, :],
                 carry_re[:, cc:cc + 1], reads=("ones_scan", f"are{p2}"),
                 writes=(f"nre{p2}",), sreads=(f"cre{cc}",), tag="scan_re")
            copy(carry_re[:, cc:cc + 1], nre[:, TB - 1:TB],
                 reads=(f"nre{p2}",), writes=(f"cre{cc}",), stream="pool",
                 tag="cre")
            scan(nim[:, :], ones_scan[:, :], aimb[p2][:, :],
                 carry_im[:, cc:cc + 1], reads=("ones_scan", f"aim{p2}"),
                 writes=(f"nim{p2}",), sreads=(f"cim{cc}",), tag="scan_im")
            copy(carry_im[:, cc:cc + 1], nim[:, TB - 1:TB],
                 reads=(f"nim{p2}",), writes=(f"cim{cc}",), stream="pool",
                 tag="cim")

        def val_m2b(cc):
            p2 = cc % 2
            nre = nreb[p2]
            nim = nimb[p2]
            copy(nbfre[p2][:, :], nre[:, :], reads=(f"nre{p2}",),
                 writes=(f"nbfre{p2}",), tag="nbfre")
            copy(nbfim[p2][:, :], nim[:, :], reads=(f"nim{p2}",),
                 writes=(f"nbfim{p2}",), stream="pool", tag="nbfim")
            tt(sqbf[p2][:, :], nbfre[p2][:, :], nbfre[p2][:, :], ALU.mult,
               reads=(f"nbfre{p2}",), writes=(f"sqbf{p2}",), tag="sqre")
            tt(sqbf[2 + p2][:, :], nbfim[p2][:, :], nbfim[p2][:, :], ALU.mult,
               reads=(f"nbfim{p2}",), writes=(f"sqbf{2 + p2}",), tag="sqim")

        def val_m2c(cc):
            p2 = cc % 2
            mm("PB6", PB[6][0:32, 0:TB], sel2t[:, cc * 32:(cc + 1) * 32],
               sqbf[p2][:, :], cc == 0, False, reads=("sel2", f"sqbf{p2}"),
               tag="ssq_re")
            mm("PB6", PB[6][0:32, 0:TB], sel2t[:, cc * 32:(cc + 1) * 32],
               sqbf[2 + p2][:, :], False, cc == NVC - 1,
               reads=("sel2", f"sqbf{2 + p2}"), tag="ssq_im")
            yb_ = PBK[4 + p2]
            ypb = PB[4 + p2]
            mm(yb_, ypb[:, 0:TB], wrebd[:, :], nbfre[p2][:, :],
               True, False, reads=("wrebd", f"nbfre{p2}"), tag="ymm")
            mm(yb_, ypb[:, 0:TB], wimbd[:, :], nbfim[p2][:, :],
               False, True, reads=("wimbd", f"nbfim{p2}"), tag="ymm")
            act(yvb[:, cc * TB:(cc + 1) * TB], ypb[:, 0:TB], AF.Identity,
                reads=(yb_,), writes=(f"yv{cc}",), tag="yv")

        for tstep in range(NVC + 3):
            if tstep < NVC:
                val_m0(tstep)
            if 1 <= tstep <= NVC:
                val_m1(tstep - 1)
            if 2 <= tstep <= NVC + 1:
                val_m2a(tstep - 2)
                val_m2b(tstep - 2)
            if tstep >= 3:
                val_m2c(tstep - 3)

        if it == 1 and dbg_b is not None:
            dma("sync", dbg_b[:, 0:TB], xvb[1][:, :], reads=("xv1",),
                writes=("dbgb0",), tag="dbg")
            dma("sync", dbg_b[:, TB:2 * TB], nreb[1][:, :], reads=("nre1",),
                writes=("dbgb1",), tag="dbg")
            dma("sync", dbg_b[:, 2 * TB:3 * TB], nimb[1][:, :], reads=("nim1",),
                writes=("dbgb2",), tag="dbg")
            dma("pool", dbg_b[:, 3 * TB:4 * TB], xlnbf[:, 0:TB],
                reads=("xln0",), writes=("dbgb3",), tag="dbg")
            dma("sync", dbg_b[:, 4 * TB:5 * TB], sinb2[1][:, :], reads=("sin1",),
                writes=("dbgb4",), tag="dbg")
            dma("sync", dbg_b[:, 5 * TB:6 * TB], ppb[1][:, :], reads=("pp1",),
                writes=("dbgb5",), tag="dbg")
            dma("sync", dbg_bs[:, 0:TB], spk[:, TB:2 * TB], reads=("pw",),
                writes=("dbgs0",), tag="dbg")
            dma("sync", dbg_bs[:, TB:2 * TB], spk[:, 2 * TB:3 * TB],
                reads=("denc",), writes=("dbgs1",), tag="dbg")
        if it + 1 < NTB:
            ld_ut_b(it + 1)
            ln_b(it + 1)
        sv = spk[:, 3 * TB:4 * TB]
        tmp = spk[:, TB:2 * TB]
        ts(tmp, PB[6][0:32, 0:TB], 1.0 / 128.0, None, ALU.mult, None,
           reads=("PB6",), writes=("ssq_s", "pexp"), tag="ssq_s")
        act(sv, denv, AF.Square, reads=("denc",), writes=("dc2",), tag="dc2")
        stt(spk[:, 0:TB], sv, 1e-5, tmp, ALU.mult, ALU.add,
            reads=("dc2", "ssq_s"), writes=("inner",), tag="inner")
        act(sv, spk[:, 0:TB], AF.Ln, reads=("inner",), writes=("rt2",),
            tag="rt2")
        act(svbb[:, :], sv, AF.Exp, reads=("rt2",), writes=("sval",),
            scale=cst[0:32, CCOL["nhalf"]:CCOL["nhalf"] + 1],
            sreads=("cst_nhalf",), tag="sval")

        if it == 1 and dbg_b2 is not None:
            dma("pool", dbg_b2[:, 0:TB], yvb[:, 15 * TB:16 * TB],
                reads=("yv15",), writes=("dbgy0",), tag="dbg")
            dma("pool", dbg_b2[:, TB:2 * TB], sqbf[0][:, :],
                reads=("sqbf0",), writes=("dbgy1",), tag="dbg")
            dma("sync", dbg_bs[:, 2 * TB:3 * TB], svalb[:, :],
                reads=("sval",), writes=("dbgy2",), tag="dbg")
        def gate_g0(g):
            cc = 16 + g
            p2 = g % 2
            chunk_z(cc, xvb[p2][:, :], f"xv{p2}", "cv_g")
            if cc % 4 == 3 and it * 8 + cc // 4 + 3 < NTB * 8:
                ld_ipg(it * 8 + cc // 4 + 3)

        def gate_g1(g):
            p2 = g % 2
            gv = gateb[p2]
            act(gv[:, :], xvb[p2][:, :], AF.Silu, reads=(f"xv{p2}",),
                writes=(f"gate{p2}",), tag="silu")
            sxb = PBK[2 + p2]
            sxp = PB[2 + p2]
            mm(sxb, sxp[:, 0:TB], selt[:, g * 128:(g + 1) * 128],
               svbb[:, :], True, True, reads=("sel", "sval"), tag="sexp")
            tt(ysc[:, :], yvb[:, g * TB:(g + 1) * TB], sxp[:, 0:TB],
               ALU.mult, reads=(f"yv{g}", sxb), writes=("ysc",), tag="yscale")
            tt(yvb[:, g * TB:(g + 1) * TB], ysc[:, :], gv[:, :], ALU.mult,
               reads=("ysc", f"gate{p2}"), writes=(f"yv{g}",), tag="ygate")

        for tstep in range(NGC + 1):
            if tstep < NGC:
                gate_g0(tstep)
            if tstep >= 1:
                gate_g1(tstep - 1)

        ld_op(0)
        ld_op(1)
        for dc in range(8):
            ob = PBK[dc % 2]
            opb = PB[dc % 2]
            for kc in range(16):
                lhs = opd[dc % 2][:, kc * 128:(kc + 1) * 128]
                mm(ob, opb[:, 0:TB], lhs, yvb[:, kc * TB:(kc + 1) * TB],
                   kc == 0, kc == 15, reads=(f"opd{dc % 2}", f"yv{kc}"),
                   tag="opmm")
            if dc + 2 < 8:
                ld_op(dc + 2)
            eb = ebuf[dc % 2]
            def _res(eng, opb=opb, eb=eb, utt=utt, dc=dc):
                return eng.tensor_tensor(eb[:, :], opb[:, 0:TB],
                                         utt[:, dc, :], op=ALU.add)
            S.add("dve", _res, reads=(ob, utkeys[dc]),
                  writes=(f"ebuf{dc % 2}",), tag="res_b")
            dst = u2_d[dc * 128:(dc + 1) * 128, t0:t0 + TB]
            dma("act", dst, eb[:, :], reads=(f"ebuf{dc % 2}",),
                writes=(f"u2d_{it}_{dc}",), tag="st_u2")

    st_b.close()
    S.barrier()

    st_c = ExitStack()
    w1sbC = sb("w1sbC", [128, NE * 8 * 512], BF16, st_c)
    w2sbC = sb("w2sbC", [128, NE * 4 * 1024], BF16, st_c)
    utC = [sb(f"utC{i}", [128, 8, TA], F32, st_c) for i in range(2)]
    hsbC = sb("hsbC", [128, NE * 4 * TA], BF16, st_c)
    hgbC = [sb(f"hgbC{i}", [128, TA], BF16, st_c) for i in range(2)]
    ewbsC = [sb(f"ewbsC{i}", [128, TA], BF16, st_c) for i in range(2)]
    ewfC = sb("ewfC", [1, NE * TA], BF16, st_c)
    qtsC = sb("qtsC", [6, 128], F32, st_c)
    ytmC = [sb(f"ytmC{i}", [128, 128], F32, st_c) for i in range(2)]

    w1keysC = [f"w1C_{e}" for e in range(NE)]
    w2keysC = [f"w2C_{e}" for e in range(NE)]
    for e in range(NE):
        src = w1_d[e].rearrange("(dc p) h -> p dc h", p=128)
        dst = w1sbC[:, e * 4096:(e + 1) * 4096].rearrange(
            "p (dc h) -> p dc h", dc=8)
        dma("pool", dst, src, reads=(), writes=(w1keysC[e],), tag="ldw1C")
    for e in range(NE):
        src = w2_d[e].rearrange("(hc p) d -> p hc d", p=128)
        dst = w2sbC[:, e * 4096:(e + 1) * 4096].rearrange(
            "p (hc d) -> p hc d", hc=4)
        dma("pool", dst, src, reads=(), writes=(w2keysC[e],), tag="ldw2C")

    emit_ffn("C", w1keysC, w2keysC, w1sbC, w2sbC, None, utC, hsbC, hgbC,
             ewbsC, ewfC, qtsC, ytmC, u2_d, None, True)
    st_c.close()

    S.finalize()
    doms = sorted(S.totals.keys() | {"pe", "dve", "act", "pool"})
    sem_cm = {}
    for d_ in doms:
        sems[d_] = st_p.enter_context(nc.semaphore("sem_" + d_))
    with st_p, nc.Block() as block:
        @block.tensor
        def _(eng):
            S.emit_stream("pe", eng, sems)

        @block.vector
        def _(eng):
            S.emit_stream("dve", eng, sems)

        @block.scalar
        def _(eng):
            S.emit_stream("act", eng, sems)

        @block.gpsimd
        def _(eng):
            S.emit_stream("pool", eng, sems)

        @block.sync
        def _(eng):
            S.emit_stream("sync", eng, sems)
            for d_ in doms:
                tot = S.totals.get(d_, 0)
                if tot:
                    eng.wait_ge(sems[d_], tot * (16 if d_.startswith("q")
                                                 else 1))
    return nc


_PROG_CACHE = {}


def _host_prep(inputs):
    ns = np.asarray(inputs["L0_attn_norm_scale"], np.float32)
    theta = np.asarray(inputs["L0_attn_theta"], np.float32).reshape(2048)
    conv_k = np.asarray(inputs["L0_attn_conv_k"], np.float32)
    conv_b = np.asarray(inputs["L0_attn_conv_b"], np.float32)
    proj_w = np.asarray(inputs["L0_ffn_proj_w"], np.float32)

    theta_t = np.ascontiguousarray(theta.reshape(16, 128).T)
    ck = np.zeros((128, 33 * 4), np.float32)
    cb = np.zeros((128, 33), np.float32)
    for ccc in range(33):
        w = 128 if ccc < 32 else 32
        for j in range(4):
            ck[:w, ccc * 4 + j] = conv_k[j, 0, ccc * 128:ccc * 128 + w]
        cb[:w, ccc] = conv_b[ccc * 128:ccc * 128 + w]
    score = np.asarray(inputs["L0_attn_score_scale"], np.float32).reshape(32, 1)
    slopes = np.concatenate([
        np.asarray(inputs["L0_attn_decay_slopes"], np.float32),
        np.asarray(inputs["L0_attn_anchor_slopes"], np.float32)]).reshape(32, 1)
    deriv = np.asarray(inputs["L0_attn_deriv_logits"],
                       np.float32).reshape(1, 3)
    wre = np.asarray(inputs["L0_attn_W_re"], np.float32)
    wim = np.asarray(inputs["L0_attn_W_im"], np.float32)
    Wre2 = np.ascontiguousarray(np.concatenate([wre, wre], 0))
    Wim2 = np.ascontiguousarray(np.concatenate([wim, wim], 0))
    nsre2 = np.ascontiguousarray(
        np.concatenate([ns[:64], ns[:64]]).reshape(128, 1))
    nsim2 = np.ascontiguousarray(
        np.concatenate([ns[64:], ns[64:]]).reshape(128, 1))
    pwT = np.zeros((128, 48), np.float32)
    for dc in range(8):
        pwT[:, dc * 6:(dc + 1) * 6] = proj_w[:, dc * 128:(dc + 1) * 128].T
    anchT = np.ascontiguousarray(
        np.asarray(inputs["L0_ffn_anchors"], np.float32).T)
    rt = np.asarray(inputs["L0_ffn_routing_temp"],
                    np.float32).reshape(1, 1)
    SEL = np.zeros((32, 2048), np.float32)
    SEL2 = np.zeros((128, 512), np.float32)
    for ccc in range(16):
        for p in range(128):
            k = 2 * ccc + p // 64
            SEL[k, ccc * 128 + p] = 1.0
            SEL2[p, ccc * 32 + k] = 1.0
    ident = np.eye(128, dtype=np.float32)
    pos = np.arange(L, dtype=np.float32)
    dist = np.zeros((32, L), np.float32)
    dist[0:28, :] = (L - 1) - pos[None, :]
    dist[28:32, :] = pos[None, :]

    shared = {
        "ipw": np.ascontiguousarray(np.asarray(inputs["L0_attn_in_proj_w"],
                                               np.float32)),
        "opw": np.ascontiguousarray(np.asarray(inputs["L0_attn_out_proj_w"],
                                               np.float32)),
        "w1": np.ascontiguousarray(np.asarray(inputs["L0_ffn_w1"],
                                              np.float32)),
        "w2": np.ascontiguousarray(np.asarray(inputs["L0_ffn_w2"],
                                              np.float32)),
        "theta_t": theta_t, "ck": ck, "cb": cb, "score": score,
        "slopes": slopes, "deriv": deriv, "Wre2": Wre2, "Wim2": Wim2,
        "nsre2": nsre2, "nsim2": nsim2, "pwT": pwT, "anchT": anchT,
        "rt": rt, "SEL": SEL, "SEL2": SEL2, "ident": ident, "dist": dist,
        "hostc_in": np.array([[1e-5, 0.0, 0.0, 0.0]], np.float32),
    }
    return shared


def kernel(**inputs):
    x = np.asarray(inputs["L0_attn_x"], np.float32)
    shared = _host_prep(inputs)
    if "prog" not in _PROG_CACHE:
        _PROG_CACHE["prog"] = build_program()
    nc = _PROG_CACHE["prog"]
    in_maps = []
    for b in range(8):
        m = dict(shared)
        m["x"] = np.ascontiguousarray(x[b])
        in_maps.append(m)
    res = run_bass_kernel_spmd(nc, in_maps, core_ids=list(range(8)))
    out = np.stack([res.results[b]["y"] for b in range(8)], axis=0)
    return out.astype(np.float32)


if __name__ == "__main__":
    import reference as _r
    ins = _r.setup_inputs(0)
    out = kernel(**{k: np.asarray(v) for k, v in ins.items()})
    ref = np.asarray(_r.reference(**ins))
    err = np.sqrt(((out - ref) ** 2).mean() / (ref ** 2).mean())
    print("rel l2:", err)


# version 18 (final)
# speedup vs baseline: 1.1923x; 1.0047x over previous
import math
import numpy as np
from contextlib import ExitStack

import concourse.bass as bass
import concourse.mybir as mybir
from concourse.bass_utils import run_bass_kernel_spmd

F32 = mybir.dt.float32
BF16 = mybir.dt.bfloat16
I32 = mybir.dt.int32
AF = mybir.ActivationFunctionType
ALU = mybir.AluOpType
AX = mybir.AxisListType
PI = float(np.pi)

L = 4096
D = 1024
DI = 2048
C = 4128
K = 32
H = 64
NE = 6
DE = 512
TA = 512
TB = 512
NTA = L // TA
NTB = L // TB
NVC = 16
NGC = 16


class Op:
    __slots__ = ("stream", "dom", "emit", "deps", "need_inc", "inc_no",
                 "dma", "tag", "gidx", "sidx", "risky")


class Sched:

    def __init__(self):
        self.ops = []
        self.bufs = {}
        self.pending_barrier = None
        self.barrier_absorbed = set()
        self.scount = {}

    def add(self, stream, emit, reads=(), writes=(), sreads=(), dma=False,
            tag="", risky=False):
        o = Op()
        o.stream = stream
        if dma:
            qi = self.scount.get("qn_" + stream, 0)
            self.scount["qn_" + stream] = qi + 1
            o.dom = f"q{stream}L{qi % 16}"
        else:
            o.dom = stream
        o.emit = emit
        o.dma = dma
        o.tag = tag
        o.need_inc = False
        o.risky = risky
        o.sidx = self.scount.get(stream, 0)
        self.scount[stream] = o.sidx + 1
        deps = {}

        def dep(p, wait_needed):
            if p is None or p is o:
                return
            k = id(p)
            if k in deps:
                deps[k] = (p, deps[k][1] or wait_needed)
            else:
                deps[k] = (p, wait_needed)

        def need_wait(p, kind, scalar=False):
            if p.dom != o.dom:
                return True
            if p.dma and o.dma:
                return kind == "raw"
            if stream == "pe":
                return False
            if kind != "raw":
                return False
            return (o.sidx - p.sidx) <= 6

        for b in reads:
            e = self.bufs.get(b)
            if e and e[0] is not None:
                dep(e[0], need_wait(e[0], "raw", False))
        for b in sreads:
            e = self.bufs.get(b)
            if e and e[0] is not None:
                dep(e[0], need_wait(e[0], "raw", True))
        for b in writes:
            e = self.bufs.get(b)
            if e:
                if e[0] is not None:
                    dep(e[0], need_wait(e[0], "waw"))
                for r in e[1]:
                    dep(r, need_wait(r, "war"))
        if self.pending_barrier is not None and stream not in self.barrier_absorbed:
            self.barrier_absorbed.add(stream)
            for p in self.pending_barrier.values():
                dep(p, True)
        o.deps = list(deps.values())
        for b in list(reads) + list(sreads):
            self.bufs.setdefault(b, [None, []])[1].append(o)
        for b in writes:
            self.bufs[b] = [o, []]
        o.gidx = len(self.ops)
        self.ops.append(o)
        return o

    def barrier(self):
        last = {}
        for o in self.ops:
            last[o.dom] = o
        self.pending_barrier = last
        self.barrier_absorbed = set()

    def finalize(self):
        for o in self.ops:
            if o.dma:
                o.need_inc = True
            for (p, w) in o.deps:
                if w:
                    p.need_inc = True
        ctr = {}
        for o in self.ops:
            if o.need_inc:
                ctr[o.dom] = ctr.get(o.dom, 0) + 1
                o.inc_no = ctr[o.dom]
            else:
                o.inc_no = None
        self.totals = ctr

    def emit_stream(self, stream, eng, sems):
        known = {}
        for o in self.ops:
            if o.stream != stream:
                continue
            for (p, w) in o.deps:
                if not w:
                    continue
                v = p.inc_no * (16 if p.dma else 1)
                if known.get(p.dom, 0) < v:
                    eng.wait_ge(sems[p.dom], v)
                    known[p.dom] = v
            inst = o.emit(eng)
            if o.need_inc:
                inst.then_inc(sems[o.dom], 16 if o.dma else 1)


def build_program(dbg=()):
    nc = bass.Bass("TRN2", target_bir_lowering=False)
    S = Sched()

    def din(name, shape):
        return nc.dram_tensor(name, shape, F32, kind="ExternalInput").ap()

    x_d = din("x", [L, D])
    ipw_d = din("ipw", [D, C])
    opw_d = din("opw", [DI, D])
    w1_d = din("w1", [NE, D, DE])
    w2_d = din("w2", [NE, DE, D])
    theta_d = din("theta_t", [128, NVC])
    ck_d = din("ck", [128, 33 * 4])
    cb_d = din("cb", [128, 33])
    score_d = din("score", [32, 1])
    slopes_d = din("slopes", [32, 1])
    deriv_d = din("deriv", [1, 3])
    wre_d = din("Wre2", [128, 64])
    wim_d = din("Wim2", [128, 64])
    nsre_d = din("nsre2", [128, 1])
    nsim_d = din("nsim2", [128, 1])
    pwt_d = din("pwT", [128, 48])
    anch_d = din("anchT", [6, 6])
    rt_d = din("rt", [1, 1])
    sel_d = din("SEL", [32, 2048])
    sel2_d = din("SEL2", [128, 512])
    ident_d = din("ident", [128, 128])
    dist_d = din("dist", [32, L])
    hostc_d = din("hostc_in", [1, 4])

    y_d = nc.dram_tensor("y", [L, D], F32, kind="ExternalOutput").ap()
    if "u1" in dbg:
        u1_d = nc.dram_tensor("u1", [D, L], F32, kind="ExternalOutput").ap()
    else:
        u1_d = nc.dram_tensor("u1", [D, L], F32).ap()
    if "u2" in dbg:
        u2_d = nc.dram_tensor("u2", [D, L], F32, kind="ExternalOutput").ap()
    else:
        u2_d = nc.dram_tensor("u2", [D, L], F32).ap()
    ipbf_d = nc.dram_tensor("ipbf", [8, 128, 4096], BF16).ap()
    opbf_d = nc.dram_tensor("opbf", [8, 128, 2048], BF16).ap()
    if "a1" in dbg:
        dbg_ut = nc.dram_tensor("dbg_ut", [128, 8, TA], F32,
                                kind="ExternalOutput").ap()
        dbg_xln = nc.dram_tensor("dbg_xln", [128, 8 * TA], F32,
                                 kind="ExternalOutput").ap()
        dbg_ewf = nc.dram_tensor("dbg_ewf", [1, NE * TA], F32,
                                 kind="ExternalOutput").ap()
        dbg_hsb = nc.dram_tensor("dbg_hsb", [128, NE * 4 * TA], F32,
                                 kind="ExternalOutput").ap()
        dbg_rpk = nc.dram_tensor("dbg_rpk", [128, 64], F32,
                                 kind="ExternalOutput").ap()
        dbg_cst = nc.dram_tensor("dbg_cst", [128, 16], F32,
                                 kind="ExternalOutput").ap()
        dbg_sm = nc.dram_tensor("dbg_sm", [1, 16], F32,
                                kind="ExternalOutput").ap()
        dbg_b = nc.dram_tensor("dbg_b", [128, 6 * TB], F32,
                               kind="ExternalOutput").ap()
        dbg_bs = nc.dram_tensor("dbg_bs", [32, 4 * TB], F32,
                                kind="ExternalOutput").ap()
        dbg_b2 = nc.dram_tensor("dbg_b2", [128, 2 * TB], F32,
                                kind="ExternalOutput").ap()
    else:
        dbg_ut = dbg_xln = dbg_ewf = dbg_hsb = dbg_rpk = dbg_b = dbg_bs = None
        dbg_b2 = None

    st_p = ExitStack()

    def sb(name, shape, dt, st=st_p):
        return st.enter_context(nc.sbuf_tensor(name, shape, dt))

    def pt(name, shape, dt, st=st_p):
        return st.enter_context(nc.psum_tensor(name, shape, dt))

    xlnbf = sb("xlnbf", [128, 4096], BF16)
    sqb = [sb(f"sqb{i}", [128, 512], BF16) for i in range(2)]
    d1b = sb("d1b", [128, 512], F32)
    rowp = sb("rowp", [1, 2048], F32)
    rpk = sb("rpk", [128, 64], F32)
    cst = sb("cst", [128, 16], F32)
    ebuf = [sb(f"ebuf{i}", [128, 512], F32) for i in range(2)]
    identt = sb("identt", [128, 128], F32)
    onesr = sb("onesr", [1, 128], F32)
    onesrb = sb("onesrb", [1, 128], BF16)
    onescb = sb("onescb", [128, 1], BF16)
    ubb = [sb(f"ubb{i}", [128, 512], BF16) for i in range(2)]
    onesc = sb("onesc", [128, 1], F32)
    theta_t = sb("theta_sb", [128, NVC], F32)
    theta2 = sb("theta2", [128, NVC], F32)
    ckt = sb("ckt", [128, 33 * 4], F32)
    cbt = sb("cbt", [128, 33], F32)
    scoret = sb("scoret", [32, 1], F32)
    slopest = sb("slopest", [32, 1], F32)
    negslope = sb("negslope", [32, 1], F32)
    derivt = sb("derivt", [1, 3], F32)
    wret = sb("wret", [128, 64], F32)
    wimt = sb("wimt", [128, 64], F32)
    nsret = sb("nsret", [128, 1], F32)
    nsimt = sb("nsimt", [128, 1], F32)
    wrens = sb("wrens", [128, 64], BF16)
    wimns = sb("wimns", [128, 64], BF16)
    pwtt = sb("pwtt", [128, 48], F32)
    ancht = sb("ancht", [6, 6], F32)
    rtt = sb("rtt", [1, 1], F32)
    selt = sb("selt", [32, 2048], BF16)
    sel2t = sb("sel2t", [128, 512], BF16)
    ones_scan = sb("ones_scan", [128, 512], F32)
    wrebd = sb("wrebd", [128, 128], BF16)
    wimbd = sb("wimbd", [128, 128], BF16)
    carry_re = sb("carry_re", [128, NVC], F32)
    carry_im = sb("carry_im", [128, NVC], F32)
    carry_den = sb("carry_den", [32, 1], F32)
    tails = sb("tails", [128, 33 * 3], F32)
    dsm = sb("dsm", [1, 8], F32)
    wsm = sb("wsm", [1, 3], F32)
    rts = sb("rts", [1, 4], F32)
    spl = sb("spl", [32, 4], F32)
    hostc = sb("hostc", [1, 4], F32)

    PB = [pt(f"PB{i}", [128, 512], F32) for i in range(8)]
    PBK = [f"PB{i}" for i in range(8)]

    CONSTS = {"c256": 256.5, "n2pi": -2 * PI, "p2pi": 2 * PI,
              "npih": -PI / 2, "nhalf": -0.5, "half": 0.5, "one": 1.0}
    CCOL = {k: i for i, k in enumerate(CONSTS)}
    NDCOL = {"w0b": 8, "w1b": 9, "nw2b": 10, "hirb": 11, "iqt": 12}

    def cc_ap(name, pc=128):
        i = CCOL[name] if name in CCOL else NDCOL[name]
        return cst[0:pc, i:i + 1]

    sems = {}

    def mm(bank, out_ap, lhsT, rhs, start, stop, reads, tag="mm",
           tile_position=None):
        def e(eng, out_ap=out_ap, lhsT=lhsT, rhs=rhs, start=start, stop=stop,
              tp=tile_position):
            return eng.matmul(out_ap, lhsT, rhs, start=start, stop=stop,
                              tile_position=tp)
        return S.add("pe", e, reads=tuple(reads), writes=(bank,), tag=tag)

    def transpose(bank, out_ap, in_ap, reads, ident_ap=None, tag="tr"):
        ia = ident_ap if ident_ap is not None else identt[:, :]
        def e(eng, out_ap=out_ap, in_ap=in_ap, ia=ia):
            return eng.transpose(out_ap, in_ap, ia)
        return S.add("pe", e, reads=tuple(reads) + ("identt",), writes=(bank,),
                     tag=tag)

    def act(out_ap, in_ap, func, reads, writes, scale=None, bias=None,
            sreads=(), tag="act"):
        def e(eng, out_ap=out_ap, in_ap=in_ap, func=func, scale=scale,
              bias=bias):
            kw = {}
            if scale is not None:
                kw["scale"] = scale
            if bias is not None:
                kw["bias"] = bias
            return eng.activation(out_ap, in_ap, func, **kw)
        return S.add("act", e, reads=tuple(reads), writes=tuple(writes),
                     sreads=tuple(sreads), tag=tag)

    def dve(emit, reads, writes, sreads=(), tag="dve"):
        return S.add("dve", emit, reads=tuple(reads), writes=tuple(writes),
                     sreads=tuple(sreads), tag=tag)

    def pool(emit, reads, writes, sreads=(), tag="pool"):
        return S.add("pool", emit, reads=tuple(reads), writes=tuple(writes),
                     sreads=tuple(sreads), tag=tag)

    def dma(queue, out_ap, in_ap, reads, writes, tag="dma"):
        def e(eng, out_ap=out_ap, in_ap=in_ap):
            return eng.dma_start(out=out_ap, in_=in_ap)
        return S.add(queue, e, reads=tuple(reads), writes=tuple(writes),
                     dma=True, tag=tag)

    def ts(out_ap, in_ap, s1, s2, op0, op1, reads, writes, sreads=(),
           stream="dve", tag="ts"):
        def e(eng, out_ap=out_ap, in_ap=in_ap, s1=s1, s2=s2, op0=op0, op1=op1):
            if op1 is None:
                return eng.tensor_scalar(out_ap, in_ap, s1, None, op0=op0)
            return eng.tensor_scalar(out_ap, in_ap, s1, s2, op0=op0, op1=op1)
        return S.add(stream, e, reads=tuple(reads), writes=tuple(writes),
                     sreads=tuple(sreads), tag=tag)

    def tt(out_ap, a_ap, b_ap, op, reads, writes, stream="dve", tag="tt"):
        def e(eng, out_ap=out_ap, a_ap=a_ap, b_ap=b_ap, op=op):
            return eng.tensor_tensor(out_ap, a_ap, b_ap, op=op)
        return S.add(stream, e, reads=tuple(reads), writes=tuple(writes),
                     tag=tag)

    def stt(out_ap, in0_ap, s_ap, in1_ap, op0, op1, reads, writes, sreads=(),
            stream="dve", tag="stt"):
        def e(eng, out_ap=out_ap, in0_ap=in0_ap, s_ap=s_ap, in1_ap=in1_ap,
              op0=op0, op1=op1):
            return eng.scalar_tensor_tensor(out_ap, in0_ap, s_ap, in1_ap,
                                            op0=op0, op1=op1)
        return S.add(stream, e, reads=tuple(reads), writes=tuple(writes),
                     sreads=tuple(sreads), tag=tag)

    def copy(out_ap, in_ap, reads, writes, stream="dve", tag="copy"):
        def e(eng, out_ap=out_ap, in_ap=in_ap):
            return eng.tensor_copy(out_ap, in_ap)
        return S.add(stream, e, reads=tuple(reads), writes=tuple(writes),
                     tag=tag)

    def memset(ap, v, writes, stream="dve", tag="memset"):
        def e(eng, ap=ap, v=v):
            return eng.memset(ap, v)
        return S.add(stream, e, writes=tuple(writes), tag=tag)

    def scan(out_ap, d0_ap, d1_ap, init_ap, reads, writes, sreads, tag="scan"):
        def e(eng, out_ap=out_ap, d0_ap=d0_ap, d1_ap=d1_ap, init_ap=init_ap):
            return eng.tensor_tensor_scan(out_ap, d0_ap, d1_ap, init_ap,
                                          op0=ALU.mult, op1=ALU.add)
        return S.add("dve", e, reads=tuple(reads), writes=tuple(writes),
                     sreads=tuple(sreads), tag=tag, risky=True)

    def reduce(out_ap, in_ap, op, reads, writes, tag="red"):
        def e(eng, out_ap=out_ap, in_ap=in_ap, op=op):
            return eng.tensor_reduce(out_ap, in_ap, axis=AX.X, op=op)
        return S.add("dve", e, reads=tuple(reads), writes=tuple(writes),
                     tag=tag, risky=True)

    def recip(out_ap, in_ap, reads, writes, tag="recip"):
        def e(eng, out_ap=out_ap, in_ap=in_ap):
            return eng.reciprocal(out_ap, in_ap)
        return S.add("dve", e, reads=tuple(reads), writes=tuple(writes),
                     tag=tag)

    for apout, src, wk in [
        (theta_t[:, :], theta_d, "theta"), (ckt[:, :], ck_d, "ck"),
        (cbt[:, :], cb_d, "cb"), (scoret[:, :], score_d, "score"),
        (slopest[:, :], slopes_d, "slopes"), (derivt[:, :], deriv_d, "deriv"),
        (wret[:, :], wre_d, "wre"), (wimt[:, :], wim_d, "wim"),
        (nsret[:, :], nsre_d, "nsre"), (nsimt[:, :], nsim_d, "nsim"),
        (pwtt[:, :], pwt_d, "pwt"), (ancht[:, :], anch_d, "anch"),
        (rtt[:, :], rt_d, "rtsb"),
        (identt[:, :], ident_d, "identt"),
        (hostc[:, :], hostc_d, "hostc"),
    ]:
        dma("sync", apout, src, reads=(), writes=(wk,), tag="ld_" + wk)

    for name, v in CONSTS.items():
        memset(cst[:, CCOL[name]:CCOL[name] + 1], v, writes=(f"cst_{name}",))
    dma("pool", sel2t[:, :], sel2_d, reads=(), writes=("sel2",),
        tag="ld_sel2")
    dma("pool", selt[:, :], sel_d, reads=(), writes=("sel",), tag="ld_sel")
    memset(onesr[:, :], 1.0, writes=("onesr",))
    memset(onesrb[:, :], 1.0, writes=("onesrb",))
    memset(onesc[:, :], 1.0, writes=("onesc",))
    memset(onescb[:, :], 1.0, writes=("onescb",))
    memset(ones_scan[:, :], 1.0, writes=("ones_scan",))
    memset(carry_re[:, :], 0.0, writes=tuple(f"cre{c}" for c in range(NVC)))
    memset(carry_im[:, :], 0.0, writes=tuple(f"cim{c}" for c in range(NVC)))
    memset(carry_den[:, :], 0.0, writes=("cden",))
    memset(tails[:, :], 0.0, writes=tuple(f"tail{c}" for c in range(33)))

    ts(theta2[:, :], theta_t[:, :], 1.0 / (2 * PI), None, ALU.mult, None,
       reads=("theta",), writes=("theta2",))
    NEG1_COL = 13
    memset(cst[:, NEG1_COL:NEG1_COL + 1], -1.0, writes=("cst_neg1",))
    act(spl[:, 0:1], slopest[:, :], AF.Abs, reads=("slopes",), writes=("spl0",))
    act(spl[:, 1:2], spl[:, 0:1], AF.Exp, reads=("spl0",), writes=("spl1",),
        scale=cst[0:32, NEG1_COL:NEG1_COL + 1], sreads=("cst_neg1",))
    act(spl[:, 2:3], spl[:, 1:2], AF.Ln, reads=("spl1",), writes=("spl2",),
        bias=cc_ap("one", 32), sreads=("cst_one",))
    ts(spl[:, 3:4], slopest[:, :], 0.0, None, ALU.max, None,
       reads=("slopes",), writes=("spl3",))
    def _ns(eng):
        return eng.tensor_tensor(negslope[:, :], spl[:, 2:3], spl[:, 3:4],
                                 op=ALU.add)
    S.add("dve", _ns, reads=("spl2", "spl3"), writes=("negslope_pos",))
    ts(negslope[:, :], negslope[:, :], -1.0, None, ALU.mult, None,
       reads=("negslope_pos",), writes=("negslope",))

    reduce(dsm[:, 0:1], derivt[:, :], ALU.max, reads=("deriv",),
           writes=("dsm_max",))
    ts(wsm[:, :], derivt[:, :], dsm[:, 0:1], None, ALU.subtract, None,
       reads=("deriv",), writes=("wsm_sub",), sreads=("dsm_max",))
    act(wsm[:, :], wsm[:, :], AF.Exp, reads=("wsm_sub",), writes=("wsm_exp",))
    reduce(dsm[:, 1:2], wsm[:, :], ALU.add, reads=("wsm_exp",),
           writes=("dsm_sum",))
    act(dsm[:, 3:4], dsm[:, 1:2], AF.Ln, reads=("dsm_sum",),
        writes=("dsm_ln",))
    act(dsm[:, 2:3], dsm[:, 3:4], AF.Exp, reads=("dsm_ln",),
        writes=("dsm_rec",), scale=cst[0:1, NEG1_COL:NEG1_COL + 1],
        sreads=("cst_neg1",))
    ts(wsm[:, :], wsm[:, :], dsm[:, 2:3], None, ALU.mult, None,
       reads=("wsm_exp",), writes=("wsm",), sreads=("dsm_rec",))
    for j, (name, neg) in enumerate([("w0b", False), ("w1b", False),
                                     ("nw2b", True)]):
        bank_ap = PB[7][:, j:j + 1]
        mm("PB7", bank_ap, onesr[:, :], wsm[:, j:j + 1], True, True,
           reads=("onesr", "wsm"), tag="bc_w")
        col = NDCOL[name]
        ts(cst[:, col:col + 1], bank_ap, -1.0 if neg else 1.0, None, ALU.mult,
           None, reads=("PB7",), writes=(f"cst_{name}",))
    ts(rts[:, 0:1], rtt[:, :], 0.1, None, ALU.max, None, reads=("rtsb",),
       writes=("rtc",))
    act(rts[:, 3:4], rts[:, 0:1], AF.Ln, reads=("rtc",), writes=("rtln",))
    act(rts[:, 1:2], rts[:, 3:4], AF.Exp, reads=("rtln",), writes=("rtrec",),
        scale=cst[0:1, NEG1_COL:NEG1_COL + 1], sreads=("cst_neg1",))
    ts(rts[:, 2:3], rts[:, 1:2], 0.5, None, ALU.mult, None, reads=("rtrec",),
       writes=("hir1",))
    mm("PB7", PB[7][:, 4:5], onesr[:, :], rts[:, 2:3], True, True,
       reads=("onesr", "hir1"), tag="bc_hir")
    ts(cst[:, NDCOL["hirb"]:NDCOL["hirb"] + 1], PB[7][:, 4:5], 1.0, None,
       ALU.mult, None, reads=("PB7",), writes=("cst_hirb",))
    memset(cst[:, NDCOL["iqt"]:NDCOL["iqt"] + 1], 1.0 / 0.3,
           writes=("cst_iqt",))
    ts(wrens[:, :], wret[:, :], nsret[:, 0:1], None, ALU.mult, None,
       reads=("wre",), writes=("wrens",), sreads=("nsre",))
    ts(wimns[:, :], wimt[:, :], nsimt[:, 0:1], None, ALU.mult, None,
       reads=("wim",), writes=("wimns",), sreads=("nsim",))
    memset(wrebd[:, :], 0.0, writes=("wrebd0",))
    memset(wimbd[:, :], 0.0, writes=("wimbd0",))
    copy(wrebd[0:64, 0:64], wrens[0:64, :], reads=("wrens", "wrebd0"),
         writes=("wrebd1",))
    copy(wrebd[64:128, 64:128], wrens[64:128, :], reads=("wrens", "wrebd1"),
         writes=("wrebd",))
    copy(wimbd[0:64, 0:64], wimns[0:64, :], reads=("wimns", "wimbd0"),
         writes=("wimbd1",))
    copy(wimbd[64:128, 64:128], wimns[64:128, :], reads=("wimns", "wimbd1"),
         writes=("wimbd",))

    def emit_ln(ut, utkeys, T, bmb, brb, phase, need_nm, xt=None, xpre=None):
        if xt is None:
            xt = xlnbf
            xpre = "xln"
        nd = 8
        sx = PB[2][0:1, 0:T]
        sq = PB[3][0:1, 0:T]
        for dc in range(nd):
            sb_ = sqb[dc % 2]
            ub_ = ubb[dc % 2]
            copy(ub_[:, 0:T], ut[:, dc, :], reads=(utkeys[dc],),
                 writes=(f"ubb{dc % 2}",), stream="pool", tag="ln_ub")
            act(sb_[:, 0:T], ut[:, dc, :], AF.Square, reads=(utkeys[dc],),
                writes=(f"sqb{dc % 2}",), tag="ln_sq")
            mm("PB2", sx, onescb[:, :], ub_[:, 0:T], dc == 0, dc == nd - 1,
               reads=("onescb", f"ubb{dc % 2}"), tag="ln_sx")
            mm("PB3", sq, onescb[:, :], sb_[:, 0:T], dc == 0, dc == nd - 1,
               reads=("onescb", f"sqb{dc % 2}"), tag="ln_sq_mm")
        m = rowp[0:1, 0:T]
        nm = rowp[0:1, 512:512 + T]
        va = rowp[0:1, 1024:1024 + T]
        rs_ = rowp[0:1, 1536:1536 + T]
        ts(m, sx, 1.0 / D, None, ALU.mult, None, reads=("PB2",),
           writes=("lnm",), tag="ln_m")
        ts(va, sq, 1.0 / D, None, ALU.mult, None, reads=("PB3",),
           writes=("lnmsq",), tag="ln_msq")
        if need_nm:
            ts(nm, m, -1.0, None, ALU.mult, None, reads=("lnm",),
               writes=("lnnm",), tag="ln_nm")
        def _mm2(eng):
            return eng.tensor_tensor(d1b[0:1, 0:T], m, m, op=ALU.mult)
        S.add("dve", _mm2, reads=("lnm",), writes=("d1row",), tag="ln_m2")
        def _var2(eng):
            return eng.tensor_tensor(va, va, d1b[0:1, 0:T], op=ALU.subtract)
        S.add("dve", _var2, reads=("lnmsq", "d1row"), writes=("lnvar",),
              tag="ln_var")
        act(va, va, AF.Ln, reads=("lnvar",), writes=("lnsd",),
            bias=hostc[0:1, 0:1], sreads=("hostc",), tag="ln_lnv")
        act(rs_, va, AF.Exp, reads=("lnsd",), writes=("lnrstd",),
            scale=cst[0:1, CCOL["nhalf"]:CCOL["nhalf"] + 1],
            sreads=("cst_nhalf",), tag="ln_rstd")
        mm("PB4", bmb[:, 0:T], onesr[:, :], m, True, True,
           reads=("onesr", "lnm"), tag="ln_mbc")
        mm("PB5", brb[:, 0:T], onesr[:, :], rs_, True, True,
           reads=("onesr", "lnrstd"), tag="ln_rbc")
        for dc in range(nd):
            def _sub(eng, dc=dc, T=T):
                return eng.tensor_tensor(d1b[:, 0:T], ut[:, dc, :],
                                         bmb[:, 0:T], op=ALU.subtract)
            S.add("dve", _sub, reads=(utkeys[dc], "PB4"), writes=("d1b",),
                  tag="ln_sub")
            def _mul(eng, dc=dc, T=T, xt=xt):
                return eng.tensor_tensor(xt[:, dc * T:(dc + 1) * T],
                                         d1b[:, 0:T], brb[:, 0:T],
                                         op=ALU.mult)
            S.add("dve", _mul, reads=("d1b", "PB5"),
                  writes=(f"{xpre}{dc}",), tag="ln_mul")

    def emit_ffn(phase, w1k, w2k, w1sb, w2sb, xin, ut, hsb, hgb, ewbs, ewf,
                 qts, ytm, in_dram, out_dram, out_tm, tile_hook=None):
        T = TA
        for it in range(NTA):
            t0 = it * T
            par = it % len(ut)
            utt = ut[par]
            utkeys = [f"ut_{phase}{par}_{dc}" for dc in range(8)]
            if phase == "A":
                for h in range(2):
                    hsrc = x_d[t0 + h * 256:t0 + (h + 1) * 256, :].rearrange(
                        "(s p) d -> p s d", p=128)
                    dma("sync", xin[:, :, :], hsrc, reads=(),
                        writes=("xin0",), tag="ldx")
                    for dc in range(8):
                        bank = PBK[dc]
                        pb = PB[dc]
                        for sl in range(2):
                            s = h * 2 + sl
                            transpose(bank, pb[:, s * 128:(s + 1) * 128],
                                      xin[:, sl, dc * 128:(dc + 1) * 128],
                                      reads=("xin0",), tag="trin")
                        if h == 1:
                            act(utt[:, dc, :], pb[:, 0:T], AF.Identity,
                                reads=(bank,), writes=(utkeys[dc],),
                                tag="ut_ev")
            else:
                src = in_dram.rearrange("(c p) t -> p c t", p=128)[
                    :, :, t0:t0 + T]
                dma("sync", utt[:, :, :], src,
                    reads=tuple(f"u2d_{it}_{dc}" for dc in range(8)),
                    writes=tuple(utkeys), tag="ldu")
            emit_ln(utt, utkeys, T, PB[4], PB[5], phase + str(it), True)
            nsub = T // 128
            rs_row = rowp[0:1, 1536:1536 + T]
            for s in range(nsub):
                transpose("PB6", PB[6][:, s:s + 1],
                          rs_row[0:1, s * 128:(s + 1) * 128],
                          reads=("lnrstd",),
                          ident_ap=identt[0:1, 0:1], tag="tr_rstd")
            copy(rpk[:, 50:50 + nsub], PB[6][:, 0:nsub], reads=("PB6",),
                 writes=("rstd_tm",), tag="rstd_tm")
            if phase == "A" and it == 0:
                for dc in range(8):
                    mm("PB6", PB[6][0:1, 8:14], onesc[:, :],
                       pwtt[:, dc * 6:(dc + 1) * 6], dc == 0, dc == 7,
                       reads=("pwt", "onesc"), tag="srow_mm")
                copy(rpk[0:1, 56:62], PB[6][0:1, 8:14], reads=("PB6",),
                     writes=("srow",), tag="srow")
            nm_row = rowp[0:1, 512:512 + T]
            for s in range(nsub):
                zr = PB[2][:, 0:6] if s % 2 == 0 else PB[3][:, 0:6]
                zrk = "PB2" if s % 2 == 0 else "PB3"
                for dc in range(8):
                    mm(zrk, zr, utt[:, dc, s * 128:(s + 1) * 128],
                       pwtt[:, dc * 6:(dc + 1) * 6], dc == 0, False,
                       reads=(utkeys[dc], "pwt"), tag="zu")
                mm(zrk, zr, nm_row[0:1, s * 128:(s + 1) * 128],
                   rpk[0:1, 56:62], False, True,
                   reads=("lnnm", "srow"), tag="zfix")
                zt = rpk[:, 38:44]
                ts(zt, zr, rpk[:, 50 + s:51 + s], None, ALU.mult, None,
                   reads=(zrk,), writes=("zt",), sreads=("rstd_tm",),
                   tag="zts")
                qv = rpk[:, 44:50]
                act(qv, zt, AF.Tanh, reads=("zt",), writes=("qv",),
                    scale=cc_ap("iqt"), sreads=("cst_iqt",), tag="qtanh")
                transpose("PB7", PB[7][0:6, 0:128], qv, reads=("qv",),
                          tag="qtr")
                copy(qts[0:6, 0:128], PB[7][0:6, 0:128], reads=("PB7",),
                     writes=("qTs",), tag="qTs")
                mm("PB7", PB[7][:, 128:134], qts[0:6, 0:128],
                   ancht[:, :], True, True, reads=("qTs", "anch"), tag="dot")
                lg = rpk[:, 8:14]
                ts(lg, PB[7][:, 128:134], -6.0, cc_ap("hirb"), ALU.add,
                   ALU.mult, reads=("PB7",), writes=("lg",),
                   sreads=("cst_hirb",), tag="lgts")
                m1 = rpk[:, 0:1]
                reduce(m1, lg, ALU.max, reads=("lg",), writes=("m1",))
                mask1 = rpk[:, 14:20]
                ts(mask1, lg, m1, None, ALU.is_ge, None, reads=("lg",),
                   writes=("mask1",), sreads=("m1",), tag="mask1")
                l2 = rpk[:, 20:26]
                stt(l2, mask1, -1e30, lg, ALU.mult, ALU.add,
                    reads=("mask1", "lg"), writes=("l2",), tag="l2")
                m2 = rpk[:, 1:2]
                reduce(m2, l2, ALU.max, reads=("l2",), writes=("m2",))
                mask2 = rpk[:, 26:32]
                ts(mask2, l2, m2, None, ALU.is_ge, None, reads=("l2",),
                   writes=("mask2",), sreads=("m2",), tag="mask2")
                d21 = rpk[:, 2:3]
                ts(d21, m2, m1, None, ALU.subtract, None, reads=("m2",),
                   writes=("d21",), sreads=("m1",), tag="d21")
                w1s = rpk[:, 5:6]
                act(w1s, d21, AF.Sigmoid, reads=("d21",), writes=("w1s",),
                    scale=cst[:, NEG1_COL:NEG1_COL + 1],
                    sreads=("cst_neg1",), tag="w1s")
                w2s = rpk[:, 6:7]
                act(w2s, d21, AF.Sigmoid, reads=("d21",), writes=("w2s",),
                    tag="w2s")
                ew1 = rpk[:, 32:38]
                ts(ew1, mask1, w1s, None, ALU.mult, None, reads=("mask1",),
                   writes=("ew1",), sreads=("w1s",), tag="ew1")
                stt(ew1, mask2, w2s, ew1, ALU.mult, ALU.add,
                    reads=("mask2", "ew1"), writes=("ew2",), sreads=("w2s",),
                    tag="ew2")
                if phase == "A" and it == 1 and s == 0 and dbg_rpk is not None:
                    dma("sync", dbg_rpk[:, :], rpk[:, :],
                        reads=("ew2", "lg", "m1", "m2", "mask1", "mask2",
                               "zt", "qv", "d21", "e21", "w1s", "w2s",
                               "rstd_tm"),
                        writes=("dbgr",), tag="dbg")
                    dma("sync", dbg_cst[:, :], cst[:, :],
                        reads=tuple("cst_" + k for k in
                                    ["w0b", "w1b", "nw2b", "hirb", "iqt"]),
                        writes=("dbgc",), tag="dbg")
                    dma("sync", dbg_sm[0:1, 0:3], wsm[:, :], reads=("wsm",),
                        writes=("dbgs1",), tag="dbg")
                    dma("sync", dbg_sm[0:1, 4:12], dsm[:, :],
                        reads=("dsm_max", "dsm_sum", "dsm_rec"),
                        writes=("dbgs2",), tag="dbg")
                    dma("sync", dbg_sm[0:1, 12:16], rts[:, :],
                        reads=("rtc", "rtrec", "hir1"),
                        writes=("dbgs3",), tag="dbg")
                for e_ in range(NE):
                    transpose("PB7", PB[7][0:1, 384:512], ew1[:, e_:e_ + 1],
                              reads=("ew2",), tag="ewtr")
                    copy(ewf[0:1, e_ * T + s * 128:e_ * T + (s + 1) * 128],
                         PB[7][0:1, 384:512], reads=("PB7",),
                         writes=(f"ewf{e_}",), tag="ewf")
            for e_ in range(NE):
                ewb = PB[6]
                mm("PB6", ewb[:, 0:T], onesrb[:, :],
                   ewf[0:1, e_ * T:(e_ + 1) * T], True, True,
                   reads=("onesrb", f"ewf{e_}"), tag="ewb")
                ewsb = ewbs[e_ % 2]
                act(ewsb[:, 0:T], ewb[:, 0:T], AF.Identity, reads=("PB6",),
                    writes=(f"ewsb{e_ % 2}",), tag="ewb_ev")
                for hc in range(4):
                    hb = PBK[hc % 2]
                    hpb = PB[hc % 2]
                    for dc in range(8):
                        lhs = w1sb[:, (e_ * 8 + dc) * 512 + hc * 128:
                                   (e_ * 8 + dc) * 512 + hc * 128 + 128]
                        mm(hb, hpb[:, 0:T], lhs,
                           xlnbf[:, dc * T:(dc + 1) * T], dc == 0, dc == 7,
                           reads=(w1k[e_], f"xln{dc}"), tag="w1mm")
                    hgv = hgb[hc % 2]
                    act(hgv[:, 0:T], hpb[:, 0:T], AF.Gelu, reads=(hb,),
                        writes=(f"hg{hc % 2}",), tag="gelu")
                    def _hs(eng, e_=e_, hc=hc, T=T, hgv=hgv, ewsb=ewsb):
                        return eng.tensor_tensor(
                            hsb[:, (e_ * 4 + hc) * T:(e_ * 4 + hc + 1) * T],
                            hgv[:, 0:T], ewsb[:, 0:T], op=ALU.mult)
                    S.add("pool", _hs, reads=(f"hg{hc % 2}", f"ewsb{e_ % 2}"),
                          writes=(f"hs{e_}_{hc}",), tag="hscale")
            if phase == "A" and it == 1 and dbg_ut is not None:
                dma("sync", dbg_ut[:, :, :], utt[:, :, :],
                    reads=tuple(utkeys), writes=("dbg1",), tag="dbg")
                dma("pool", dbg_xln[:, :], xlnbf[:, 0:8 * T],
                    reads=tuple(f"xln{d_}" for d_ in range(8)),
                    writes=("dbg2",), tag="dbg")
                dma("pool", dbg_ewf[:, :], ewf[0:1, :],
                    reads=tuple(f"ewf{e_}" for e_ in range(NE)),
                    writes=("dbg3",), tag="dbg")
                dma("pool", dbg_hsb[:, :], hsb[:, :],
                    reads=tuple(f"hs{e_}_{h_}" for e_ in range(NE)
                                for h_ in range(4)),
                    writes=("dbg4",), tag="dbg")
            for dc in range(8):
                ob = PBK[4 + (dc % 2)]
                opb = PB[4 + (dc % 2)]
                first = True
                for e_ in range(NE):
                    for hc in range(4):
                        lhs = w2sb[:, (e_ * 4 + hc) * 1024 + dc * 128:
                                   (e_ * 4 + hc) * 1024 + dc * 128 + 128]
                        mm(ob, opb[:, 0:T], lhs,
                           hsb[:, (e_ * 4 + hc) * T:(e_ * 4 + hc + 1) * T],
                           first, (e_ == NE - 1) and (hc == 3),
                           reads=(w2k[e_], f"hs{e_}_{hc}"), tag="w2mm")
                        first = False
                eb = ebuf[dc % 2]
                stt(eb[:, 0:T], opb[:, 0:T], 0.5, utt[:, dc, :], ALU.mult,
                    ALU.add, reads=(ob, utkeys[dc]), writes=(f"ebuf{dc % 2}",),
                    sreads=(), tag="res_ev")
                if not out_tm:
                    dst = out_dram[dc * 128:(dc + 1) * 128, t0:t0 + T]
                    dma("act", dst, eb[:, 0:T], reads=(f"ebuf{dc % 2}",),
                        writes=(f"u1d_{it}_{dc}",), tag="st_u1")
                    if dc == 7 and tile_hook is not None:
                        tile_hook(it)
                else:
                    for s in range(T // 128):
                        bank = PBK[6 + (s % 2)]
                        pb = PB[6 + (s % 2)]
                        transpose(bank, pb[:, 0:128],
                                  eb[:, s * 128:(s + 1) * 128],
                                  reads=(f"ebuf{dc % 2}",), tag="trout")
                        yp = ytm[s % 2]
                        act(yp[:, 0:128], pb[:, 0:128],
                            AF.Identity, reads=(bank,),
                            writes=(f"yp{s % 2}",), tag="ytm_ev")
                        dst = y_d[t0 + s * 128:t0 + (s + 1) * 128,
                                  dc * 128:(dc + 1) * 128]
                        dma("sync", dst, yp[:, 0:128],
                            reads=(f"yp{s % 2}",),
                            writes=(f"yd_{it}_{dc}_{s}",), tag="st_y")

    st_a = ExitStack()
    w1sb = sb("w1sb", [128, NE * 8 * 512], BF16, st_a)
    w2sb = sb("w2sb", [128, NE * 4 * 1024], BF16, st_a)
    xin = sb("xin0t", [128, 2, D], F32, st_a)
    utA = [sb("utA0", [128, 8, TA], F32, st_a)]
    hsb = sb("hsb", [128, NE * 4 * TA], BF16, st_a)
    hgb = [sb(f"hgb{i}", [128, TA], BF16, st_a) for i in range(2)]
    ewbsA = [sb(f"ewbs{i}", [128, TA], BF16, st_a) for i in range(2)]
    ewfA = sb("ewfA", [1, NE * TA], BF16, st_a)
    qtsA = sb("qtsA", [6, 128], F32, st_a)
    ipstg = sb("ipstg", [128, 4096], BF16, st_a)

    w1keys = [f"w1_{e}" for e in range(NE)]
    w2keys = [f"w2_{e}" for e in range(NE)]
    for e in range(NE):
        src = w1_d[e].rearrange("(dc p) h -> p dc h", p=128)
        dst = w1sb[:, e * 4096:(e + 1) * 4096].rearrange(
            "p (dc h) -> p dc h", dc=8)
        dma("pool", dst, src, reads=(), writes=(w1keys[e],), tag="ldw1")
    for e in range(NE):
        src = w2_d[e].rearrange("(hc p) d -> p hc d", p=128)
        dst = w2sb[:, e * 4096:(e + 1) * 4096].rearrange(
            "p (hc d) -> p hc d", hc=4)
        dma("pool", dst, src, reads=(), writes=(w2keys[e],), tag="ldw2")
    def stage_hook(it):
        for g in (2 * it, 2 * it + 1):
            if g < 8:
                gsrc = ipw_d[:, g * 512:(g + 1) * 512].rearrange(
                    "(dc p) c -> p dc c", p=128)
                dma("pool", ipstg[:, :].rearrange("p (dc c) -> p dc c", dc=8),
                    gsrc, reads=(), writes=("ipstg",), tag="ipcast")
                dma("sync", ipbf_d[g], ipstg[:, :], reads=("ipstg",),
                    writes=(f"ipbf{g}",), tag="ipstore")
        q = it - 4
        if 0 <= q < 4:
            qsrc = opw_d.rearrange("(kc p) d -> p kc d", p=128)[
                :, 4 * q:4 * q + 4, :]
            dma("pool", ipstg[:, :].rearrange("p (kc d) -> p kc d", kc=4),
                qsrc, reads=(), writes=("ipstg",), tag="opcast")
            for dcq in range(8):
                ssrc = ipstg[:, :].rearrange("p (kc d) -> p kc d", kc=4)[
                    :, :, dcq * 128:(dcq + 1) * 128]
                sdst = opbf_d[dcq][:, q * 512:(q + 1) * 512].rearrange(
                    "p (kc w) -> p kc w", kc=4)
                dma("sync", sdst, ssrc, reads=("ipstg",),
                    writes=(f"opbf{dcq}_{q}",), tag="opstore")

    emit_ffn("A", w1keys, w2keys, w1sb, w2sb, xin, utA, hsb, hgb, ewbsA,
             ewfA, qtsA, None, None, u1_d, False, tile_hook=stage_hook)
    st_a.close()
    S.barrier()

    st_b = ExitStack()
    ipb = [sb(f"ipb{i}", [128, 8 * 512], BF16, st_b) for i in range(3)]
    ipsr = sb("ipsr", [128, 8 * 32], BF16, st_b)
    opd = [sb(f"opd{i}", [128, 2048], BF16, st_b) for i in range(2)]
    xlnb2 = sb("xlnb2", [128, 4096], BF16, st_b)
    utB = [sb(f"utB{i}", [128, 8, TB], F32, st_b) for i in range(2)]
    zb = [sb(f"zb{i}", [128, TB + 8], F32, st_b) for i in range(3)]
    cvA = sb("cvA", [128, TB], F32, st_b)
    cvB = sb("cvB", [128, TB], F32, st_b)
    xvb = [sb(f"xvb{i}", [128, TB], F32, st_b) for i in range(2)]
    ph3b = sb("ph3b", [128, TB], F32, st_b)
    mib = sb("mib", [128, TB], I32, st_b)
    r2b = sb("r2b", [128, TB], F32, st_b)
    sinb2 = [sb(f"sinb{i}", [128, TB], F32, st_b) for i in range(2)]
    cosb2 = [sb(f"cosb{i}", [128, TB], F32, st_b) for i in range(2)]
    sqvb = [sb("sqvb0", [128, TB], F32, st_b)] * 2
    t1b = sb("t1b", [128, TB], F32, st_b)
    polyb = sb("polyb", [128, TB], F32, st_b)
    ppb = [sb(f"ppb{i}", [128, TB], F32, st_b) for i in range(2)]
    areb = [sb(f"areb{i}", [128, TB], F32, st_b) for i in range(2)]
    aimb = [sb(f"aimb{i}", [128, TB], F32, st_b) for i in range(2)]
    nreb = [sb(f"nreb{i}", [128, TB], F32, st_b) for i in range(2)]
    nimb = [sb(f"nimb{i}", [128, TB], F32, st_b) for i in range(2)]
    nbfre = [sb(f"nbfre{i}", [128, TB], BF16, st_b) for i in range(2)]
    nbfim = [sb(f"nbfim{i}", [128, TB], BF16, st_b) for i in range(2)]
    sqbf = [sb(f"sqbf{i}", [128, TB], BF16, st_b) for i in range(4)]
    yvb = sb("yvb", [128, NVC * TB], BF16, st_b)
    gateb = [sb(f"gateb{i}", [128, TB], BF16, st_b) for i in range(2)]
    spk = sb("spk", [32, 4 * TB], F32, st_b)
    twtb = sb("twtb", [32, TB], F32, st_b)
    dsl = sb("dsl", [32, TB], F32, st_b)
    svalb = sb("svalb", [32, TB], F32, st_b)
    pwbb = sb("pwbb", [32, TB], BF16, st_b)
    svbb = sb("svbb", [32, TB], BF16, st_b)
    ysc = polyb

    def ld_op(dcq):
        dma("sync", opd[dcq % 2][:, :], opbf_d[dcq],
            reads=tuple(f"opbf{dcq}_{q}" for q in range(4)),
            writes=(f"opd{dcq % 2}",), tag="ldop")
    srsrc = ipw_d[:, 4096:4128].rearrange("(dc p) c -> p dc c", p=128)
    dma("pool", ipsr[:, :].rearrange("p (dc c) -> p dc c", dc=8), srsrc,
        reads=(), writes=("ipsr",), tag="ldipsr")

    def ip_lhs(gpar, dcc, cc_in_g):
        return ipb[gpar][:, dcc * 512 + cc_in_g * 128:
                         dcc * 512 + cc_in_g * 128 + 128]

    def ld_ipg(gidx):
        g = gidx % 8
        dma("sync", ipb[gidx % 3][:, :], ipbf_d[g], reads=(f"ipbf{g}",),
            writes=(f"ipg{gidx % 3}",), tag="ldipg")

    xlnbufs = [xlnbf, xlnb2]

    def ld_ut_b(it):
        par = it % 2
        utt = utB[par]
        utkeys = [f"utB{par}_{dc}" for dc in range(8)]
        src = u1_d.rearrange("(c p) t -> p c t", p=128)[
            :, :, it * TB:(it + 1) * TB]
        dma("sync", utt[:, :, :], src,
            reads=tuple(f"u1d_{it}_{dc}" for dc in range(8)),
            writes=tuple(utkeys), tag="ldu1")

    def ln_b(it):
        par = it % 2
        emit_ln(utB[par], [f"utB{par}_{dc}" for dc in range(8)], TB,
                PB[4], PB[5], "B" + str(it), False,
                xt=xlnbufs[par], xpre=f"xlnB{par}_")

    ld_ut_b(0)
    ln_b(0)
    for it in range(NTB):
        t0 = it * TB
        par = it % 2
        utt = utB[par]
        utkeys = [f"utB{par}_{dc}" for dc in range(8)]
        xlt = xlnbufs[par]
        xlk = [f"xlnB{par}_{dc}" for dc in range(8)]
        if it == 0:
            for g in range(3):
                ld_ipg(g)
        dma("sync", dsl[:, :], dist_d[:, t0:t0 + TB], reads=(),
            writes=("dsl",), tag="lddist")
        act(twtb[:, :], dsl[:, :], AF.Exp, reads=("dsl",), writes=("twt",),
            scale=negslope[:, 0:1], sreads=("negslope",), tag="tw")

        zrot = [0]
        ZB3 = (0, 1, 7)

        def chunk_z(cc, outbuf, outkey, tag):
            gpar = (it * 8 + cc // 4) % 3
            gkey = f"ipg{gpar}"
            zbi = ZB3[zrot[0] % 3]
            bank = PBK[zbi]
            pb = PB[zbi]
            wk = 128 if cc < 32 else 32
            for dc in range(8):
                if cc < 32:
                    lhs = ip_lhs(gpar, dc, cc % 4)
                    rk = gkey
                else:
                    lhs = ipsr[:, dc * 32:(dc + 1) * 32]
                    rk = "ipsr"
                mm(bank, pb[0:wk, 0:TB], lhs,
                   xlt[:, dc * TB:(dc + 1) * TB], dc == 0, dc == 7,
                   reads=(rk, xlk[dc]), tag="ipmm")
            zr = zb[zrot[0] % 3]
            zrk = f"zb{zrot[0] % 3}"
            zrot[0] += 1
            act(zr[0:wk, 3:3 + TB], pb[0:wk, 0:TB], AF.Identity,
                reads=(bank,), writes=(zrk,), tag="zev")
            def _ti(eng, zr=zr, cc=cc, wk=wk):
                return eng.tensor_copy(zr[0:wk, 0:3],
                                       tails[0:wk, cc * 3:cc * 3 + 3])
            S.add("pool", _ti, reads=(f"tail{cc}",), writes=(zrk + "_t",),
                  tag="ti")
            def _tu(eng, zr=zr, cc=cc, wk=wk):
                return eng.tensor_copy(tails[0:wk, cc * 3:cc * 3 + 3],
                                       zr[0:wk, TB:TB + 3])
            S.add("pool", _tu, reads=(zrk,), writes=(f"tail{cc}",), tag="tu")
            k3 = ckt[0:wk, cc * 4 + 3:cc * 4 + 4]
            cbv = cbt[0:wk, cc:cc + 1]
            act(cvA[0:wk, :], zr[0:wk, 3:3 + TB], AF.Identity,
                reads=(zrk,), writes=("cvA",), scale=k3, bias=cbv,
                sreads=("ck", "cb"), tag="cv1")
            stt(cvB[0:wk, :], zr[0:wk, 2:2 + TB],
                ckt[0:wk, cc * 4 + 2:cc * 4 + 3], cvA[0:wk, :], ALU.mult,
                ALU.add, reads=(zrk, "cvA", zrk + "_t"), writes=("cvB",),
                sreads=("ck",), tag="cv2")
            stt(cvA[0:wk, :], zr[0:wk, 1:1 + TB],
                ckt[0:wk, cc * 4 + 1:cc * 4 + 2], cvB[0:wk, :], ALU.mult,
                ALU.add, reads=(zrk, zrk + "_t", "cvB"), writes=("cvA",),
                sreads=("ck",), tag="cv3")
            stt(outbuf, zr[0:wk, 0:TB], ckt[0:wk, cc * 4:cc * 4 + 1],
                cvA[0:wk, :], ALU.mult, ALU.add,
                reads=(zrk, zrk + "_t", "cvA"), writes=(outkey,),
                sreads=("ck",), tag=tag)

        zs = spk[:, 0:TB]
        chunk_z(32, zs[0:32, :], "zs", "cv_s")
        pwv = spk[:, TB:2 * TB]
        ts(pwv, zs, scoret[:, 0:1], 20.0, ALU.mult, ALU.min, reads=("zs",),
           writes=("pws1",), sreads=("score",), tag="score")
        ts(pwv, pwv, -20.0, None, ALU.max, None, reads=("pws1",),
           writes=("pws2",), tag="clip")
        act(pwv, pwv, AF.Exp, reads=("pws2",), writes=("pexp",), tag="pexp")
        tt(pwbb[:, :], pwv, twtb[:, :], ALU.mult, reads=("pexp", "twt"),
           writes=("pw",), tag="pw")
        denv = spk[:, 2 * TB:3 * TB]
        scan(denv, ones_scan[0:32, :], pwbb[:, :], carry_den[:, 0:1],
             reads=("ones_scan", "pw"), writes=("den",), sreads=("cden",),
             tag="denscan")
        copy(carry_den[:, 0:1], denv[:, TB - 1:TB], reads=("den",),
             writes=("cden",), tag="dencarry")
        ts(denv, denv, 1e-4, None, ALU.max, None, reads=("den",),
           writes=("denc",), tag="denc")

        def val_m0(cc):
            p2 = cc % 2
            chunk_z(cc, xvb[p2][:, :], f"xv{p2}", "cv_xv")
            if cc % 4 == 3 and it * 8 + cc // 4 + 3 < NTB * 8:
                ld_ipg(it * 8 + cc // 4 + 3)
            mm(PBK[2 + p2], PB[2 + p2][:, 0:TB],
               selt[:, cc * 128:(cc + 1) * 128], pwbb[:, :], True, True,
               reads=("sel", "pw"), tag="pwx")

        def val_m1(cc):
            p2 = cc % 2
            xvv = xvb[p2]
            pxb = PBK[2 + p2]
            pxp = PB[2 + p2]
            act(ph3b[:, :], xvv[:, :], AF.Identity, reads=(f"xv{p2}",),
                writes=("ph3",), scale=theta2[:, cc:cc + 1],
                bias=cc_ap("c256"), sreads=("theta2", "cst_c256"), tag="ph3")
            act(mib[:, :], ph3b[:, :], AF.Identity, reads=("ph3",),
                writes=("mi",), tag="mi")
            stt(r2b[:, :], mib[:, :], -1.0, ph3b[:, :], ALU.mult, ALU.add,
                reads=("mi", "ph3"), writes=("r2",), tag="r2")
            act(sinb2[p2][:, :], r2b[:, :], AF.Sin, reads=("r2",),
                writes=(f"sin{p2}",), scale=cc_ap("n2pi"),
                sreads=("cst_n2pi",), tag="sin")
            act(ph3b[:, :], r2b[:, :], AF.Abs, reads=("r2",),
                writes=("absr", "ph3"), scale=cc_ap("p2pi"),
                sreads=("cst_p2pi",), tag="absr")
            act(cosb2[p2][:, :], ph3b[:, :], AF.Sin, reads=("absr",),
                writes=(f"cos{p2}",), bias=cc_ap("npih"),
                sreads=("cst_npih",), tag="cos")
            sqv = sqvb[p2]
            act(sqv[:, :], xvv[:, :], AF.Square, reads=(f"xv{p2}",),
                writes=("sqv",), tag="sqv")
            act(t1b[:, :], xvv[:, :], AF.Identity, reads=(f"xv{p2}",),
                writes=("t1",), scale=cc_ap("w1b"), bias=cc_ap("w0b"),
                sreads=("cst_w1b", "cst_w0b"), tag="t1")
            stt(polyb[:, :], sqv[:, :], cc_ap("nw2b"), t1b[:, :], ALU.mult,
                ALU.add, reads=("sqv", "t1"), writes=("poly",),
                sreads=("cst_nw2b",), tag="poly")
            ppv = ppb[p2]
            tt(ppv[:, :], polyb[:, :], pxp[:, 0:TB], ALU.mult,
               reads=("poly", pxb), writes=(f"pp{p2}",), tag="pp")
            tt(areb[p2][:, :], ppv[:, :], cosb2[p2][:, :], ALU.mult,
               reads=(f"pp{p2}", f"cos{p2}"), writes=(f"are{p2}",),
               stream="pool", tag="are")
            tt(aimb[p2][:, :], ppv[:, :], sinb2[p2][:, :], ALU.mult,
               reads=(f"pp{p2}", f"sin{p2}"), writes=(f"aim{p2}",),
               stream="pool", tag="aim")

        def val_m2a(cc):
            p2 = cc % 2
            nre = nreb[p2]
            nim = nimb[p2]
            scan(nre[:, :], ones_scan[:, :], areb[p2][:, :],
                 carry_re[:, cc:cc + 1], reads=("ones_scan", f"are{p2}"),
                 writes=(f"nre{p2}",), sreads=(f"cre{cc}",), tag="scan_re")
            copy(carry_re[:, cc:cc + 1], nre[:, TB - 1:TB],
                 reads=(f"nre{p2}",), writes=(f"cre{cc}",), stream="pool",
                 tag="cre")
            scan(nim[:, :], ones_scan[:, :], aimb[p2][:, :],
                 carry_im[:, cc:cc + 1], reads=("ones_scan", f"aim{p2}"),
                 writes=(f"nim{p2}",), sreads=(f"cim{cc}",), tag="scan_im")
            copy(carry_im[:, cc:cc + 1], nim[:, TB - 1:TB],
                 reads=(f"nim{p2}",), writes=(f"cim{cc}",), stream="pool",
                 tag="cim")

        def val_m2b(cc):
            p2 = cc % 2
            nre = nreb[p2]
            nim = nimb[p2]
            copy(nbfre[p2][:, :], nre[:, :], reads=(f"nre{p2}",),
                 writes=(f"nbfre{p2}",), tag="nbfre")
            copy(nbfim[p2][:, :], nim[:, :], reads=(f"nim{p2}",),
                 writes=(f"nbfim{p2}",), stream="pool", tag="nbfim")
            tt(sqbf[p2][:, :], nbfre[p2][:, :], nbfre[p2][:, :], ALU.mult,
               reads=(f"nbfre{p2}",), writes=(f"sqbf{p2}",), tag="sqre")
            tt(sqbf[2 + p2][:, :], nbfim[p2][:, :], nbfim[p2][:, :], ALU.mult,
               reads=(f"nbfim{p2}",), writes=(f"sqbf{2 + p2}",), tag="sqim")

        def val_m2c(cc):
            p2 = cc % 2
            mm("PB6", PB[6][0:32, 0:TB], sel2t[:, cc * 32:(cc + 1) * 32],
               sqbf[p2][:, :], cc == 0, False, reads=("sel2", f"sqbf{p2}"),
               tag="ssq_re")
            mm("PB6", PB[6][0:32, 0:TB], sel2t[:, cc * 32:(cc + 1) * 32],
               sqbf[2 + p2][:, :], False, cc == NVC - 1,
               reads=("sel2", f"sqbf{2 + p2}"), tag="ssq_im")
            yb_ = PBK[4 + p2]
            ypb = PB[4 + p2]
            mm(yb_, ypb[:, 0:TB], wrebd[:, :], nbfre[p2][:, :],
               True, False, reads=("wrebd", f"nbfre{p2}"), tag="ymm")
            mm(yb_, ypb[:, 0:TB], wimbd[:, :], nbfim[p2][:, :],
               False, True, reads=("wimbd", f"nbfim{p2}"), tag="ymm")
            act(yvb[:, cc * TB:(cc + 1) * TB], ypb[:, 0:TB], AF.Identity,
                reads=(yb_,), writes=(f"yv{cc}",), tag="yv")

        for tstep in range(NVC + 3):
            if tstep < NVC:
                val_m0(tstep)
            if 1 <= tstep <= NVC:
                val_m1(tstep - 1)
            if 2 <= tstep <= NVC + 1:
                val_m2a(tstep - 2)
                val_m2b(tstep - 2)
            if tstep >= 3:
                val_m2c(tstep - 3)
            if tstep == 2 and it + 1 < NTB:
                ld_ut_b(it + 1)

        if it == 1 and dbg_b is not None:
            dma("sync", dbg_b[:, 0:TB], xvb[1][:, :], reads=("xv1",),
                writes=("dbgb0",), tag="dbg")
            dma("sync", dbg_b[:, TB:2 * TB], nreb[1][:, :], reads=("nre1",),
                writes=("dbgb1",), tag="dbg")
            dma("sync", dbg_b[:, 2 * TB:3 * TB], nimb[1][:, :], reads=("nim1",),
                writes=("dbgb2",), tag="dbg")
            dma("pool", dbg_b[:, 3 * TB:4 * TB], xlnbf[:, 0:TB],
                reads=("xln0",), writes=("dbgb3",), tag="dbg")
            dma("sync", dbg_b[:, 4 * TB:5 * TB], sinb2[1][:, :], reads=("sin1",),
                writes=("dbgb4",), tag="dbg")
            dma("sync", dbg_b[:, 5 * TB:6 * TB], ppb[1][:, :], reads=("pp1",),
                writes=("dbgb5",), tag="dbg")
            dma("sync", dbg_bs[:, 0:TB], spk[:, TB:2 * TB], reads=("pw",),
                writes=("dbgs0",), tag="dbg")
            dma("sync", dbg_bs[:, TB:2 * TB], spk[:, 2 * TB:3 * TB],
                reads=("denc",), writes=("dbgs1",), tag="dbg")
        if it + 1 < NTB:
            ln_b(it + 1)
        sv = spk[:, 3 * TB:4 * TB]
        tmp = spk[:, TB:2 * TB]
        ts(tmp, PB[6][0:32, 0:TB], 1.0 / 128.0, None, ALU.mult, None,
           reads=("PB6",), writes=("ssq_s", "pexp"), tag="ssq_s")
        act(sv, denv, AF.Square, reads=("denc",), writes=("dc2",), tag="dc2")
        stt(spk[:, 0:TB], sv, 1e-5, tmp, ALU.mult, ALU.add,
            reads=("dc2", "ssq_s"), writes=("inner",), tag="inner")
        act(sv, spk[:, 0:TB], AF.Ln, reads=("inner",), writes=("rt2",),
            tag="rt2")
        act(svbb[:, :], sv, AF.Exp, reads=("rt2",), writes=("sval",),
            scale=cst[0:32, CCOL["nhalf"]:CCOL["nhalf"] + 1],
            sreads=("cst_nhalf",), tag="sval")

        if it == 1 and dbg_b2 is not None:
            dma("pool", dbg_b2[:, 0:TB], yvb[:, 15 * TB:16 * TB],
                reads=("yv15",), writes=("dbgy0",), tag="dbg")
            dma("pool", dbg_b2[:, TB:2 * TB], sqbf[0][:, :],
                reads=("sqbf0",), writes=("dbgy1",), tag="dbg")
            dma("sync", dbg_bs[:, 2 * TB:3 * TB], svalb[:, :],
                reads=("sval",), writes=("dbgy2",), tag="dbg")
        def gate_g0(g):
            cc = 16 + g
            p2 = g % 2
            chunk_z(cc, xvb[p2][:, :], f"xv{p2}", "cv_g")
            if cc % 4 == 3 and it * 8 + cc // 4 + 3 < NTB * 8:
                ld_ipg(it * 8 + cc // 4 + 3)

        def gate_g1(g):
            p2 = g % 2
            gv = gateb[p2]
            act(gv[:, :], xvb[p2][:, :], AF.Silu, reads=(f"xv{p2}",),
                writes=(f"gate{p2}",), tag="silu")
            sxb = PBK[2 + p2]
            sxp = PB[2 + p2]
            mm(sxb, sxp[:, 0:TB], selt[:, g * 128:(g + 1) * 128],
               svbb[:, :], True, True, reads=("sel", "sval"), tag="sexp")
            tt(ysc[:, :], yvb[:, g * TB:(g + 1) * TB], sxp[:, 0:TB],
               ALU.mult, reads=(f"yv{g}", sxb), writes=("ysc",), tag="yscale")
            tt(yvb[:, g * TB:(g + 1) * TB], ysc[:, :], gv[:, :], ALU.mult,
               reads=("ysc", f"gate{p2}"), writes=(f"yv{g}",), tag="ygate")

        for tstep in range(NGC + 1):
            if tstep < NGC:
                gate_g0(tstep)
            if tstep >= 1:
                gate_g1(tstep - 1)

        ld_op(0)
        ld_op(1)
        for dc in range(8):
            ob = PBK[dc % 2]
            opb = PB[dc % 2]
            for kc in range(16):
                lhs = opd[dc % 2][:, kc * 128:(kc + 1) * 128]
                mm(ob, opb[:, 0:TB], lhs, yvb[:, kc * TB:(kc + 1) * TB],
                   kc == 0, kc == 15, reads=(f"opd{dc % 2}", f"yv{kc}"),
                   tag="opmm")
            if dc + 2 < 8:
                ld_op(dc + 2)
            eb = ebuf[dc % 2]
            def _res(eng, opb=opb, eb=eb, utt=utt, dc=dc):
                return eng.tensor_tensor(eb[:, :], opb[:, 0:TB],
                                         utt[:, dc, :], op=ALU.add)
            S.add("dve", _res, reads=(ob, utkeys[dc]),
                  writes=(f"ebuf{dc % 2}",), tag="res_b")
            dst = u2_d[dc * 128:(dc + 1) * 128, t0:t0 + TB]
            dma("act", dst, eb[:, :], reads=(f"ebuf{dc % 2}",),
                writes=(f"u2d_{it}_{dc}",), tag="st_u2")

    st_b.close()
    S.barrier()

    st_c = ExitStack()
    w1sbC = sb("w1sbC", [128, NE * 8 * 512], BF16, st_c)
    w2sbC = sb("w2sbC", [128, NE * 4 * 1024], BF16, st_c)
    utC = [sb(f"utC{i}", [128, 8, TA], F32, st_c) for i in range(2)]
    hsbC = sb("hsbC", [128, NE * 4 * TA], BF16, st_c)
    hgbC = [sb(f"hgbC{i}", [128, TA], BF16, st_c) for i in range(2)]
    ewbsC = [sb(f"ewbsC{i}", [128, TA], BF16, st_c) for i in range(2)]
    ewfC = sb("ewfC", [1, NE * TA], BF16, st_c)
    qtsC = sb("qtsC", [6, 128], F32, st_c)
    ytmC = [sb(f"ytmC{i}", [128, 128], F32, st_c) for i in range(2)]

    w1keysC = [f"w1C_{e}" for e in range(NE)]
    w2keysC = [f"w2C_{e}" for e in range(NE)]
    for e in range(NE):
        src = w1_d[e].rearrange("(dc p) h -> p dc h", p=128)
        dst = w1sbC[:, e * 4096:(e + 1) * 4096].rearrange(
            "p (dc h) -> p dc h", dc=8)
        dma("pool", dst, src, reads=(), writes=(w1keysC[e],), tag="ldw1C")
    for e in range(NE):
        src = w2_d[e].rearrange("(hc p) d -> p hc d", p=128)
        dst = w2sbC[:, e * 4096:(e + 1) * 4096].rearrange(
            "p (hc d) -> p hc d", hc=4)
        dma("pool", dst, src, reads=(), writes=(w2keysC[e],), tag="ldw2C")

    emit_ffn("C", w1keysC, w2keysC, w1sbC, w2sbC, None, utC, hsbC, hgbC,
             ewbsC, ewfC, qtsC, ytmC, u2_d, None, True)
    st_c.close()

    S.finalize()
    doms = sorted(S.totals.keys() | {"pe", "dve", "act", "pool"})
    sem_cm = {}
    for d_ in doms:
        sems[d_] = st_p.enter_context(nc.semaphore("sem_" + d_))
    with st_p, nc.Block() as block:
        @block.tensor
        def _(eng):
            S.emit_stream("pe", eng, sems)

        @block.vector
        def _(eng):
            S.emit_stream("dve", eng, sems)

        @block.scalar
        def _(eng):
            S.emit_stream("act", eng, sems)

        @block.gpsimd
        def _(eng):
            S.emit_stream("pool", eng, sems)

        @block.sync
        def _(eng):
            S.emit_stream("sync", eng, sems)
            for d_ in doms:
                tot = S.totals.get(d_, 0)
                if tot:
                    eng.wait_ge(sems[d_], tot * (16 if d_.startswith("q")
                                                 else 1))
    return nc


_PROG_CACHE = {}


def _host_prep(inputs):
    ns = np.asarray(inputs["L0_attn_norm_scale"], np.float32)
    theta = np.asarray(inputs["L0_attn_theta"], np.float32).reshape(2048)
    conv_k = np.asarray(inputs["L0_attn_conv_k"], np.float32)
    conv_b = np.asarray(inputs["L0_attn_conv_b"], np.float32)
    proj_w = np.asarray(inputs["L0_ffn_proj_w"], np.float32)

    theta_t = np.ascontiguousarray(theta.reshape(16, 128).T)
    ck = np.zeros((128, 33 * 4), np.float32)
    cb = np.zeros((128, 33), np.float32)
    for ccc in range(33):
        w = 128 if ccc < 32 else 32
        for j in range(4):
            ck[:w, ccc * 4 + j] = conv_k[j, 0, ccc * 128:ccc * 128 + w]
        cb[:w, ccc] = conv_b[ccc * 128:ccc * 128 + w]
    score = np.asarray(inputs["L0_attn_score_scale"], np.float32).reshape(32, 1)
    slopes = np.concatenate([
        np.asarray(inputs["L0_attn_decay_slopes"], np.float32),
        np.asarray(inputs["L0_attn_anchor_slopes"], np.float32)]).reshape(32, 1)
    deriv = np.asarray(inputs["L0_attn_deriv_logits"],
                       np.float32).reshape(1, 3)
    wre = np.asarray(inputs["L0_attn_W_re"], np.float32)
    wim = np.asarray(inputs["L0_attn_W_im"], np.float32)
    Wre2 = np.ascontiguousarray(np.concatenate([wre, wre], 0))
    Wim2 = np.ascontiguousarray(np.concatenate([wim, wim], 0))
    nsre2 = np.ascontiguousarray(
        np.concatenate([ns[:64], ns[:64]]).reshape(128, 1))
    nsim2 = np.ascontiguousarray(
        np.concatenate([ns[64:], ns[64:]]).reshape(128, 1))
    pwT = np.zeros((128, 48), np.float32)
    for dc in range(8):
        pwT[:, dc * 6:(dc + 1) * 6] = proj_w[:, dc * 128:(dc + 1) * 128].T
    anchT = np.ascontiguousarray(
        np.asarray(inputs["L0_ffn_anchors"], np.float32).T)
    rt = np.asarray(inputs["L0_ffn_routing_temp"],
                    np.float32).reshape(1, 1)
    SEL = np.zeros((32, 2048), np.float32)
    SEL2 = np.zeros((128, 512), np.float32)
    for ccc in range(16):
        for p in range(128):
            k = 2 * ccc + p // 64
            SEL[k, ccc * 128 + p] = 1.0
            SEL2[p, ccc * 32 + k] = 1.0
    ident = np.eye(128, dtype=np.float32)
    pos = np.arange(L, dtype=np.float32)
    dist = np.zeros((32, L), np.float32)
    dist[0:28, :] = (L - 1) - pos[None, :]
    dist[28:32, :] = pos[None, :]

    shared = {
        "ipw": np.ascontiguousarray(np.asarray(inputs["L0_attn_in_proj_w"],
                                               np.float32)),
        "opw": np.ascontiguousarray(np.asarray(inputs["L0_attn_out_proj_w"],
                                               np.float32)),
        "w1": np.ascontiguousarray(np.asarray(inputs["L0_ffn_w1"],
                                              np.float32)),
        "w2": np.ascontiguousarray(np.asarray(inputs["L0_ffn_w2"],
                                              np.float32)),
        "theta_t": theta_t, "ck": ck, "cb": cb, "score": score,
        "slopes": slopes, "deriv": deriv, "Wre2": Wre2, "Wim2": Wim2,
        "nsre2": nsre2, "nsim2": nsim2, "pwT": pwT, "anchT": anchT,
        "rt": rt, "SEL": SEL, "SEL2": SEL2, "ident": ident, "dist": dist,
        "hostc_in": np.array([[1e-5, 0.0, 0.0, 0.0]], np.float32),
    }
    return shared


def kernel(**inputs):
    x = np.asarray(inputs["L0_attn_x"], np.float32)
    shared = _host_prep(inputs)
    if "prog" not in _PROG_CACHE:
        _PROG_CACHE["prog"] = build_program()
    nc = _PROG_CACHE["prog"]
    in_maps = []
    for b in range(8):
        m = dict(shared)
        m["x"] = np.ascontiguousarray(x[b])
        in_maps.append(m)
    res = run_bass_kernel_spmd(nc, in_maps, core_ids=list(range(8)))
    out = np.stack([res.results[b]["y"] for b in range(8)], axis=0)
    return out.astype(np.float32)


if __name__ == "__main__":
    import reference as _r
    ins = _r.setup_inputs(0)
    out = kernel(**{k: np.asarray(v) for k, v in ins.items()})
    ref = np.asarray(_r.reference(**ins))
    err = np.sqrt(((out - ref) ** 2).mean() / (ref ** 2).mean())
    print("rel l2:", err)
